# version 22
# baseline (speedup 1.0000x reference)
import numpy as np
import concourse.bass as bass
import concourse.mybir as mybir
from concourse.bass_utils import run_bass_kernel_spmd
from contextlib import ExitStack

F32 = mybir.dt.float32
BF16 = mybir.dt.bfloat16
AF = mybir.ActivationFunctionType
ALU = mybir.AluOpType
AX = mybir.AxisListType

_DEBUG = False

T, D, H, DK, E, FF = 2048, 1024, 16, 64, 8, 4096
NDC = 8
NTS = 16
NPAIR = 8

ENGS = ("sync", "gpsimd", "tensor", "vector", "scalar")
SEMS = ("pe", "act", "dve", "pool", "ds0", "ds1",
        "dg", "dg1", "dg2", "dg3", "dg4", "dg5", "da0", "da1")
ESEM = {"tensor": "pe", "vector": "dve", "scalar": "act", "gpsimd": "pool"}
DSEM = {"sync": ("ds0", "ds1"),
        "gpsimd": ("dg", "dg1", "dg2", "dg3", "dg4", "dg5"),
        "scalar": ("da0", "da1")}


class Prog:
    def __init__(self):
        self.q = {e: [] for e in ENGS}
        self.cnt = {s: 0 for s in SEMS}
        self.dq = {e: 0 for e in DSEM}

    def c(self, s):
        return self.cnt[s]

    def emit(self, eng, fn, wait=(), inc=()):
        wd = {}
        for s, v in wait:
            if v > 0:
                wd[s] = max(wd.get(s, 0), v)
        for s, n in inc:
            self.cnt[s] += n
        self.q[eng].append((tuple(wd.items()), fn, tuple(inc)))
        return {s: self.cnt[s] for s, _ in inc}

    def cop(self, eng, fn, wait=()):
        s = ESEM[eng]
        w = list(wait)
        if eng != "tensor":
            w.append((s, self.c(s)))
        return self.emit(eng, fn, w, [(s, 1)])[s]

    def pe(self, fn, wait=(), inc=False):
        if inc:
            return self.emit("tensor", fn, wait, [("pe", 1)])["pe"]
        self.emit("tensor", fn, wait, ())
        return None

    def dma(self, eng, fn, wait=()):
        sems = DSEM[eng]
        s = sems[self.dq[eng] % len(sems)]
        self.dq[eng] += 1
        w = list(wait)
        if len(sems) > 1 and self.c(s) > 0:
            w.append((s, self.c(s)))
        return (s, self.emit(eng, fn, w, [(s, 16)])[s])

    def barrier(self):
        snap = [(s, self.c(s)) for s in SEMS]
        for e in ENGS:
            self.emit(e, None, snap, ())


def build_program():
    nc = bass.Bass("TRN2", target_bir_lowering=False)
    p = Prog()

    x_d = nc.dram_tensor("x", [T, D], F32, kind="ExternalInput").ap()
    wqT_d = nc.dram_tensor("wqT", [D, D], F32, kind="ExternalInput").ap()
    wkT_d = nc.dram_tensor("wkT", [D, D], F32, kind="ExternalInput").ap()
    wvT_d = nc.dram_tensor("wvT", [D, D], F32, kind="ExternalInput").ap()
    woT_d = nc.dram_tensor("woT", [D, D], F32, kind="ExternalInput").ap()
    rexp_d = nc.dram_tensor("rexp", [H, 128, 4096], F32, kind="ExternalInput").ap()
    gwT_d = nc.dram_tensor("gwT", [D, E], F32, kind="ExternalInput").ap()
    bq_d = nc.dram_tensor("bq_c", [128, 8], F32, kind="ExternalInput").ap()
    bk_d = nc.dram_tensor("bk_c", [128, 8], F32, kind="ExternalInput").ap()
    bv_d = nc.dram_tensor("bv_c", [128, 8], F32, kind="ExternalInput").ap()
    bo_d = nc.dram_tensor("bo_r", [1, D], F32, kind="ExternalInput").ap()
    b1_d = nc.dram_tensor("b1_c", [E, 128, 32], F32, kind="ExternalInput").ap()
    b2_d = nc.dram_tensor("b2_r", [E, D], F32, kind="ExternalInput").ap()
    w1_d = nc.dram_tensor("w1", [E, D, FF], F32, kind="ExternalInput").ap()
    w2_d = nc.dram_tensor("w2", [E, FF, D], F32, kind="ExternalInput").ap()
    ones_d = nc.dram_tensor("ones_r", [1, 128], F32, kind="ExternalInput").ap()
    ident_d = nc.dram_tensor("ident", [128, 128], F32, kind="ExternalInput").ap()
    out_d = nc.dram_tensor("out", [T, D], F32, kind="ExternalOutput").ap()
    w1b_d = nc.dram_tensor("w1b", [E, D, FF], BF16).ap()

    taps = {}
    if _DEBUG:
        taps["xln1"] = nc.dram_tensor("tap_xln1", [128, NDC * T], F32, kind="ExternalOutput").ap()
        taps["xattn"] = nc.dram_tensor("tap_xattn", [128, NTS * D], F32, kind="ExternalOutput").ap()
        taps["wg"] = nc.dram_tensor("tap_wg", [128, 64], F32, kind="ExternalOutput").ap()
        taps["qp"] = nc.dram_tensor("tap_qp", [128, 2048], F32, kind="ExternalOutput").ap()
        taps["kp"] = nc.dram_tensor("tap_kp", [128, 2048], F32, kind="ExternalOutput").ap()
        taps["vtm"] = nc.dram_tensor("tap_vtm", [128, 2048], F32, kind="ExternalOutput").ap()
        taps["pb"] = nc.dram_tensor("tap_pb", [128, 2048], F32, kind="ExternalOutput").ap()
        taps["ctx"] = nc.dram_tensor("tap_ctx", [128, NPAIR * 2048], F32, kind="ExternalOutput").ap()
        taps["hid"] = nc.dram_tensor("tap_hid", [128, 1024], F32, kind="ExternalOutput").ap()

    st_g = ExitStack()

    def sb(name, shape, dt):
        return st_g.enter_context(nc.sbuf_tensor(name, shape, dt))

    X = sb("X", [128, NTS * 1024], F32)
    identb = sb("identb", [128, 128], BF16)
    identf = sb("identf", [128, 128], F32)
    ones_sb = sb("ones_sb", [1, 128], F32)
    eps_col = sb("eps_col", [128, 1], F32)
    bq_sb = sb("bq_sb", [128, 8], F32)
    bk_sb = sb("bk_sb", [128, 8], F32)
    bv_sb = sb("bv_sb", [128, 8], F32)
    gw_sb = sb("gw_sb", [128, NDC * E], BF16)
    lnstats = sb("lnstats", [128, 2 * 12], F32)
    lnmv = sb("lnmv", [128, 2 * 2], F32)
    lnsq = sb("lnsq", [128, 2], F32)
    lnrstd = sb("lnrstd", [128, 2], F32)
    lntm = sb("lntm", [128, 1024], BF16)

    sem_objs = {}
    for s in SEMS:
        sem_objs[s] = st_g.enter_context(nc.semaphore("sem_" + s))

    ds_x = []
    for xc in range(4):
        ds_x.append(p.dma("sync", lambda e, xc=xc: e.dma_start(
            X[:, xc * 4096:(xc + 1) * 4096].rearrange("p (t d) -> p t d", t=4),
            x_d.rearrange("(t p) d -> p t d", p=128)[:, xc * 4:(xc + 1) * 4])))
    ds_identf = p.dma("sync", lambda e: e.dma_start(identf[:], ident_d))
    ds_ones = p.dma("sync", lambda e: e.dma_start(ones_sb[:], ones_d))
    p.dma("sync", lambda e: e.dma_start(bq_sb[:], bq_d))
    p.dma("sync", lambda e: e.dma_start(bk_sb[:], bk_d))
    p.dma("sync", lambda e: e.dma_start(bv_sb[:], bv_d))

    p.cop("gpsimd", lambda e: e.memset(eps_col[:], 1e-5))
    p.dma("gpsimd", lambda e: e.dma_start(identb[:], ident_d))
    p.dma("gpsimd", lambda e: e.dma_start(
        gw_sb[:].rearrange("p (c e) -> p c e", c=NDC), gwT_d.rearrange("(c p) e -> p c e", p=128)))
    dg_qkvo = ("dg", 0)

    psT = [None, None]
    ln_war = {"sqrt": [0, 0], "recip": [0, 0], "trans": [0, 0], "copy": [0, 0]}

    def emit_ln(ts, dst, dst_stride, dst_off, x_wait, xln_war_pe):
        par = ts % 2
        so, mo = par * 12, par * 2
        pt = psT[par]
        xs = X[:, ts * 1024:(ts + 1) * 1024]
        p.cop("vector", lambda e, xs=xs, so=so: e.bn_stats(lnstats[:, so:so + 6], xs[:, 0:512]),
              wait=x_wait)
        p.cop("vector", lambda e, xs=xs, so=so: e.bn_stats(lnstats[:, so + 6:so + 12], xs[:, 512:1024]))
        aggr = p.cop("vector", lambda e, so=so, mo=mo: e.bn_aggr(lnmv[:, mo:mo + 2], lnstats[:, so:so + 12]),
                     wait=[("act", ln_war["sqrt"][par])])
        sq = p.cop("scalar", lambda e, par=par, mo=mo: e.activation(
            lnsq[:, par:par + 1], lnmv[:, mo + 1:mo + 2], AF.Sqrt, bias=eps_col[:], scale=1.0),
            wait=[("dve", max(aggr, ln_war["recip"][par]))])
        ln_war["sqrt"][par] = sq
        rc = p.cop("vector", lambda e, par=par: e.reciprocal(lnrstd[:, par:par + 1], lnsq[:, par:par + 1]),
                   wait=[("act", sq)])
        ln_war["recip"][par] = rc
        ap = p.cop("vector", lambda e, xs=xs, par=par, mo=mo: e.tensor_scalar(
            lntm[:], xs, lnmv[:, mo:mo + 1],
            lnrstd[:, par:par + 1], ALU.subtract, ALU.mult),
            wait=[("pe", ln_war["trans"][0])])
        last = None
        for c in range(NDC):
            last = p.pe(lambda e, c=c, pt=pt: e.transpose(
                pt[:, c * 128:(c + 1) * 128],
                lntm[:, c * 128:(c + 1) * 128], identb[:]),
                wait=([("dve", max(ap, ln_war["copy"][par]))] if c == 0 else ()),
                inc=(c == NDC - 1))
        ln_war["trans"][0] = last
        cp = p.cop("vector", lambda e, pt=pt, dst=dst, dst_off=dst_off: e.tensor_copy(
            dst[:].rearrange("p (c t) -> p c t", c=NDC)[:, :, dst_off:dst_off + 128],
            pt[:].rearrange("p (c f) -> p c f", c=NDC)),
            wait=[("pe", last), ("pe", xln_war_pe)])
        ln_war["copy"][par] = cp
        return cp

    st_a = ExitStack()

    def sba(name, shape, dt):
        return st_a.enter_context(nc.sbuf_tensor(name, shape, dt))

    XLN = sba("XLN", [128, NDC * 2048], BF16)
    bo_row = sba("bo_row", [1, D], F32)
    CTX = sba("CTX", [128, NPAIR * 2048], BF16)

    ln1_done = 0
    with nc.psum_tensor("psT0", [128, 1024], BF16) as psT0, \
         nc.psum_tensor("psT1", [128, 1024], BF16) as psT1:
        psT[0], psT[1] = psT0, psT1
        for ts in range(NTS):
            ln1_done = emit_ln(ts, XLN, 2048, ts * 128, [ds_x[ts // 4], ("pool", 1)], 0)

    if _DEBUG:
        p.dma("gpsimd", lambda e: e.dma_start(taps["xln1"], XLN[:]), wait=[("dve", ln1_done)])

    st_ap = ExitStack()

    def sbp(name, shape, dt):
        return st_ap.enter_context(nc.sbuf_tensor(name, shape, dt))

    QP = sbp("QP", [128, 2048], BF16)
    KP2 = sbp("KP2", [128, 2 * 2048], BF16)
    VTM = sbp("VTM", [128, 16 * 130], BF16)
    VFM = sbp("VFM", [128, 2048], BF16)
    PBT = sbp("PBT", [128, 16 * 512], BF16)
    RT = sbp("RT", [128, 2 * 4096], BF16)
    WSL = sbp("WSL", [128, 3 * 1024], BF16)
    QSQ = sbp("QSQ", [128, 2048], BF16)
    onesb = sbp("onesb", [128, 128], BF16)
    BND = sbp("BND", [128, 12], F32)
    ctxh = sbp("ctxh", [128, 2 * 64], BF16)
    recipc = sbp("recipc", [128, 2], F32)

    st_aps = ExitStack()
    S_ps = st_aps.enter_context(nc.psum_tensor("S_ps", [128, 2048], F32))
    PT_ps = [st_aps.enter_context(nc.psum_tensor("PT_ps0", [128, 1024], BF16))]
    CTXPS = st_aps.enter_context(nc.psum_tensor("CTXPS", [128, 512], F32))
    sreg = [S_ps[:, 0:1024], S_ps[:, 1024:2048]]
    cslot = [CTXPS[:, 0:65], CTXPS[:, 65:130]]

    p.cop("gpsimd", lambda e: e.memset(onesb[:], 1.0))
    p.cop("gpsimd", lambda e: e.memset(KP2[64:128, 0:2048], 0.0))
    p.cop("gpsimd", lambda e: e.memset(KP2[0:64, 2048:4096], 0.0))
    for kc16 in range(16):
        for h2 in range(2):
            p.cop("gpsimd", lambda e, kc16=kc16, h2=h2: e.memset(
                VTM[:, kc16 * 130 + h2 * 65 + 64: kc16 * 130 + h2 * 65 + 65], 1.0))
    vtm_ones = p.c("pool")

    wsl_rd_pe = [0, 0, 0]
    rt_rd_pe = [0, 0]
    qp_rd_pe = 0
    vfm_rd_pe = 0
    vtm_rd_pe = 0
    qsq_rd_pe = 0
    pt_ps_rd = [("dve", 0)]
    slot_rd = [("dve", 0)] * 4
    sps_rd = [("act", 0), ("act", 0)]
    pbt_rd_pe = [0] * 16
    cevac = [("dve", 0), ("dve", 0)]
    ctxh_tr = [0, 0]

    wq_srcs = (wqT_d, wkT_d, wvT_d)
    w1cast = []

    for c in range(NPAIR):
        wsl_ds = []
        for m in range(3):
            wsl_ds.append(p.dma("gpsimd", lambda e, m=m, c=c: e.dma_start(
                WSL[:, m * 1024:(m + 1) * 1024].rearrange("p (k o) -> p k o", k=NDC),
                wq_srcs[m][:, c * 128:(c + 1) * 128].rearrange("(k p) o -> p k o", p=128)),
                wait=[("pe", wsl_rd_pe[m])]))
        rt_ds = []
        for hh in range(2):
            rt_ds.append(p.dma("gpsimd", lambda e, hh=hh, c=c: e.dma_start(
                RT[:, hh * 4096:(hh + 1) * 4096], rexp_d[2 * c + hh]),
                wait=[("pool", rt_rd_pe[hh])]))
        w1cast.append(p.dma("gpsimd", lambda e, c=c: e.dma_start(
            w1b_d[c], w1_d[c], max_dma_last_dim=8192)))

        for tb in range(4):
            for dc in range(NDC):
                for h in range(2):
                    w = []
                    if dc == 0 and h == 0:
                        w = [wsl_ds[0], slot_rd[tb], ("dve", ln1_done), ("pool", vtm_ones)]
                    p.pe(lambda e, h=h, tb=tb, dc=dc: e.matmul(
                        S_ps[h * 64:h * 64 + 64, tb * 512:(tb + 1) * 512],
                        WSL[:, dc * 128 + h * 64: dc * 128 + h * 64 + 64],
                        XLN[:, dc * 2048 + tb * 512: dc * 2048 + (tb + 1) * 512],
                        start=(dc == 0), stop=(dc == NDC - 1)),
                        wait=w, inc=(tb == 3 and dc == NDC - 1 and h == 1))
        qmm = p.c("pe")
        ev_q = 0
        for h in range(2):
            ev_q = p.cop("vector", lambda e, c=c, h=h: e.tensor_scalar(
                QP[h * 64:h * 64 + 64, :], S_ps[h * 64:h * 64 + 64, :],
                bq_sb[h * 64:h * 64 + 64, c:c + 1],
                0.125, ALU.add, ALU.mult),
                wait=[("pe", max(qmm, qp_rd_pe))])
        for q4 in range(4):
            slot_rd[q4] = ("dve", ev_q)

        def production(m, evac_fn):
            for tb in range(4):
                for dc in range(NDC):
                    w = []
                    if dc == 0:
                        w = [wsl_ds[m], slot_rd[tb]]
                    p.pe(lambda e, m=m, tb=tb, dc=dc: e.matmul(
                        S_ps[:, tb * 512:(tb + 1) * 512],
                        WSL[:, m * 1024 + dc * 128: m * 1024 + (dc + 1) * 128],
                        XLN[:, dc * 2048 + tb * 512: dc * 2048 + (tb + 1) * 512],
                        start=(dc == 0), stop=(dc == NDC - 1)),
                        wait=w, inc=(tb == 3 and dc == NDC - 1))
            mm_done = p.c("pe")
            ev = evac_fn(mm_done)
            for q4 in range(4):
                slot_rd[q4] = ("dve", ev)
            return ev

        def evac_k(mm):
            evk = 0
            for h in range(2):
                hb = h * 64
                evk = p.cop("vector", lambda e, c=c, h=h, hb=hb: e.tensor_scalar(
                    KP2[hb:hb + 64, h * 2048:(h + 1) * 2048], S_ps[hb:hb + 64, :],
                    bk_sb[hb:hb + 64, c:c + 1], None, ALU.add),
                    wait=([("pe", max(mm, qp_rd_pe))] if h == 0 else ()))
            return evk

        ev_k = production(1, evac_k)
        ev_v = production(2, lambda mm: p.cop("vector", lambda e, c=c: e.tensor_scalar(
            VFM[:], S_ps[:], bv_sb[:, c:c + 1], None, ALU.add),
            wait=[("pe", max(mm, vfm_rd_pe))]))
        wsl_rd_pe[0] = qmm
        wsl_rd_pe[1] = wsl_rd_pe[2] = p.c("pe")

        vcp = ("dve", 0)
        for g in range(2):
            lastt = None
            for k8 in range(8):
                ts8 = g * 8 + k8
                lastt = p.pe(lambda e, ts8=ts8, k8=k8: e.transpose(
                    PT_ps[0][:, k8 * 128:(k8 + 1) * 128],
                    VFM[:, ts8 * 128:(ts8 + 1) * 128], identb[:]),
                    wait=([("dve", ev_v), pt_ps_rd[0]] if k8 == 0 else ()),
                    inc=(k8 == 7))
            vcp = ("dve", p.cop("vector", lambda e, g=g: e.tensor_copy(
                VTM[:].rearrange("p (k s o) -> p k s o", k=16, s=2, o=65)[:, g * 8:(g + 1) * 8, :, 0:64],
                PT_ps[0][:].rearrange("p (k s o) -> p k s o", k=8, s=2, o=64)),
                wait=[("pe", max(lastt, vtm_rd_pe))]))
            pt_ps_rd[0] = vcp
        vfm_rd_pe = p.c("pe")

        sqk = 0
        for h in range(2):
            hb = h * 64
            sqk = p.cop("vector", lambda e, h=h, hb=hb: e.tensor_tensor(
                QSQ[hb:hb + 64, :], KP2[hb:hb + 64, h * 2048:(h + 1) * 2048],
                KP2[hb:hb + 64, h * 2048:(h + 1) * 2048], ALU.mult),
                wait=[("pe", qsq_rd_pe)])

        def bound_group(sq_done, dstcol):
            for h in range(2):
                hb = h * 64
                for j in range(4):
                    sl = j
                    p.pe(lambda e, hb=hb, j=j, sl=sl: e.matmul(
                        S_ps[:, sl * 512:(sl + 1) * 512],
                        onesb[hb:hb + 64, :], QSQ[hb:hb + 64, j * 512:(j + 1) * 512],
                        start=True, stop=True),
                        wait=[("dve", sq_done), slot_rd[sl]], inc=True)
                    mmid = p.c("pe")
                    rid = p.cop("vector", lambda e, j=j, sl=sl: e.tensor_reduce(
                        BND[:, 4 + j:5 + j], S_ps[:, sl * 512:(sl + 1) * 512], AX.X, ALU.max),
                        wait=[("pe", mmid)])
                    slot_rd[sl] = ("dve", rid)
                p.cop("vector", lambda e: e.tensor_tensor(
                    BND[:, 8:9], BND[:, 4:5], BND[:, 5:6], ALU.max))
                p.cop("vector", lambda e: e.tensor_tensor(
                    BND[:, 9:10], BND[:, 6:7], BND[:, 7:8], ALU.max))
                p.cop("vector", lambda e, h=h, dstcol=dstcol: e.tensor_tensor(
                    BND[:, dstcol + h:dstcol + h + 1], BND[:, 8:9], BND[:, 9:10], ALU.max))
            return p.c("pe")

        k_ones_pe = bound_group(sqk, 2)
        sqq = p.cop("vector", lambda e: e.tensor_tensor(QSQ[:], QP[:], QP[:], ALU.mult),
                    wait=[("pe", k_ones_pe)])
        qsq_rd_pe = bound_group(sqq, 10)
        bneg_dve = [0, 0]
        rexp_done = [0, 0]
        for h in range(2):
            mrb = p.cop("vector", lambda e, h=h: e.tensor_reduce(
                BND[:, 8:9], RT[:, h * 4096:(h + 1) * 4096], AX.X, ALU.max, negate=True),
                wait=[rt_ds[h]] + ([("act", rexp_done[0])] if h == 1 else []))
            rexp_done[h] = p.cop("scalar", lambda e, h=h: e.activation(
                RT[:, h * 4096:(h + 1) * 4096], RT[:, h * 4096:(h + 1) * 4096],
                AF.Exp, bias=BND[:, 8:9], scale=1.0),
                wait=[("dve", mrb)])
            p.cop("vector", lambda e, h=h: e.tensor_tensor(
                BND[:, 9:10], BND[:, 2 + h:3 + h], BND[:, 10 + h:11 + h], ALU.mult))
            sq = p.cop("scalar", lambda e: e.activation(
                BND[:, 9:10], BND[:, 9:10], AF.Sqrt, bias=eps_col[:], scale=1.0),
                wait=[("dve", p.c("dve"))])
            bneg_dve[h] = p.cop("vector", lambda e, h=h: e.tensor_scalar(
                BND[:, h:h + 1], BND[:, 9:10], -1.02, None, ALU.mult),
                wait=[("act", sq)])

        bounds_dve = p.c("dve")
        for h in range(2):
            hb = h * 64
            for qc in range(4):
                ex_ids = [None] * 16
                for kcp in range(8):
                    reg = kcp % 2
                    for sub in range(2):
                        kc = 2 * kcp + sub
                        w = []
                        if sub == 0:
                            w = [sps_rd[reg]]
                            if qc == 0 and kc == 0:
                                w += [("dve", max(bounds_dve, ev_k))]
                        p.pe(lambda e, h=h, qc=qc, kc=kc, reg=reg, sub=sub: e.matmul(
                            sreg[reg][:, sub * 512:(sub + 1) * 512],
                            KP2[:, h * 2048 + kc * 128: h * 2048 + (kc + 1) * 128],
                            QP[:, qc * 512:(qc + 1) * 512],
                            start=True, stop=True),
                            wait=w, inc=(sub == 1))
                    kqmm = p.c("pe")
                    exw = [("pe", max(kqmm, pbt_rd_pe[2 * kcp], pbt_rd_pe[2 * kcp + 1]))]
                    if qc == 0 and kcp == 0:
                        exw.append(("dve", bneg_dve[h]))
                    ex = p.cop("scalar", lambda e, reg=reg, kcp=kcp, h=h: e.activation(
                        PBT[:, kcp * 1024:(kcp + 1) * 1024], sreg[reg][:], AF.Exp,
                        bias=BND[:, h:h + 1], scale=1.0), wait=exw)
                    sps_rd[reg] = ("act", ex)
                    for sub in range(2):
                        kc = 2 * kcp + sub
                        off = 2047 - 128 * kc + qc * 512
                        mw = [("act", ex)]
                        if qc == 0 and kcp == 0 and sub == 0:
                            mw.append(("act", rexp_done[h]))
                        mid = p.cop("gpsimd", lambda e, h=h, kc=kc, off=off: e.tensor_tensor(
                            PBT[:, kc * 512:(kc + 1) * 512],
                            PBT[:, kc * 512:(kc + 1) * 512],
                            RT[:, h * 4096 + off: h * 4096 + off + 512], ALU.mult),
                            wait=mw)
                        ex_ids[kc] = ("pool", mid)
                for qt in range(4):
                    cp = qt % 2
                    j = qc * 4 + qt
                    ctx_mm = 0
                    for kc in range(16):
                        cw = []
                        if qt == 0:
                            cw.append(ex_ids[kc])
                        if kc == 0:
                            cw += [vcp, cevac[cp]]
                        ctx_mm = p.pe(lambda e, h=h, kc=kc, cp=cp, qt=qt: e.matmul(
                            cslot[cp][:],
                            PBT[:, kc * 512 + qt * 128: kc * 512 + qt * 128 + 128],
                            VTM[:, kc * 130 + h * 65: kc * 130 + h * 65 + 65],
                            start=(kc == 0), stop=(kc == 15)),
                            wait=cw, inc=(kc == 15 or qt == 3))
                        if qt == 3:
                            pbt_rd_pe[kc] = p.c("pe")
                    ctx_mm = p.c("pe")
                    rcp = p.cop("vector", lambda e, cp=cp: e.reciprocal(
                        recipc[:, cp:cp + 1], cslot[cp][:, 64:65]),
                        wait=[("pe", ctx_mm)])
                    ev = p.cop("vector", lambda e, cp=cp: e.tensor_scalar(
                        ctxh[:, cp * 64:(cp + 1) * 64], cslot[cp][:, 0:64],
                        recipc[:, cp:cp + 1], None, ALU.mult),
                        wait=[("pe", ctxh_tr[cp])])
                    cevac[cp] = ("dve", ev)
                    tr = p.pe(lambda e, cp=cp: e.transpose(
                        PT_ps[0][0:64, cp * 128:(cp + 1) * 128],
                        ctxh[:, cp * 64:(cp + 1) * 64], identb[:]),
                        wait=[("dve", ev), pt_ps_rd[0]], inc=True)
                    ctxh_tr[cp] = tr
                    cpy = p.cop("vector", lambda e, c=c, hb=hb, j=j, cp=cp: e.tensor_copy(
                        CTX[hb:hb + 64, c * 2048 + j * 128: c * 2048 + (j + 1) * 128],
                        PT_ps[0][0:64, cp * 128:(cp + 1) * 128]),
                        wait=[("pe", tr)])
                    pt_ps_rd[0] = ("dve", cpy)
            rt_rd_pe[h] = p.c("pool")
        qp_rd_pe = p.c("pe")
        vtm_rd_pe = p.c("pe")
        slot_rd[0] = slot_rd[1] = sps_rd[0]
        slot_rd[2] = slot_rd[3] = sps_rd[1]

    ctx_done = ("dve", p.c("dve"))
    attn_pe_last = p.c("pe")
    st_aps.close()
    st_ap.close()

    st_w = ExitStack()
    WOT = st_w.enter_context(nc.sbuf_tensor("WOT", [128, NDC * 1024], BF16))
    wo_ps = [st_w.enter_context(nc.psum_tensor(f"wo_ps{i}", [128, 512], F32)) for i in range(2)]
    bo_ps = st_w.enter_context(nc.psum_tensor("bo_ps", [128, 1024], F32))

    ds_bo = p.dma("sync", lambda e: e.dma_start(bo_row[:], bo_d))
    ds_wot = 0
    for oc in range(NDC):
        ds_wot = p.dma("gpsimd", lambda e, oc=oc: e.dma_start(
            WOT[:, oc * 1024:(oc + 1) * 1024], woT_d[oc * 128:(oc + 1) * 128, :]),
            wait=([dg_qkvo, ("pe", p.c("pe")), ("act", p.c("act")),
                   ("pool", p.c("pool")), ("dve", p.c("dve"))] if oc == 0 else ()))
    bomm = None
    for half in range(2):
        bomm = p.pe(lambda e, half=half: e.matmul(
            bo_ps[:, half * 512:(half + 1) * 512], ones_sb[:],
            bo_row[:, half * 512:(half + 1) * 512], start=True, stop=True),
            wait=([ds_bo, ds_ones, ctx_done] if half == 0 else ()), inc=(half == 1))

    wo_rd = [0, 0]
    attn_x_done = [0] * NTS
    for ts in range(NTS):
        for bank in range(2):
            par = bank
            for oc in range(NDC):
                w = []
                if oc == 0:
                    w = [ctx_done, ("dve", wo_rd[par]), ds_wot]
                p.pe(lambda e, ts=ts, bank=bank, oc=oc, par=par: e.matmul(
                    wo_ps[par][:],
                    CTX[:, oc * 2048 + ts * 128: oc * 2048 + (ts + 1) * 128],
                    WOT[:, oc * 1024 + bank * 512: oc * 1024 + (bank + 1) * 512],
                    start=(oc == 0), stop=(oc == NDC - 1)),
                    wait=w, inc=(oc == NDC - 1))
            wmm = p.c("pe")
            xsl = X[:, ts * 1024 + bank * 512: ts * 1024 + (bank + 1) * 512]
            p.cop("vector", lambda e, xsl=xsl, par=par: e.tensor_tensor(
                xsl, xsl, wo_ps[par][:], ALU.add), wait=[("pe", wmm)])
            tt2 = p.cop("vector", lambda e, xsl=xsl, bank=bank: e.tensor_tensor(
                xsl, xsl, bo_ps[:, bank * 512:(bank + 1) * 512], ALU.add),
                wait=[("pe", bomm)])
            wo_rd[par] = tt2
        attn_x_done[ts] = p.c("dve")

    if _DEBUG:
        p.dma("sync", lambda e: e.dma_start(taps["xattn"], X[:]),
              wait=[("dve", attn_x_done[NTS - 1])])
        p.barrier()
    st_w.close()
    st_a.close()
    p.barrier()

    st_f = ExitStack()

    def sbf(name, shape, dt):
        return st_f.enter_context(nc.sbuf_tensor(name, shape, dt))

    XLN2 = sbf("XLN2", [128, NDC * 1024], BF16)
    HID = sbf("HID", [128, 32 * 1024], BF16)
    W2Q = sbf("W2Q", [128, 2 * 32 * 256], BF16)
    W1R = sbf("W1R", [128, 2 * 2048], BF16)
    b1_sb = sbf("b1_sb", [128, E * 32], F32)
    b2_sb = sbf("b2_sb", [E, D], F32)
    WG = sbf("WG", [128, 8 * E], F32)
    WMT = sbf("WMT", [8, 8 * 128], F32)
    GSC = sbf("GSC", [128, 8 * 8], F32)
    gcol = sbf("gcol", [128, 8], F32)
    tmpq = sbf("tmpq", [128, 2 * 256], F32)

    w1_ps = [st_f.enter_context(nc.psum_tensor(f"w1_ps{i}", [128, 512], F32)) for i in range(2)]
    w2_ps = [st_f.enter_context(nc.psum_tensor(f"w2_ps{i}", [128, 256], F32)) for i in range(2)]
    psT2 = [st_f.enter_context(nc.psum_tensor(f"psT2_{i}", [128, 1024], BF16)) for i in range(2)]
    psT[0], psT[1] = psT2[0], psT2[1]

    ds_b1 = p.dma("sync", lambda e: e.dma_start(
        b1_sb[:].rearrange("p (e f) -> p e f", e=E), b1_d.rearrange("e p f -> p e f")))
    ds_b2 = p.dma("sync", lambda e: e.dma_start(b2_sb[:], b2_d))
    w1_slot_pe = [0, 0]
    w1_slot_ds = [("ds0", 0), ("ds0", 0)]
    w2_buf_pe = [0, 0]
    w2_buf_da = [("da0", 0), ("da1", 0)]
    w1_war = [0, 0]
    w2_war = [0, 0]
    w2_evac_war = [0, 0]
    hid_rd_pe = 0
    xln_rd_pe = attn_pe_last

    def issue_w1_load(th, e_idx, fcp):
        slot = (th * 128 + e_idx * 16 + fcp) % 2
        w1_slot_ds[slot] = p.dma("gpsimd", lambda en, e_idx=e_idx, fcp=fcp, slot=slot: en.dma_start(
            W1R[:, slot * 2048:(slot + 1) * 2048].rearrange("p (c f) -> p c f", c=NDC),
            w1b_d[e_idx][:, fcp * 256:(fcp + 1) * 256].rearrange("(c p) f -> p c f", p=128)),
            wait=[("pe", w1_slot_pe[slot]), w1cast[e_idx]])
        return slot

    for th in range(2):
        ln2_done = 0
        for tsl in range(8):
            ts = th * 8 + tsl
            ln2_done = emit_ln(ts, XLN2, 1024, tsl * 128, [("dve", attn_x_done[ts])], xln_rd_pe)
        b2add_done = 0
        gmm = None
        for tsl in range(8):
            ts = th * 8 + tsl
            for dc in range(NDC):
                gmm = p.pe(lambda e, dc=dc, tsl=tsl: e.matmul(
                    w2_ps[0][:, 0:8],
                    XLN2[:, dc * 1024 + tsl * 128: dc * 1024 + (tsl + 1) * 128],
                    gw_sb[:, dc * 8:(dc + 1) * 8],
                    start=(dc == 0), stop=(dc == NDC - 1)),
                    wait=([("dve", ln2_done), ("act", p.c("act")), dg_qkvo]
                          if dc == 0 else ()),
                    inc=(dc == NDC - 1))
            nmg = p.cop("vector", lambda e: e.tensor_reduce(
                gcol[:, 0:1], w2_ps[0][:, 0:8], AX.X, ALU.max, negate=True),
                wait=[("pe", gmm)])
            exg = p.cop("scalar", lambda e: e.activation(
                GSC[:, 0:8], w2_ps[0][:, 0:8], AF.Exp, bias=gcol[:, 0:1], scale=1.0,
                accum_out=gcol[:, 1:2]), wait=[("dve", nmg)])
            rcg = p.cop("vector", lambda e: e.reciprocal(gcol[:, 2:3], gcol[:, 1:2]),
                        wait=[("act", exg)])
            p.cop("vector", lambda e: e.tensor_scalar(
                GSC[:, 8:16], GSC[:, 0:8], gcol[:, 2:3], None, ALU.mult))
            p.cop("vector", lambda e: e.tensor_reduce(
                gcol[:, 3:4], GSC[:, 8:16], AX.X, ALU.max))
            p.cop("vector", lambda e: e.tensor_scalar(
                GSC[:, 16:24], GSC[:, 8:16], gcol[:, 3:4], None, ALU.is_ge))
            p.cop("vector", lambda e: e.tensor_scalar(
                GSC[:, 24:32], GSC[:, 16:24], -2.0, None, ALU.mult))
            p.cop("vector", lambda e: e.tensor_tensor(
                GSC[:, 24:32], GSC[:, 8:16], GSC[:, 24:32], ALU.add))
            p.cop("vector", lambda e: e.tensor_reduce(
                gcol[:, 4:5], GSC[:, 24:32], AX.X, ALU.max))
            p.cop("vector", lambda e: e.tensor_scalar(
                GSC[:, 32:40], GSC[:, 24:32], gcol[:, 4:5], None, ALU.is_ge))
            p.cop("vector", lambda e: e.tensor_tensor(
                GSC[:, 16:24], GSC[:, 16:24], GSC[:, 32:40], ALU.add))
            wgw = p.cop("vector", lambda e, tsl=tsl: e.tensor_tensor(
                WG[:, tsl * 8:(tsl + 1) * 8], GSC[:, 16:24], GSC[:, 8:16], ALU.mult),
                wait=[("act", p.c("act"))])
            wt = p.pe(lambda e, tsl=tsl: e.transpose(
                w2_ps[1][0:8, 0:128], WG[:, tsl * 8:(tsl + 1) * 8], identf[:]),
                wait=[("dve", wgw), ("act", p.c("act")), ds_identf], inc=True)
            wcp = p.cop("vector", lambda e, tsl=tsl: e.tensor_copy(
                WMT[:, tsl * 128:(tsl + 1) * 128], w2_ps[1][0:8, 0:128]),
                wait=[("pe", max(wt, xln_rd_pe))])
            for bank in range(2):
                bmm = p.pe(lambda e, tsl=tsl, bank=bank: e.matmul(
                    w1_ps[0][:],
                    WMT[:, tsl * 128:(tsl + 1) * 128],
                    b2_sb[:, bank * 512:(bank + 1) * 512],
                    start=True, stop=True),
                    wait=[("dve", max(wcp, b2add_done)), ("act", p.c("act")), ds_b1, ds_b2],
                    inc=True)
                b2add_done = p.cop("vector", lambda e, ts=ts, bank=bank: e.tensor_tensor(
                    X[:, ts * 1024 + bank * 512: ts * 1024 + (bank + 1) * 512],
                    X[:, ts * 1024 + bank * 512: ts * 1024 + (bank + 1) * 512],
                    w1_ps[0][:], ALU.add), wait=[("pe", bmm)])

        for e_idx in range(E):
            def issue_w2_load(q, e_idx=None):
                qb = q % 2
                w2_buf_da[qb] = p.dma("gpsimd", lambda en, e_idx=e_idx, q=q, qb=qb: en.dma_start(
                    W2Q[:, qb * 8192:(qb + 1) * 8192].rearrange("p (fc h) -> p fc h", fc=32),
                    w2_d[e_idx][:, q * 256:(q + 1) * 256].rearrange("(fc p) h -> p fc h", p=128)),
                    wait=[("pe", w2_buf_pe[qb])])

            for fcp in range(2):
                issue_w1_load(th, e_idx, fcp)
            issue_w2_load(0, e_idx=e_idx)
            issue_w2_load(1, e_idx=e_idx)
            relu_done = 0
            for fc in range(32):
                fcp = fc // 2
                slot = (th * 128 + e_idx * 16 + fcp) % 2
                sub = (fc % 2) * 128
                for tb in range(2):
                    par = (fc * 2 + tb) % 2
                    for dc in range(NDC):
                        w = []
                        if dc == 0:
                            w = [w1_slot_ds[slot],
                                 ("dve", max(ln2_done, b2add_done)),
                                 ("act", w1_war[par])]
                        p.pe(lambda e, slot=slot, sub=sub, dc=dc, tb=tb, par=par: e.matmul(
                            w1_ps[par][:],
                            W1R[:, slot * 2048 + dc * 256 + sub: slot * 2048 + dc * 256 + sub + 128],
                            XLN2[:, dc * 1024 + tb * 512: dc * 1024 + (tb + 1) * 512],
                            start=(dc == 0), stop=(dc == NDC - 1)),
                            wait=w, inc=(dc == NDC - 1))
                    wmm = p.c("pe")
                    if tb == 1:
                        w1_slot_pe[slot] = wmm
                        if fc % 2 == 1 and fcp + 2 < 16:
                            issue_w1_load(th, e_idx, fcp + 2)
                    relu_done = p.cop("scalar", lambda e, e_idx=e_idx, fc=fc, tb=tb, par=par: e.activation(
                        HID[:, fc * 1024 + tb * 512: fc * 1024 + (tb + 1) * 512],
                        w1_ps[par][:], AF.Relu,
                        bias=b1_sb[:, e_idx * 32 + fc: e_idx * 32 + fc + 1], scale=1.0),
                        wait=[("pe", max(wmm, hid_rd_pe))])
                    w1_war[par] = relu_done
            for q in range(4):
                qb = q % 2
                for tsl in range(8):
                    ts = th * 8 + tsl
                    par = (q * 8 + tsl) % 2
                    for fc in range(32):
                        w = []
                        if fc == 0:
                            w = [w2_buf_da[qb], ("act", max(relu_done, w2_war[par]))]
                        p.pe(lambda e, fc=fc, tsl=tsl, qb=qb, par=par: e.matmul(
                            w2_ps[par][:],
                            HID[:, fc * 1024 + tsl * 128: fc * 1024 + tsl * 128 + 128],
                            W2Q[:, qb * 8192 + fc * 256: qb * 8192 + (fc + 1) * 256],
                            start=(fc == 0), stop=(fc == 31)),
                            wait=w, inc=(fc == 31))
                    w2mm = p.c("pe")
                    hid_rd_pe = w2mm
                    if tsl == 7:
                        w2_buf_pe[qb] = w2mm
                    evq = p.cop("scalar", lambda e, par=par, tsl=tsl, e_idx=e_idx: e.activation(
                        tmpq[:, par * 256:(par + 1) * 256], w2_ps[par][:], AF.Copy,
                        bias=0.0, scale=WG[:, tsl * 8 + e_idx: tsl * 8 + e_idx + 1]),
                        wait=[("pe", w2mm), ("pool", w2_evac_war[par])])
                    w2_war[par] = evq
                    xadd = p.cop("gpsimd", lambda e, ts=ts, q=q, par=par: e.tensor_tensor(
                        X[:, ts * 1024 + q * 256: ts * 1024 + q * 256 + 256],
                        X[:, ts * 1024 + q * 256: ts * 1024 + q * 256 + 256],
                        tmpq[:, par * 256:(par + 1) * 256], ALU.add),
                        wait=[("act", evq)])
                    w2_evac_war[par] = xadd
                if q < 2:
                    issue_w2_load(q + 2, e_idx=e_idx)
        xln_rd_pe = p.c("pe")

    if _DEBUG:
        p.dma("sync", lambda e: e.dma_start(taps["wg"], WG[:]), wait=[("dve", p.c("dve"))])
        p.dma("gpsimd", lambda e: e.dma_start(taps["hid"], HID[:, 0:1024]), wait=[("act", p.c("act"))])

    st_f.close()

    p.barrier()
    ds_out = p.dma("sync", lambda e: e.dma_start(
        out_d.rearrange("(t p) d -> p t d", p=128),
        X[:].rearrange("p (t d) -> p t d", t=NTS)))
    p.emit("sync", None, [ds_out], ())

    with nc.Block() as block:
        for engname in ENGS:
            def mk(name):
                def body(eng):
                    for wait, fn, inc in p.q[name]:
                        for s, v in wait:
                            eng.wait_ge(sem_objs[s], v)
                        inst = fn(eng) if fn is not None else None
                        for s, n in inc:
                            inst.then_inc(sem_objs[s], n)
                return body
            getattr(block, engname)(mk(engname))

    st_g.close()
    return nc


def _host_arrays(L0_attn_Wq, L0_attn_bq, L0_attn_Wk, L0_attn_bk, L0_attn_Wv, L0_attn_bv,
                 L0_attn_Wo, L0_attn_bo, L0_attn_rel_bias, L0_ffn_gate_w,
                 L0_ffn_w1, L0_ffn_b1, L0_ffn_w2, L0_ffn_b2):
    f = np.float32
    c = np.ascontiguousarray
    rel = np.asarray(L0_attn_rel_bias, dtype=f)
    idxT = np.clip(np.arange(4096)[None, :] - 2047 - np.arange(128)[:, None], -128, 128) + 128
    rexp = c(rel[:, idxT])
    shared = dict(
        wqT=c(np.asarray(L0_attn_Wq, f).T), wkT=c(np.asarray(L0_attn_Wk, f).T),
        wvT=c(np.asarray(L0_attn_Wv, f).T), woT=c(np.asarray(L0_attn_Wo, f).T),
        rexp=rexp,
        gwT=c(np.asarray(L0_ffn_gate_w, f).T),
        bq_c=c(np.asarray(L0_attn_bq, f).reshape(8, 128).T),
        bk_c=c(np.asarray(L0_attn_bk, f).reshape(8, 128).T),
        bv_c=c(np.asarray(L0_attn_bv, f).reshape(8, 128).T),
        bo_r=c(np.asarray(L0_attn_bo, f).reshape(1, D)),
        b1_c=c(np.asarray(L0_ffn_b1, f).reshape(E, 32, 128).transpose(0, 2, 1)),
        b2_r=c(np.asarray(L0_ffn_b2, f)),
        w1=c(np.asarray(L0_ffn_w1, f)),
        w2=c(np.asarray(L0_ffn_w2, f)),
        ones_r=np.ones((1, 128), f),
        ident=np.eye(128, dtype=f),
    )
    return shared


def kernel(L0_attn_x, L0_attn_Wq, L0_attn_bq, L0_attn_Wk, L0_attn_bk, L0_attn_Wv,
           L0_attn_bv, L0_attn_Wo, L0_attn_bo, L0_attn_rel_bias, L0_ffn_gate_w,
           L0_ffn_w1, L0_ffn_b1, L0_ffn_w2, L0_ffn_b2):
    x = np.asarray(L0_attn_x, dtype=np.float32)
    shared = _host_arrays(L0_attn_Wq, L0_attn_bq, L0_attn_Wk, L0_attn_bk, L0_attn_Wv,
                          L0_attn_bv, L0_attn_Wo, L0_attn_bo, L0_attn_rel_bias,
                          L0_ffn_gate_w, L0_ffn_w1, L0_ffn_b1, L0_ffn_w2, L0_ffn_b2)
    nc = build_program()
    in_maps = [dict(shared, x=np.ascontiguousarray(x[ci])) for ci in range(8)]
    res = run_bass_kernel_spmd(nc, in_maps, core_ids=list(range(8)))
    return np.stack([r["out"] for r in res.results]).astype(np.float32)


# version 23
# speedup vs baseline: 1.0124x; 1.0124x over previous
import numpy as np
import concourse.bass as bass
import concourse.mybir as mybir
from concourse.bass_utils import run_bass_kernel_spmd
from contextlib import ExitStack

F32 = mybir.dt.float32
BF16 = mybir.dt.bfloat16
AF = mybir.ActivationFunctionType
ALU = mybir.AluOpType
AX = mybir.AxisListType

_DEBUG = False

T, D, H, DK, E, FF = 2048, 1024, 16, 64, 8, 4096
NDC = 8
NTS = 16
NPAIR = 8

ENGS = ("sync", "gpsimd", "tensor", "vector", "scalar")
SEMS = ("pe", "act", "dve", "pool", "ds0", "ds1",
        "dg", "dg1", "dg2", "dg3", "dg4", "dg5", "da0", "da1")
ESEM = {"tensor": "pe", "vector": "dve", "scalar": "act", "gpsimd": "pool"}
DSEM = {"sync": ("ds0", "ds1"),
        "gpsimd": ("dg", "dg1", "dg2", "dg3", "dg4", "dg5"),
        "scalar": ("da0", "da1")}


class Prog:
    def __init__(self):
        self.q = {e: [] for e in ENGS}
        self.cnt = {s: 0 for s in SEMS}
        self.dq = {e: 0 for e in DSEM}

    def c(self, s):
        return self.cnt[s]

    def emit(self, eng, fn, wait=(), inc=()):
        wd = {}
        for s, v in wait:
            if v > 0:
                wd[s] = max(wd.get(s, 0), v)
        for s, n in inc:
            self.cnt[s] += n
        self.q[eng].append((tuple(wd.items()), fn, tuple(inc)))
        return {s: self.cnt[s] for s, _ in inc}

    def cop(self, eng, fn, wait=()):
        s = ESEM[eng]
        w = list(wait)
        if eng != "tensor":
            w.append((s, self.c(s)))
        return self.emit(eng, fn, w, [(s, 1)])[s]

    def pe(self, fn, wait=(), inc=False):
        if inc:
            return self.emit("tensor", fn, wait, [("pe", 1)])["pe"]
        self.emit("tensor", fn, wait, ())
        return None

    def dma(self, eng, fn, wait=()):
        sems = DSEM[eng]
        s = sems[self.dq[eng] % len(sems)]
        self.dq[eng] += 1
        w = list(wait)
        if len(sems) > 1 and self.c(s) > 0:
            w.append((s, self.c(s)))
        return (s, self.emit(eng, fn, w, [(s, 16)])[s])

    def barrier(self):
        snap = [(s, self.c(s)) for s in SEMS]
        for e in ENGS:
            self.emit(e, None, snap, ())


def build_program():
    nc = bass.Bass("TRN2", target_bir_lowering=False)
    p = Prog()

    x_d = nc.dram_tensor("x", [T, D], F32, kind="ExternalInput").ap()
    wqT_d = nc.dram_tensor("wqT", [D, D], F32, kind="ExternalInput").ap()
    wkT_d = nc.dram_tensor("wkT", [D, D], F32, kind="ExternalInput").ap()
    wvT_d = nc.dram_tensor("wvT", [D, D], F32, kind="ExternalInput").ap()
    woT_d = nc.dram_tensor("woT", [D, D], F32, kind="ExternalInput").ap()
    rexp_d = nc.dram_tensor("rexp", [H, 128, 4096], F32, kind="ExternalInput").ap()
    gwT_d = nc.dram_tensor("gwT", [D, E], F32, kind="ExternalInput").ap()
    bq_d = nc.dram_tensor("bq_c", [128, 8], F32, kind="ExternalInput").ap()
    bk_d = nc.dram_tensor("bk_c", [128, 8], F32, kind="ExternalInput").ap()
    bv_d = nc.dram_tensor("bv_c", [128, 8], F32, kind="ExternalInput").ap()
    bo_d = nc.dram_tensor("bo_r", [1, D], F32, kind="ExternalInput").ap()
    b1_d = nc.dram_tensor("b1_c", [E, 128, 32], F32, kind="ExternalInput").ap()
    b2_d = nc.dram_tensor("b2_r", [E, D], F32, kind="ExternalInput").ap()
    w1_d = nc.dram_tensor("w1", [E, D, FF], F32, kind="ExternalInput").ap()
    w2_d = nc.dram_tensor("w2", [E, FF, D], F32, kind="ExternalInput").ap()
    ones_d = nc.dram_tensor("ones_r", [1, 128], F32, kind="ExternalInput").ap()
    ident_d = nc.dram_tensor("ident", [128, 128], F32, kind="ExternalInput").ap()
    out_d = nc.dram_tensor("out", [T, D], F32, kind="ExternalOutput").ap()
    w1b_d = nc.dram_tensor("w1b", [E, D, FF], BF16).ap()

    taps = {}
    if _DEBUG:
        taps["xln1"] = nc.dram_tensor("tap_xln1", [128, NDC * T], F32, kind="ExternalOutput").ap()
        taps["xattn"] = nc.dram_tensor("tap_xattn", [128, NTS * D], F32, kind="ExternalOutput").ap()
        taps["wg"] = nc.dram_tensor("tap_wg", [128, 64], F32, kind="ExternalOutput").ap()
        taps["qp"] = nc.dram_tensor("tap_qp", [128, 2048], F32, kind="ExternalOutput").ap()
        taps["kp"] = nc.dram_tensor("tap_kp", [128, 2048], F32, kind="ExternalOutput").ap()
        taps["vtm"] = nc.dram_tensor("tap_vtm", [128, 2048], F32, kind="ExternalOutput").ap()
        taps["pb"] = nc.dram_tensor("tap_pb", [128, 2048], F32, kind="ExternalOutput").ap()
        taps["ctx"] = nc.dram_tensor("tap_ctx", [128, NPAIR * 2048], F32, kind="ExternalOutput").ap()
        taps["hid"] = nc.dram_tensor("tap_hid", [128, 1024], F32, kind="ExternalOutput").ap()

    st_g = ExitStack()

    def sb(name, shape, dt):
        return st_g.enter_context(nc.sbuf_tensor(name, shape, dt))

    X = sb("X", [128, NTS * 1024], F32)
    identb = sb("identb", [128, 128], BF16)
    identf = sb("identf", [128, 128], F32)
    ones_sb = sb("ones_sb", [1, 128], F32)
    eps_col = sb("eps_col", [128, 1], F32)
    bq_sb = sb("bq_sb", [128, 8], F32)
    bk_sb = sb("bk_sb", [128, 8], F32)
    bv_sb = sb("bv_sb", [128, 8], F32)
    gw_sb = sb("gw_sb", [128, NDC * E], BF16)
    lnstats = sb("lnstats", [128, 2 * 12], F32)
    lnmv = sb("lnmv", [128, 2 * 2], F32)
    lnsq = sb("lnsq", [128, 2], F32)
    lnrstd = sb("lnrstd", [128, 2], F32)
    lntm = sb("lntm", [128, 1024], BF16)

    sem_objs = {}
    for s in SEMS:
        sem_objs[s] = st_g.enter_context(nc.semaphore("sem_" + s))

    ds_x = []
    for xc in range(4):
        ds_x.append(p.dma("sync", lambda e, xc=xc: e.dma_start(
            X[:, xc * 4096:(xc + 1) * 4096].rearrange("p (t d) -> p t d", t=4),
            x_d.rearrange("(t p) d -> p t d", p=128)[:, xc * 4:(xc + 1) * 4])))
    ds_identf = p.dma("sync", lambda e: e.dma_start(identf[:], ident_d))
    ds_ones = p.dma("sync", lambda e: e.dma_start(ones_sb[:], ones_d))
    p.dma("sync", lambda e: e.dma_start(bq_sb[:], bq_d))
    p.dma("sync", lambda e: e.dma_start(bk_sb[:], bk_d))
    p.dma("sync", lambda e: e.dma_start(bv_sb[:], bv_d))

    p.cop("gpsimd", lambda e: e.memset(eps_col[:], 1e-5))
    p.dma("gpsimd", lambda e: e.dma_start(identb[:], ident_d))
    p.dma("gpsimd", lambda e: e.dma_start(
        gw_sb[:].rearrange("p (c e) -> p c e", c=NDC), gwT_d.rearrange("(c p) e -> p c e", p=128)))
    dg_qkvo = ("dg", 0)

    psT = [None, None]
    ln_war = {"sqrt": [0, 0], "recip": [0, 0], "trans": [0, 0], "copy": [0, 0]}

    def emit_ln(ts, dst, dst_stride, dst_off, x_wait, xln_war_pe):
        par = ts % 2
        so, mo = par * 12, par * 2
        pt = psT[par]
        xs = X[:, ts * 1024:(ts + 1) * 1024]
        p.cop("vector", lambda e, xs=xs, so=so: e.bn_stats(lnstats[:, so:so + 6], xs[:, 0:512]),
              wait=x_wait)
        p.cop("vector", lambda e, xs=xs, so=so: e.bn_stats(lnstats[:, so + 6:so + 12], xs[:, 512:1024]))
        aggr = p.cop("vector", lambda e, so=so, mo=mo: e.bn_aggr(lnmv[:, mo:mo + 2], lnstats[:, so:so + 12]),
                     wait=[("act", ln_war["sqrt"][par])])
        sq = p.cop("scalar", lambda e, par=par, mo=mo: e.activation(
            lnsq[:, par:par + 1], lnmv[:, mo + 1:mo + 2], AF.Sqrt, bias=eps_col[:], scale=1.0),
            wait=[("dve", max(aggr, ln_war["recip"][par]))])
        ln_war["sqrt"][par] = sq
        rc = p.cop("vector", lambda e, par=par: e.reciprocal(lnrstd[:, par:par + 1], lnsq[:, par:par + 1]),
                   wait=[("act", sq)])
        ln_war["recip"][par] = rc
        ap = p.cop("vector", lambda e, xs=xs, par=par, mo=mo: e.tensor_scalar(
            lntm[:], xs, lnmv[:, mo:mo + 1],
            lnrstd[:, par:par + 1], ALU.subtract, ALU.mult),
            wait=[("pe", ln_war["trans"][0])])
        last = None
        for c in range(NDC):
            last = p.pe(lambda e, c=c, pt=pt: e.transpose(
                pt[:, c * 128:(c + 1) * 128],
                lntm[:, c * 128:(c + 1) * 128], identb[:]),
                wait=([("dve", max(ap, ln_war["copy"][par]))] if c == 0 else ()),
                inc=(c == NDC - 1))
        ln_war["trans"][0] = last
        cp = p.cop("vector", lambda e, pt=pt, dst=dst, dst_off=dst_off: e.tensor_copy(
            dst[:].rearrange("p (c t) -> p c t", c=NDC)[:, :, dst_off:dst_off + 128],
            pt[:].rearrange("p (c f) -> p c f", c=NDC)),
            wait=[("pe", last), ("pe", xln_war_pe)])
        ln_war["copy"][par] = cp
        return cp

    st_a = ExitStack()

    def sba(name, shape, dt):
        return st_a.enter_context(nc.sbuf_tensor(name, shape, dt))

    XLN = sba("XLN", [128, NDC * 2048], BF16)
    bo_row = sba("bo_row", [1, D], F32)
    CTX = sba("CTX", [128, NPAIR * 2048], BF16)

    ln1_done = 0
    with nc.psum_tensor("psT0", [128, 1024], BF16) as psT0, \
         nc.psum_tensor("psT1", [128, 1024], BF16) as psT1:
        psT[0], psT[1] = psT0, psT1
        for ts in range(NTS):
            ln1_done = emit_ln(ts, XLN, 2048, ts * 128, [ds_x[ts // 4], ("pool", 1)], 0)

    if _DEBUG:
        p.dma("gpsimd", lambda e: e.dma_start(taps["xln1"], XLN[:]), wait=[("dve", ln1_done)])

    st_ap = ExitStack()

    def sbp(name, shape, dt):
        return st_ap.enter_context(nc.sbuf_tensor(name, shape, dt))

    QP = sbp("QP", [128, 2048], BF16)
    KP2 = sbp("KP2", [128, 2 * 2048], BF16)
    VTM = sbp("VTM", [128, 16 * 130], BF16)
    VFM = sbp("VFM", [128, 2048], BF16)
    PBT = sbp("PBT", [128, 16 * 512], BF16)
    RT = sbp("RT", [128, 2 * 4096], BF16)
    WSL = sbp("WSL", [128, 3 * 1024], BF16)
    QSQ = sbp("QSQ", [128, 2048], BF16)
    onesb = sbp("onesb", [128, 128], BF16)
    BND = sbp("BND", [128, 12], F32)
    ctxh = sbp("ctxh", [128, 2 * 64], BF16)
    recipc = sbp("recipc", [128, 2], F32)

    st_aps = ExitStack()
    S_ps = st_aps.enter_context(nc.psum_tensor("S_ps", [128, 2048], F32))
    PT_ps = [st_aps.enter_context(nc.psum_tensor("PT_ps0", [128, 1024], BF16))]
    CTXPS = st_aps.enter_context(nc.psum_tensor("CTXPS", [128, 512], F32))
    sreg = [S_ps[:, 0:1024], S_ps[:, 1024:2048]]
    cslot = [CTXPS[:, 0:65], CTXPS[:, 65:130]]

    p.cop("gpsimd", lambda e: e.memset(onesb[:], 1.0))
    p.cop("gpsimd", lambda e: e.memset(KP2[64:128, 0:2048], 0.0))
    p.cop("gpsimd", lambda e: e.memset(KP2[0:64, 2048:4096], 0.0))
    for kc16 in range(16):
        for h2 in range(2):
            p.cop("gpsimd", lambda e, kc16=kc16, h2=h2: e.memset(
                VTM[:, kc16 * 130 + h2 * 65 + 64: kc16 * 130 + h2 * 65 + 65], 1.0))
    vtm_ones = p.c("pool")

    wsl_rd_pe = [0, 0, 0]
    rt_rd_pe = [0, 0]
    qp_rd_pe = 0
    vfm_rd_pe = 0
    vtm_rd_pe = 0
    qsq_rd_pe = 0
    pt_ps_rd = [("dve", 0)]
    slot_rd = [("dve", 0)] * 4
    sps_rd = [("act", 0), ("act", 0)]
    pbt_rd_pe = [0] * 16
    cevac = [("dve", 0), ("dve", 0)]
    ctxh_tr = [0, 0]

    wq_srcs = (wqT_d, wkT_d, wvT_d)
    w1cast = []

    for c in range(NPAIR):
        wsl_ds = []
        for m in range(3):
            wsl_ds.append(p.dma("gpsimd", lambda e, m=m, c=c: e.dma_start(
                WSL[:, m * 1024:(m + 1) * 1024].rearrange("p (k o) -> p k o", k=NDC),
                wq_srcs[m][:, c * 128:(c + 1) * 128].rearrange("(k p) o -> p k o", p=128)),
                wait=[("pe", wsl_rd_pe[m])]))
        rt_ds = []
        for hh in range(2):
            rt_ds.append(p.dma("gpsimd", lambda e, hh=hh, c=c: e.dma_start(
                RT[:, hh * 4096:(hh + 1) * 4096], rexp_d[2 * c + hh]),
                wait=[("pe", rt_rd_pe[hh])]))
        w1cast.append(p.dma("gpsimd", lambda e, c=c: e.dma_start(
            w1b_d[c], w1_d[c], max_dma_last_dim=8192)))

        for tb in range(4):
            for dc in range(NDC):
                for h in range(2):
                    w = []
                    if dc == 0 and h == 0:
                        w = [wsl_ds[0], slot_rd[tb], ("dve", ln1_done), ("pool", vtm_ones)]
                    p.pe(lambda e, h=h, tb=tb, dc=dc: e.matmul(
                        S_ps[h * 64:h * 64 + 64, tb * 512:(tb + 1) * 512],
                        WSL[:, dc * 128 + h * 64: dc * 128 + h * 64 + 64],
                        XLN[:, dc * 2048 + tb * 512: dc * 2048 + (tb + 1) * 512],
                        start=(dc == 0), stop=(dc == NDC - 1)),
                        wait=w, inc=(tb == 3 and dc == NDC - 1 and h == 1))
        qmm = p.c("pe")
        ev_q = 0
        for h in range(2):
            ev_q = p.cop("vector", lambda e, c=c, h=h: e.tensor_scalar(
                QP[h * 64:h * 64 + 64, :], S_ps[h * 64:h * 64 + 64, :],
                bq_sb[h * 64:h * 64 + 64, c:c + 1],
                0.125, ALU.add, ALU.mult),
                wait=[("pe", max(qmm, qp_rd_pe))])
        for q4 in range(4):
            slot_rd[q4] = ("dve", ev_q)

        def production(m, evac_fn):
            for tb in range(4):
                for dc in range(NDC):
                    w = []
                    if dc == 0:
                        w = [wsl_ds[m], slot_rd[tb]]
                    p.pe(lambda e, m=m, tb=tb, dc=dc: e.matmul(
                        S_ps[:, tb * 512:(tb + 1) * 512],
                        WSL[:, m * 1024 + dc * 128: m * 1024 + (dc + 1) * 128],
                        XLN[:, dc * 2048 + tb * 512: dc * 2048 + (tb + 1) * 512],
                        start=(dc == 0), stop=(dc == NDC - 1)),
                        wait=w, inc=(tb == 3 and dc == NDC - 1))
            mm_done = p.c("pe")
            ev = evac_fn(mm_done)
            for q4 in range(4):
                slot_rd[q4] = ("dve", ev)
            return ev

        def evac_k(mm):
            evk = 0
            for h in range(2):
                hb = h * 64
                evk = p.cop("vector", lambda e, c=c, h=h, hb=hb: e.tensor_scalar(
                    KP2[hb:hb + 64, h * 2048:(h + 1) * 2048], S_ps[hb:hb + 64, :],
                    bk_sb[hb:hb + 64, c:c + 1], None, ALU.add),
                    wait=([("pe", max(mm, qp_rd_pe))] if h == 0 else ()))
            return evk

        ev_k = production(1, evac_k)
        ev_v = production(2, lambda mm: p.cop("vector", lambda e, c=c: e.tensor_scalar(
            VFM[:], S_ps[:], bv_sb[:, c:c + 1], None, ALU.add),
            wait=[("pe", max(mm, vfm_rd_pe))]))
        wsl_rd_pe[0] = qmm
        wsl_rd_pe[1] = wsl_rd_pe[2] = p.c("pe")

        vcp = ("dve", 0)
        for g in range(2):
            lastt = None
            for k8 in range(8):
                ts8 = g * 8 + k8
                lastt = p.pe(lambda e, ts8=ts8, k8=k8: e.transpose(
                    PT_ps[0][:, k8 * 128:(k8 + 1) * 128],
                    VFM[:, ts8 * 128:(ts8 + 1) * 128], identb[:]),
                    wait=([("dve", ev_v), pt_ps_rd[0]] if k8 == 0 else ()),
                    inc=(k8 == 7))
            vcp = ("dve", p.cop("vector", lambda e, g=g: e.tensor_copy(
                VTM[:].rearrange("p (k s o) -> p k s o", k=16, s=2, o=65)[:, g * 8:(g + 1) * 8, :, 0:64],
                PT_ps[0][:].rearrange("p (k s o) -> p k s o", k=8, s=2, o=64)),
                wait=[("pe", max(lastt, vtm_rd_pe))]))
            pt_ps_rd[0] = vcp
        vfm_rd_pe = p.c("pe")

        sqk = 0
        for h in range(2):
            hb = h * 64
            sqk = p.cop("vector", lambda e, h=h, hb=hb: e.tensor_tensor(
                QSQ[hb:hb + 64, :], KP2[hb:hb + 64, h * 2048:(h + 1) * 2048],
                KP2[hb:hb + 64, h * 2048:(h + 1) * 2048], ALU.mult),
                wait=[("pe", qsq_rd_pe)])

        def bound_group(sq_done, dstcol):
            for h in range(2):
                hb = h * 64
                for j in range(4):
                    sl = j
                    p.pe(lambda e, hb=hb, j=j, sl=sl: e.matmul(
                        S_ps[:, sl * 512:(sl + 1) * 512],
                        onesb[hb:hb + 64, :], QSQ[hb:hb + 64, j * 512:(j + 1) * 512],
                        start=True, stop=True),
                        wait=[("dve", sq_done), slot_rd[sl]], inc=True)
                    mmid = p.c("pe")
                    rid = p.cop("vector", lambda e, j=j, sl=sl: e.tensor_reduce(
                        BND[:, 4 + j:5 + j], S_ps[:, sl * 512:(sl + 1) * 512], AX.X, ALU.max),
                        wait=[("pe", mmid)])
                    slot_rd[sl] = ("dve", rid)
                p.cop("vector", lambda e: e.tensor_tensor(
                    BND[:, 8:9], BND[:, 4:5], BND[:, 5:6], ALU.max))
                p.cop("vector", lambda e: e.tensor_tensor(
                    BND[:, 9:10], BND[:, 6:7], BND[:, 7:8], ALU.max))
                p.cop("vector", lambda e, h=h, dstcol=dstcol: e.tensor_tensor(
                    BND[:, dstcol + h:dstcol + h + 1], BND[:, 8:9], BND[:, 9:10], ALU.max))
            return p.c("pe")

        k_ones_pe = bound_group(sqk, 2)
        sqq = p.cop("vector", lambda e: e.tensor_tensor(QSQ[:], QP[:], QP[:], ALU.mult),
                    wait=[("pe", k_ones_pe)])
        qsq_rd_pe = bound_group(sqq, 10)
        bneg_dve = [0, 0]
        for h in range(2):
            mrb = p.cop("vector", lambda e, h=h: e.tensor_reduce(
                BND[:, 8:9], RT[:, h * 4096:(h + 1) * 4096], AX.X, ALU.max),
                wait=[rt_ds[h]])
            p.cop("vector", lambda e, h=h: e.tensor_tensor(
                BND[:, 9:10], BND[:, 2 + h:3 + h], BND[:, 10 + h:11 + h], ALU.mult))
            sq = p.cop("scalar", lambda e: e.activation(
                BND[:, 9:10], BND[:, 9:10], AF.Sqrt, bias=eps_col[:], scale=1.0),
                wait=[("dve", p.c("dve"))])
            bneg_dve[h] = p.cop("vector", lambda e, h=h: e.tensor_scalar(
                BND[:, h:h + 1], BND[:, 9:10], -1.02, BND[:, 8:9], ALU.mult, ALU.subtract),
                wait=[("act", sq)])

        bounds_dve = p.c("dve")
        for h in range(2):
            hb = h * 64
            for qc in range(4):
                ex_ids = [0] * 16
                for kcp in range(8):
                    reg = kcp % 2
                    for sub in range(2):
                        kc = 2 * kcp + sub
                        off = 2047 - 128 * kc + qc * 512
                        w = []
                        if sub == 0:
                            w = [sps_rd[reg]]
                            if qc == 0 and kc == 0:
                                w += [rt_ds[h], ("dve", bounds_dve)]
                        p.pe(lambda e, h=h, off=off, reg=reg, sub=sub: e.matmul(
                            sreg[reg][:, sub * 512:(sub + 1) * 512], identb[:],
                            RT[:, h * 4096 + off: h * 4096 + off + 512],
                            start=True, stop=False), wait=w)
                        p.pe(lambda e, h=h, qc=qc, kc=kc, reg=reg, sub=sub: e.matmul(
                            sreg[reg][:, sub * 512:(sub + 1) * 512],
                            KP2[:, h * 2048 + kc * 128: h * 2048 + (kc + 1) * 128],
                            QP[:, qc * 512:(qc + 1) * 512],
                            start=False, stop=True),
                            wait=([("dve", ev_k)] if (qc == 0 and kc == 0) else ()),
                            inc=(sub == 1))
                    kqmm = p.c("pe")
                    exw = [("pe", max(kqmm, pbt_rd_pe[2 * kcp], pbt_rd_pe[2 * kcp + 1]))]
                    if qc == 0 and kcp == 0:
                        exw.append(("dve", bneg_dve[h]))
                    ex = p.cop("scalar", lambda e, reg=reg, kcp=kcp, h=h: e.activation(
                        PBT[:, kcp * 1024:(kcp + 1) * 1024], sreg[reg][:], AF.Exp,
                        bias=BND[:, h:h + 1], scale=1.0), wait=exw)
                    ex_ids[2 * kcp] = ex_ids[2 * kcp + 1] = ex
                    sps_rd[reg] = ("act", ex)
                for qt in range(4):
                    cp = qt % 2
                    j = qc * 4 + qt
                    ctx_mm = 0
                    for kc in range(16):
                        cw = []
                        if qt == 0:
                            cw.append(("act", ex_ids[kc]))
                        if kc == 0:
                            cw += [vcp, cevac[cp]]
                        ctx_mm = p.pe(lambda e, h=h, kc=kc, cp=cp, qt=qt: e.matmul(
                            cslot[cp][:],
                            PBT[:, kc * 512 + qt * 128: kc * 512 + qt * 128 + 128],
                            VTM[:, kc * 130 + h * 65: kc * 130 + h * 65 + 65],
                            start=(kc == 0), stop=(kc == 15)),
                            wait=cw, inc=(kc == 15 or qt == 3))
                        if qt == 3:
                            pbt_rd_pe[kc] = p.c("pe")
                    ctx_mm = p.c("pe")
                    rcp = p.cop("vector", lambda e, cp=cp: e.reciprocal(
                        recipc[:, cp:cp + 1], cslot[cp][:, 64:65]),
                        wait=[("pe", ctx_mm)])
                    ev = p.cop("vector", lambda e, cp=cp: e.tensor_scalar(
                        ctxh[:, cp * 64:(cp + 1) * 64], cslot[cp][:, 0:64],
                        recipc[:, cp:cp + 1], None, ALU.mult),
                        wait=[("pe", ctxh_tr[cp])])
                    cevac[cp] = ("dve", ev)
                    tr = p.pe(lambda e, cp=cp: e.transpose(
                        PT_ps[0][0:64, cp * 128:(cp + 1) * 128],
                        ctxh[:, cp * 64:(cp + 1) * 64], identb[:]),
                        wait=[("dve", ev), pt_ps_rd[0]], inc=True)
                    ctxh_tr[cp] = tr
                    cpy = p.cop("vector", lambda e, c=c, hb=hb, j=j, cp=cp: e.tensor_copy(
                        CTX[hb:hb + 64, c * 2048 + j * 128: c * 2048 + (j + 1) * 128],
                        PT_ps[0][0:64, cp * 128:(cp + 1) * 128]),
                        wait=[("pe", tr)])
                    pt_ps_rd[0] = ("dve", cpy)
            rt_rd_pe[h] = p.c("pe")
        qp_rd_pe = p.c("pe")
        vtm_rd_pe = p.c("pe")
        slot_rd[0] = slot_rd[1] = sps_rd[0]
        slot_rd[2] = slot_rd[3] = sps_rd[1]

    ctx_done = ("dve", p.c("dve"))
    attn_pe_last = p.c("pe")
    st_aps.close()
    st_ap.close()

    st_w = ExitStack()
    WOT = st_w.enter_context(nc.sbuf_tensor("WOT", [128, NDC * 1024], BF16))
    wo_ps = [st_w.enter_context(nc.psum_tensor(f"wo_ps{i}", [128, 512], F32)) for i in range(2)]
    bo_ps = st_w.enter_context(nc.psum_tensor("bo_ps", [128, 1024], F32))

    ds_bo = p.dma("sync", lambda e: e.dma_start(bo_row[:], bo_d))
    ds_wot = 0
    for oc in range(NDC):
        ds_wot = p.dma("gpsimd", lambda e, oc=oc: e.dma_start(
            WOT[:, oc * 1024:(oc + 1) * 1024], woT_d[oc * 128:(oc + 1) * 128, :]),
            wait=([dg_qkvo, ("pe", p.c("pe")), ("act", p.c("act")),
                   ("pool", p.c("pool")), ("dve", p.c("dve"))] if oc == 0 else ()))
    bomm = None
    for half in range(2):
        bomm = p.pe(lambda e, half=half: e.matmul(
            bo_ps[:, half * 512:(half + 1) * 512], ones_sb[:],
            bo_row[:, half * 512:(half + 1) * 512], start=True, stop=True),
            wait=([ds_bo, ds_ones, ctx_done] if half == 0 else ()), inc=(half == 1))

    wo_rd = [0, 0]
    attn_x_done = [0] * NTS
    for ts in range(NTS):
        for bank in range(2):
            par = bank
            for oc in range(NDC):
                w = []
                if oc == 0:
                    w = [ctx_done, ("dve", wo_rd[par]), ds_wot]
                p.pe(lambda e, ts=ts, bank=bank, oc=oc, par=par: e.matmul(
                    wo_ps[par][:],
                    CTX[:, oc * 2048 + ts * 128: oc * 2048 + (ts + 1) * 128],
                    WOT[:, oc * 1024 + bank * 512: oc * 1024 + (bank + 1) * 512],
                    start=(oc == 0), stop=(oc == NDC - 1)),
                    wait=w, inc=(oc == NDC - 1))
            wmm = p.c("pe")
            xsl = X[:, ts * 1024 + bank * 512: ts * 1024 + (bank + 1) * 512]
            p.cop("vector", lambda e, xsl=xsl, par=par: e.tensor_tensor(
                xsl, xsl, wo_ps[par][:], ALU.add), wait=[("pe", wmm)])
            tt2 = p.cop("vector", lambda e, xsl=xsl, bank=bank: e.tensor_tensor(
                xsl, xsl, bo_ps[:, bank * 512:(bank + 1) * 512], ALU.add),
                wait=[("pe", bomm)])
            wo_rd[par] = tt2
        attn_x_done[ts] = p.c("dve")

    if _DEBUG:
        p.dma("sync", lambda e: e.dma_start(taps["xattn"], X[:]),
              wait=[("dve", attn_x_done[NTS - 1])])
        p.barrier()
    st_w.close()
    st_a.close()
    p.barrier()

    st_f = ExitStack()

    def sbf(name, shape, dt):
        return st_f.enter_context(nc.sbuf_tensor(name, shape, dt))

    XLN2 = sbf("XLN2", [128, NDC * 1024], BF16)
    HID = sbf("HID", [128, 32 * 1024], BF16)
    W2Q = sbf("W2Q", [128, 2 * 32 * 256], BF16)
    W1R = sbf("W1R", [128, 2 * 2048], BF16)
    b1_sb = sbf("b1_sb", [128, E * 32], F32)
    b2_sb = sbf("b2_sb", [E, D], F32)
    WG = sbf("WG", [128, 8 * E], F32)
    WMT = sbf("WMT", [8, 8 * 128], F32)
    GSC = sbf("GSC", [128, 8 * 8], F32)
    gcol = sbf("gcol", [128, 8], F32)
    tmpq = sbf("tmpq", [128, 2 * 256], F32)

    w1_ps = [st_f.enter_context(nc.psum_tensor(f"w1_ps{i}", [128, 512], F32)) for i in range(2)]
    w2_ps = [st_f.enter_context(nc.psum_tensor(f"w2_ps{i}", [128, 256], F32)) for i in range(2)]
    psT2 = [st_f.enter_context(nc.psum_tensor(f"psT2_{i}", [128, 1024], BF16)) for i in range(2)]
    psT[0], psT[1] = psT2[0], psT2[1]

    ds_b1 = p.dma("sync", lambda e: e.dma_start(
        b1_sb[:].rearrange("p (e f) -> p e f", e=E), b1_d.rearrange("e p f -> p e f")))
    ds_b2 = p.dma("sync", lambda e: e.dma_start(b2_sb[:], b2_d))
    w1_slot_pe = [0, 0]
    w1_slot_ds = [("ds0", 0), ("ds0", 0)]
    w2_buf_pe = [0, 0]
    w2_buf_da = [("da0", 0), ("da1", 0)]
    w1_war = [0, 0]
    w2_war = [0, 0]
    w2_evac_war = [0, 0]
    hid_rd_pe = 0
    xln_rd_pe = attn_pe_last

    def issue_w1_load(th, e_idx, fcp):
        slot = (th * 128 + e_idx * 16 + fcp) % 2
        w1_slot_ds[slot] = p.dma("gpsimd", lambda en, e_idx=e_idx, fcp=fcp, slot=slot: en.dma_start(
            W1R[:, slot * 2048:(slot + 1) * 2048].rearrange("p (c f) -> p c f", c=NDC),
            w1b_d[e_idx][:, fcp * 256:(fcp + 1) * 256].rearrange("(c p) f -> p c f", p=128)),
            wait=[("pe", w1_slot_pe[slot]), w1cast[e_idx]])
        return slot

    for th in range(2):
        ln2_ids = []
        for tsl in range(8):
            ts = th * 8 + tsl
            ln2_ids.append(emit_ln(ts, XLN2, 1024, tsl * 128, [("dve", attn_x_done[ts])], xln_rd_pe))
        ln2_half = [ln2_ids[3], ln2_ids[7]]
        ln2_done = ln2_ids[7]
        b2add_done = 0
        gmm = None
        for tsl in range(8):
            ts = th * 8 + tsl
            for dc in range(NDC):
                gmm = p.pe(lambda e, dc=dc, tsl=tsl: e.matmul(
                    w2_ps[0][:, 0:8],
                    XLN2[:, dc * 1024 + tsl * 128: dc * 1024 + (tsl + 1) * 128],
                    gw_sb[:, dc * 8:(dc + 1) * 8],
                    start=(dc == 0), stop=(dc == NDC - 1)),
                    wait=([("dve", ln2_ids[tsl]), ("act", p.c("act")), dg_qkvo]
                          if dc == 0 else ()),
                    inc=(dc == NDC - 1))
            nmg = p.cop("vector", lambda e: e.tensor_reduce(
                gcol[:, 0:1], w2_ps[0][:, 0:8], AX.X, ALU.max, negate=True),
                wait=[("pe", gmm)])
            exg = p.cop("scalar", lambda e: e.activation(
                GSC[:, 0:8], w2_ps[0][:, 0:8], AF.Exp, bias=gcol[:, 0:1], scale=1.0,
                accum_out=gcol[:, 1:2]), wait=[("dve", nmg)])
            rcg = p.cop("vector", lambda e: e.reciprocal(gcol[:, 2:3], gcol[:, 1:2]),
                        wait=[("act", exg)])
            p.cop("vector", lambda e: e.tensor_scalar(
                GSC[:, 8:16], GSC[:, 0:8], gcol[:, 2:3], None, ALU.mult))
            p.cop("vector", lambda e: e.tensor_reduce(
                gcol[:, 3:4], GSC[:, 8:16], AX.X, ALU.max))
            p.cop("vector", lambda e: e.tensor_scalar(
                GSC[:, 16:24], GSC[:, 8:16], gcol[:, 3:4], None, ALU.is_ge))
            p.cop("vector", lambda e: e.tensor_scalar(
                GSC[:, 24:32], GSC[:, 16:24], -2.0, None, ALU.mult))
            p.cop("vector", lambda e: e.tensor_tensor(
                GSC[:, 24:32], GSC[:, 8:16], GSC[:, 24:32], ALU.add))
            p.cop("vector", lambda e: e.tensor_reduce(
                gcol[:, 4:5], GSC[:, 24:32], AX.X, ALU.max))
            p.cop("vector", lambda e: e.tensor_scalar(
                GSC[:, 32:40], GSC[:, 24:32], gcol[:, 4:5], None, ALU.is_ge))
            p.cop("vector", lambda e: e.tensor_tensor(
                GSC[:, 16:24], GSC[:, 16:24], GSC[:, 32:40], ALU.add))
            wgw = p.cop("vector", lambda e, tsl=tsl: e.tensor_tensor(
                WG[:, tsl * 8:(tsl + 1) * 8], GSC[:, 16:24], GSC[:, 8:16], ALU.mult),
                wait=[("act", p.c("act"))])
            wt = p.pe(lambda e, tsl=tsl: e.transpose(
                w2_ps[1][0:8, 0:128], WG[:, tsl * 8:(tsl + 1) * 8], identf[:]),
                wait=[("dve", wgw), ("act", p.c("act")), ds_identf], inc=True)
            wcp = p.cop("vector", lambda e, tsl=tsl: e.tensor_copy(
                WMT[:, tsl * 128:(tsl + 1) * 128], w2_ps[1][0:8, 0:128]),
                wait=[("pe", max(wt, xln_rd_pe))])
            for bank in range(2):
                bmm = p.pe(lambda e, tsl=tsl, bank=bank: e.matmul(
                    w1_ps[0][:],
                    WMT[:, tsl * 128:(tsl + 1) * 128],
                    b2_sb[:, bank * 512:(bank + 1) * 512],
                    start=True, stop=True),
                    wait=[("dve", max(wcp, b2add_done)), ("act", p.c("act")), ds_b1, ds_b2],
                    inc=True)
                b2add_done = p.cop("vector", lambda e, ts=ts, bank=bank: e.tensor_tensor(
                    X[:, ts * 1024 + bank * 512: ts * 1024 + (bank + 1) * 512],
                    X[:, ts * 1024 + bank * 512: ts * 1024 + (bank + 1) * 512],
                    w1_ps[0][:], ALU.add), wait=[("pe", bmm)])

        for e_idx in range(E):
            def issue_w2_load(q, e_idx=None):
                qb = q % 2
                w2_buf_da[qb] = p.dma("gpsimd", lambda en, e_idx=e_idx, q=q, qb=qb: en.dma_start(
                    W2Q[:, qb * 8192:(qb + 1) * 8192].rearrange("p (fc h) -> p fc h", fc=32),
                    w2_d[e_idx][:, q * 256:(q + 1) * 256].rearrange("(fc p) h -> p fc h", p=128)),
                    wait=[("pe", w2_buf_pe[qb])])

            if th == 0 and e_idx == 0:
                for fcp in range(2):
                    issue_w1_load(th, e_idx, fcp)
                issue_w2_load(0, e_idx=e_idx)
                issue_w2_load(1, e_idx=e_idx)
            e_next = e_idx + 1 if e_idx + 1 < E else (0 if th == 0 else None)
            relu_done = 0
            for fc in range(32):
                fcp = fc // 2
                slot = (th * 128 + e_idx * 16 + fcp) % 2
                sub = (fc % 2) * 128
                for tb in range(2):
                    par = (fc * 2 + tb) % 2
                    for dc in range(NDC):
                        w = []
                        if dc == 0:
                            w = [w1_slot_ds[slot],
                                 ("dve", max(ln2_half[tb], b2add_done)),
                                 ("act", w1_war[par])]
                        p.pe(lambda e, slot=slot, sub=sub, dc=dc, tb=tb, par=par: e.matmul(
                            w1_ps[par][:],
                            W1R[:, slot * 2048 + dc * 256 + sub: slot * 2048 + dc * 256 + sub + 128],
                            XLN2[:, dc * 1024 + tb * 512: dc * 1024 + (tb + 1) * 512],
                            start=(dc == 0), stop=(dc == NDC - 1)),
                            wait=w, inc=(dc == NDC - 1))
                    wmm = p.c("pe")
                    if tb == 1:
                        w1_slot_pe[slot] = wmm
                        if fc % 2 == 1 and fcp + 2 < 16:
                            issue_w1_load(th, e_idx, fcp + 2)
                    relu_done = p.cop("scalar", lambda e, e_idx=e_idx, fc=fc, tb=tb, par=par: e.activation(
                        HID[:, fc * 1024 + tb * 512: fc * 1024 + (tb + 1) * 512],
                        w1_ps[par][:], AF.Relu,
                        bias=b1_sb[:, e_idx * 32 + fc: e_idx * 32 + fc + 1], scale=1.0),
                        wait=[("pe", max(wmm, hid_rd_pe))])
                    w1_war[par] = relu_done
            if e_next is not None:
                issue_w1_load(th, e_next, 0)
                issue_w1_load(th, e_next, 1)
            for q in range(4):
                qb = q % 2
                for tsl in range(8):
                    ts = th * 8 + tsl
                    par = (q * 8 + tsl) % 2
                    for fc in range(32):
                        w = []
                        if fc == 0:
                            w = [w2_buf_da[qb], ("act", max(relu_done, w2_war[par]))]
                        p.pe(lambda e, fc=fc, tsl=tsl, qb=qb, par=par: e.matmul(
                            w2_ps[par][:],
                            HID[:, fc * 1024 + tsl * 128: fc * 1024 + tsl * 128 + 128],
                            W2Q[:, qb * 8192 + fc * 256: qb * 8192 + (fc + 1) * 256],
                            start=(fc == 0), stop=(fc == 31)),
                            wait=w, inc=(fc == 31))
                    w2mm = p.c("pe")
                    hid_rd_pe = w2mm
                    if tsl == 7:
                        w2_buf_pe[qb] = w2mm
                    evq = p.cop("scalar", lambda e, par=par, tsl=tsl, e_idx=e_idx: e.activation(
                        tmpq[:, par * 256:(par + 1) * 256], w2_ps[par][:], AF.Copy,
                        bias=0.0, scale=WG[:, tsl * 8 + e_idx: tsl * 8 + e_idx + 1]),
                        wait=[("pe", w2mm), ("pool", w2_evac_war[par])])
                    w2_war[par] = evq
                    xadd = p.cop("gpsimd", lambda e, ts=ts, q=q, par=par: e.tensor_tensor(
                        X[:, ts * 1024 + q * 256: ts * 1024 + q * 256 + 256],
                        X[:, ts * 1024 + q * 256: ts * 1024 + q * 256 + 256],
                        tmpq[:, par * 256:(par + 1) * 256], ALU.add),
                        wait=[("act", evq)])
                    w2_evac_war[par] = xadd
                if q < 2:
                    issue_w2_load(q + 2, e_idx=e_idx)
                elif e_next is not None:
                    issue_w2_load(q - 2, e_idx=e_next)
        if th == 0:
            ds_out0 = p.dma("sync", lambda e: e.dma_start(
                out_d.rearrange("(t p) d -> p t d", p=128)[:, 0:8],
                X[:, 0:8 * 1024].rearrange("p (t d) -> p t d", t=8)),
                wait=[("pool", p.c("pool"))])
        xln_rd_pe = p.c("pe")

    if _DEBUG:
        p.dma("sync", lambda e: e.dma_start(taps["wg"], WG[:]), wait=[("dve", p.c("dve"))])
        p.dma("gpsimd", lambda e: e.dma_start(taps["hid"], HID[:, 0:1024]), wait=[("act", p.c("act"))])

    st_f.close()

    p.barrier()
    ds_out = p.dma("sync", lambda e: e.dma_start(
        out_d.rearrange("(t p) d -> p t d", p=128)[:, 8:16],
        X[:, 8 * 1024:16 * 1024].rearrange("p (t d) -> p t d", t=8)))
    p.emit("sync", None, [ds_out, ds_out0], ())

    with nc.Block() as block:
        for engname in ENGS:
            def mk(name):
                def body(eng):
                    for wait, fn, inc in p.q[name]:
                        for s, v in wait:
                            eng.wait_ge(sem_objs[s], v)
                        inst = fn(eng) if fn is not None else None
                        for s, n in inc:
                            inst.then_inc(sem_objs[s], n)
                return body
            getattr(block, engname)(mk(engname))

    st_g.close()
    return nc


def _host_arrays(L0_attn_Wq, L0_attn_bq, L0_attn_Wk, L0_attn_bk, L0_attn_Wv, L0_attn_bv,
                 L0_attn_Wo, L0_attn_bo, L0_attn_rel_bias, L0_ffn_gate_w,
                 L0_ffn_w1, L0_ffn_b1, L0_ffn_w2, L0_ffn_b2):
    f = np.float32
    c = np.ascontiguousarray
    rel = np.asarray(L0_attn_rel_bias, dtype=f)
    idxT = np.clip(np.arange(4096)[None, :] - 2047 - np.arange(128)[:, None], -128, 128) + 128
    rexp = c(rel[:, idxT])
    shared = dict(
        wqT=c(np.asarray(L0_attn_Wq, f).T), wkT=c(np.asarray(L0_attn_Wk, f).T),
        wvT=c(np.asarray(L0_attn_Wv, f).T), woT=c(np.asarray(L0_attn_Wo, f).T),
        rexp=rexp,
        gwT=c(np.asarray(L0_ffn_gate_w, f).T),
        bq_c=c(np.asarray(L0_attn_bq, f).reshape(8, 128).T),
        bk_c=c(np.asarray(L0_attn_bk, f).reshape(8, 128).T),
        bv_c=c(np.asarray(L0_attn_bv, f).reshape(8, 128).T),
        bo_r=c(np.asarray(L0_attn_bo, f).reshape(1, D)),
        b1_c=c(np.asarray(L0_ffn_b1, f).reshape(E, 32, 128).transpose(0, 2, 1)),
        b2_r=c(np.asarray(L0_ffn_b2, f)),
        w1=c(np.asarray(L0_ffn_w1, f)),
        w2=c(np.asarray(L0_ffn_w2, f)),
        ones_r=np.ones((1, 128), f),
        ident=np.eye(128, dtype=f),
    )
    return shared


def kernel(L0_attn_x, L0_attn_Wq, L0_attn_bq, L0_attn_Wk, L0_attn_bk, L0_attn_Wv,
           L0_attn_bv, L0_attn_Wo, L0_attn_bo, L0_attn_rel_bias, L0_ffn_gate_w,
           L0_ffn_w1, L0_ffn_b1, L0_ffn_w2, L0_ffn_b2):
    x = np.asarray(L0_attn_x, dtype=np.float32)
    shared = _host_arrays(L0_attn_Wq, L0_attn_bq, L0_attn_Wk, L0_attn_bk, L0_attn_Wv,
                          L0_attn_bv, L0_attn_Wo, L0_attn_bo, L0_attn_rel_bias,
                          L0_ffn_gate_w, L0_ffn_w1, L0_ffn_b1, L0_ffn_w2, L0_ffn_b2)
    nc = build_program()
    in_maps = [dict(shared, x=np.ascontiguousarray(x[ci])) for ci in range(8)]
    res = run_bass_kernel_spmd(nc, in_maps, core_ids=list(range(8)))
    return np.stack([r["out"] for r in res.results]).astype(np.float32)


# version 24
# speedup vs baseline: 1.1078x; 1.0943x over previous
import numpy as np
import concourse.bass as bass
import concourse.mybir as mybir
from concourse.bass_utils import run_bass_kernel_spmd
from contextlib import ExitStack

F32 = mybir.dt.float32
BF16 = mybir.dt.bfloat16
AF = mybir.ActivationFunctionType
ALU = mybir.AluOpType
AX = mybir.AxisListType

_DEBUG = False

T, D, H, DK, E, FF = 2048, 1024, 16, 64, 8, 4096
NDC = 8
NTS = 16
NPAIR = 8

ENGS = ("sync", "gpsimd", "tensor", "vector", "scalar")
SEMS = ("pe", "act", "dve", "pool", "ds0", "ds1",
        "dg", "dg1", "dg2", "dg3", "dg4", "dg5", "da0", "da1")
ESEM = {"tensor": "pe", "vector": "dve", "scalar": "act", "gpsimd": "pool"}
DSEM = {"sync": ("ds0", "ds1"),
        "gpsimd": ("dg", "dg1", "dg2", "dg3", "dg4", "dg5"),
        "scalar": ("da0", "da1")}


class Prog:
    def __init__(self):
        self.q = {e: [] for e in ENGS}
        self.cnt = {s: 0 for s in SEMS}
        self.dq = {e: 0 for e in DSEM}

    def c(self, s):
        return self.cnt[s]

    def emit(self, eng, fn, wait=(), inc=()):
        wd = {}
        for s, v in wait:
            if v > 0:
                wd[s] = max(wd.get(s, 0), v)
        for s, n in inc:
            self.cnt[s] += n
        self.q[eng].append((tuple(wd.items()), fn, tuple(inc)))
        return {s: self.cnt[s] for s, _ in inc}

    def cop(self, eng, fn, wait=()):
        s = ESEM[eng]
        w = list(wait)
        if eng != "tensor":
            w.append((s, self.c(s)))
        return self.emit(eng, fn, w, [(s, 1)])[s]

    def pe(self, fn, wait=(), inc=False):
        if inc:
            return self.emit("tensor", fn, wait, [("pe", 1)])["pe"]
        self.emit("tensor", fn, wait, ())
        return None

    def dma(self, eng, fn, wait=()):
        sems = DSEM[eng]
        s = sems[self.dq[eng] % len(sems)]
        self.dq[eng] += 1
        w = list(wait)
        if len(sems) > 1 and self.c(s) > 0:
            w.append((s, self.c(s)))
        return (s, self.emit(eng, fn, w, [(s, 16)])[s])

    def barrier(self):
        snap = [(s, self.c(s)) for s in SEMS]
        for e in ENGS:
            self.emit(e, None, snap, ())


def build_program():
    nc = bass.Bass("TRN2", target_bir_lowering=False)
    p = Prog()

    x_d = nc.dram_tensor("x", [T, D], F32, kind="ExternalInput").ap()
    wqT_d = nc.dram_tensor("wqT", [D, D], F32, kind="ExternalInput").ap()
    wkT_d = nc.dram_tensor("wkT", [D, D], F32, kind="ExternalInput").ap()
    wvT_d = nc.dram_tensor("wvT", [D, D], F32, kind="ExternalInput").ap()
    woT_d = nc.dram_tensor("woT", [D, D], F32, kind="ExternalInput").ap()
    rexp_d = nc.dram_tensor("rexp", [H, 128, 4096], F32, kind="ExternalInput").ap()
    gwT_d = nc.dram_tensor("gwT", [D, E], F32, kind="ExternalInput").ap()
    bq_d = nc.dram_tensor("bq_c", [128, 8], F32, kind="ExternalInput").ap()
    bk_d = nc.dram_tensor("bk_c", [128, 8], F32, kind="ExternalInput").ap()
    bv_d = nc.dram_tensor("bv_c", [128, 8], F32, kind="ExternalInput").ap()
    bo_d = nc.dram_tensor("bo_r", [1, D], F32, kind="ExternalInput").ap()
    b1_d = nc.dram_tensor("b1_c", [E, 128, 32], F32, kind="ExternalInput").ap()
    b2_d = nc.dram_tensor("b2_r", [E, D], F32, kind="ExternalInput").ap()
    w1_d = nc.dram_tensor("w1", [E, D, FF], F32, kind="ExternalInput").ap()
    w2_d = nc.dram_tensor("w2", [E, FF, D], F32, kind="ExternalInput").ap()
    ones_d = nc.dram_tensor("ones_r", [1, 128], F32, kind="ExternalInput").ap()
    ident_d = nc.dram_tensor("ident", [128, 128], F32, kind="ExternalInput").ap()
    out_d = nc.dram_tensor("out", [T, D], F32, kind="ExternalOutput").ap()
    w1b_d = nc.dram_tensor("w1b", [E, D, FF], BF16).ap()

    taps = {}
    if _DEBUG:
        taps["xln1"] = nc.dram_tensor("tap_xln1", [128, NDC * T], F32, kind="ExternalOutput").ap()
        taps["xattn"] = nc.dram_tensor("tap_xattn", [128, NTS * D], F32, kind="ExternalOutput").ap()
        taps["wg"] = nc.dram_tensor("tap_wg", [128, 64], F32, kind="ExternalOutput").ap()
        taps["qp"] = nc.dram_tensor("tap_qp", [128, 2048], F32, kind="ExternalOutput").ap()
        taps["kp"] = nc.dram_tensor("tap_kp", [128, 2048], F32, kind="ExternalOutput").ap()
        taps["vtm"] = nc.dram_tensor("tap_vtm", [128, 2048], F32, kind="ExternalOutput").ap()
        taps["pb"] = nc.dram_tensor("tap_pb", [128, 2048], F32, kind="ExternalOutput").ap()
        taps["ctx"] = nc.dram_tensor("tap_ctx", [128, NPAIR * 2048], F32, kind="ExternalOutput").ap()
        taps["hid"] = nc.dram_tensor("tap_hid", [128, 1024], F32, kind="ExternalOutput").ap()

    st_g = ExitStack()

    def sb(name, shape, dt):
        return st_g.enter_context(nc.sbuf_tensor(name, shape, dt))

    X = sb("X", [128, NTS * 1024], F32)
    identb = sb("identb", [128, 128], BF16)
    identf = sb("identf", [128, 128], F32)
    ones_sb = sb("ones_sb", [1, 128], F32)
    eps_col = sb("eps_col", [128, 1], F32)
    bq_sb = sb("bq_sb", [128, 8], F32)
    bk_sb = sb("bk_sb", [128, 8], F32)
    bv_sb = sb("bv_sb", [128, 8], F32)
    gw_sb = sb("gw_sb", [128, NDC * E], BF16)
    lnstats = sb("lnstats", [128, 2 * 12], F32)
    lnmv = sb("lnmv", [128, 2 * 2], F32)
    lnsq = sb("lnsq", [128, 2], F32)
    lnrstd = sb("lnrstd", [128, 2], F32)
    lntm = sb("lntm", [128, 1024], BF16)

    sem_objs = {}
    for s in SEMS:
        sem_objs[s] = st_g.enter_context(nc.semaphore("sem_" + s))

    ds_x = []
    for xc in range(4):
        ds_x.append(p.dma("sync", lambda e, xc=xc: e.dma_start(
            X[:, xc * 4096:(xc + 1) * 4096].rearrange("p (t d) -> p t d", t=4),
            x_d.rearrange("(t p) d -> p t d", p=128)[:, xc * 4:(xc + 1) * 4])))
    ds_identf = p.dma("sync", lambda e: e.dma_start(identf[:], ident_d))
    ds_ones = p.dma("sync", lambda e: e.dma_start(ones_sb[:], ones_d))
    p.dma("sync", lambda e: e.dma_start(bq_sb[:], bq_d))
    p.dma("sync", lambda e: e.dma_start(bk_sb[:], bk_d))
    p.dma("sync", lambda e: e.dma_start(bv_sb[:], bv_d))

    p.cop("gpsimd", lambda e: e.memset(eps_col[:], 1e-5))
    p.dma("gpsimd", lambda e: e.dma_start(identb[:], ident_d))
    p.dma("gpsimd", lambda e: e.dma_start(
        gw_sb[:].rearrange("p (c e) -> p c e", c=NDC), gwT_d.rearrange("(c p) e -> p c e", p=128)))
    dg_qkvo = ("dg", 0)

    psT = [None, None]
    ln_war = {"sqrt": [0, 0], "recip": [0, 0], "trans": [0, 0], "copy": [0, 0]}

    def emit_ln(ts, dst, dst_stride, dst_off, x_wait, xln_war_pe):
        par = ts % 2
        so, mo = par * 12, par * 2
        pt = psT[par]
        xs = X[:, ts * 1024:(ts + 1) * 1024]
        p.cop("vector", lambda e, xs=xs, so=so: e.bn_stats(lnstats[:, so:so + 6], xs[:, 0:512]),
              wait=x_wait)
        p.cop("vector", lambda e, xs=xs, so=so: e.bn_stats(lnstats[:, so + 6:so + 12], xs[:, 512:1024]))
        aggr = p.cop("vector", lambda e, so=so, mo=mo: e.bn_aggr(lnmv[:, mo:mo + 2], lnstats[:, so:so + 12]),
                     wait=[("act", ln_war["sqrt"][par])])
        sq = p.cop("scalar", lambda e, par=par, mo=mo: e.activation(
            lnsq[:, par:par + 1], lnmv[:, mo + 1:mo + 2], AF.Sqrt, bias=eps_col[:], scale=1.0),
            wait=[("dve", max(aggr, ln_war["recip"][par]))])
        ln_war["sqrt"][par] = sq
        rc = p.cop("vector", lambda e, par=par: e.reciprocal(lnrstd[:, par:par + 1], lnsq[:, par:par + 1]),
                   wait=[("act", sq)])
        ln_war["recip"][par] = rc
        ap = p.cop("vector", lambda e, xs=xs, par=par, mo=mo: e.tensor_scalar(
            lntm[:], xs, lnmv[:, mo:mo + 1],
            lnrstd[:, par:par + 1], ALU.subtract, ALU.mult),
            wait=[("pe", ln_war["trans"][0])])
        last = None
        for c in range(NDC):
            last = p.pe(lambda e, c=c, pt=pt: e.transpose(
                pt[:, c * 128:(c + 1) * 128],
                lntm[:, c * 128:(c + 1) * 128], identb[:]),
                wait=([("dve", max(ap, ln_war["copy"][par]))] if c == 0 else ()),
                inc=(c == NDC - 1))
        ln_war["trans"][0] = last
        cp = p.cop("vector", lambda e, pt=pt, dst=dst, dst_off=dst_off: e.tensor_copy(
            dst[:].rearrange("p (c t) -> p c t", c=NDC)[:, :, dst_off:dst_off + 128],
            pt[:].rearrange("p (c f) -> p c f", c=NDC)),
            wait=[("pe", last), ("pe", xln_war_pe)])
        ln_war["copy"][par] = cp
        return cp

    st_a = ExitStack()

    def sba(name, shape, dt):
        return st_a.enter_context(nc.sbuf_tensor(name, shape, dt))

    XLN = sba("XLN", [128, NDC * 2048], BF16)
    bo_row = sba("bo_row", [1, D], F32)
    CTX = sba("CTX", [128, NPAIR * 2048], BF16)

    ln1_done = 0
    with nc.psum_tensor("psT0", [128, 1024], BF16) as psT0, \
         nc.psum_tensor("psT1", [128, 1024], BF16) as psT1:
        psT[0], psT[1] = psT0, psT1
        for ts in range(NTS):
            ln1_done = emit_ln(ts, XLN, 2048, ts * 128, [ds_x[ts // 4], ("pool", 1)], 0)

    if _DEBUG:
        p.dma("gpsimd", lambda e: e.dma_start(taps["xln1"], XLN[:]), wait=[("dve", ln1_done)])

    st_ap = ExitStack()

    def sbp(name, shape, dt):
        return st_ap.enter_context(nc.sbuf_tensor(name, shape, dt))

    QP = sbp("QP", [128, 2048], BF16)
    KP2 = sbp("KP2", [128, 2 * 2048], BF16)
    VTM = sbp("VTM", [128, 16 * 130], BF16)
    VFM = sbp("VFM", [128, 2048], BF16)
    PBT = sbp("PBT", [128, 16 * 512], BF16)
    RT = sbp("RT", [128, 2 * 4096], BF16)
    WSL = sbp("WSL", [128, 3 * 1024], BF16)
    QSQ = sbp("QSQ", [128, 2048], BF16)
    onesb = sbp("onesb", [128, 128], BF16)
    BND = sbp("BND", [128, 12], F32)
    ctxh = sbp("ctxh", [128, 2 * 64], BF16)
    recipc = sbp("recipc", [128, 2], F32)

    st_aps = ExitStack()
    S_ps = st_aps.enter_context(nc.psum_tensor("S_ps", [128, 2048], F32))
    PT_ps = [st_aps.enter_context(nc.psum_tensor("PT_ps0", [128, 1024], BF16))]
    CTXPS = st_aps.enter_context(nc.psum_tensor("CTXPS", [128, 512], F32))
    sreg = [S_ps[:, 0:1024], S_ps[:, 1024:2048]]
    cslot = [CTXPS[:, 0:65], CTXPS[:, 65:130]]

    p.cop("gpsimd", lambda e: e.memset(onesb[:], 1.0))
    p.cop("gpsimd", lambda e: e.memset(KP2[64:128, 0:2048], 0.0))
    p.cop("gpsimd", lambda e: e.memset(KP2[0:64, 2048:4096], 0.0))
    for kc16 in range(16):
        for h2 in range(2):
            p.cop("gpsimd", lambda e, kc16=kc16, h2=h2: e.memset(
                VTM[:, kc16 * 130 + h2 * 65 + 64: kc16 * 130 + h2 * 65 + 65], 1.0))
    vtm_ones = p.c("pool")

    wsl_rd_pe = [0, 0, 0]
    rt_rd_pe = [0, 0]
    qp_rd_pe = 0
    vfm_rd_pe = 0
    vtm_rd_pe = 0
    qsq_rd_pe = 0
    pt_ps_rd = [("dve", 0)]
    slot_rd = [("dve", 0)] * 4
    sps_rd = [("act", 0), ("act", 0)]
    pbt_rd_pe = [0] * 16
    cevac = [("dve", 0), ("dve", 0)]
    ctxh_tr = [0, 0]

    wq_srcs = (wqT_d, wkT_d, wvT_d)
    w1cast = []

    for c in range(NPAIR):
        wsl_ds = []
        for m in range(3):
            wsl_ds.append(p.dma("gpsimd", lambda e, m=m, c=c: e.dma_start(
                WSL[:, m * 1024:(m + 1) * 1024].rearrange("p (k o) -> p k o", k=NDC),
                wq_srcs[m][:, c * 128:(c + 1) * 128].rearrange("(k p) o -> p k o", p=128)),
                wait=[("pe", wsl_rd_pe[m])]))
        rt_ds = []
        for hh in range(2):
            rt_ds.append(p.dma("gpsimd", lambda e, hh=hh, c=c: e.dma_start(
                RT[:, hh * 4096:(hh + 1) * 4096], rexp_d[2 * c + hh]),
                wait=[("pe", rt_rd_pe[hh])]))
        w1cast.append(p.dma("gpsimd", lambda e, c=c: e.dma_start(
            w1b_d[c], w1_d[c], max_dma_last_dim=8192)))

        ev_q = 0
        for tb in range(4):
            for dc in range(NDC):
                for h in range(2):
                    w = []
                    if dc == 0 and h == 0:
                        w = [wsl_ds[0], slot_rd[tb], ("dve", ln1_done), ("pool", vtm_ones)]
                    p.pe(lambda e, h=h, tb=tb, dc=dc: e.matmul(
                        S_ps[h * 64:h * 64 + 64, tb * 512:(tb + 1) * 512],
                        WSL[:, dc * 128 + h * 64: dc * 128 + h * 64 + 64],
                        XLN[:, dc * 2048 + tb * 512: dc * 2048 + (tb + 1) * 512],
                        start=(dc == 0), stop=(dc == NDC - 1)),
                        wait=w, inc=(dc == NDC - 1 and h == 1))
            qmm_tb = p.c("pe")
            for h in range(2):
                ev_q = p.cop("vector", lambda e, c=c, h=h, tb=tb: e.tensor_scalar(
                    QP[h * 64:h * 64 + 64, tb * 512:(tb + 1) * 512],
                    S_ps[h * 64:h * 64 + 64, tb * 512:(tb + 1) * 512],
                    bq_sb[h * 64:h * 64 + 64, c:c + 1],
                    0.125, ALU.add, ALU.mult),
                    wait=([("pe", max(qmm_tb, qp_rd_pe))] if h == 0 else ()))
            slot_rd[tb] = ("dve", ev_q)
        qmm = p.c("pe")

        def production(m, evac_tb_fn):
            ev = 0
            for tb in range(4):
                for dc in range(NDC):
                    w = []
                    if dc == 0:
                        w = [wsl_ds[m], slot_rd[tb]]
                    p.pe(lambda e, m=m, tb=tb, dc=dc: e.matmul(
                        S_ps[:, tb * 512:(tb + 1) * 512],
                        WSL[:, m * 1024 + dc * 128: m * 1024 + (dc + 1) * 128],
                        XLN[:, dc * 2048 + tb * 512: dc * 2048 + (tb + 1) * 512],
                        start=(dc == 0), stop=(dc == NDC - 1)),
                        wait=w, inc=(dc == NDC - 1))
                ev = evac_tb_fn(tb, p.c("pe"))
                slot_rd[tb] = ("dve", ev)
            return ev

        def evac_k_tb(tb, mm):
            evk = 0
            for h in range(2):
                hb = h * 64
                evk = p.cop("vector", lambda e, c=c, h=h, hb=hb, tb=tb: e.tensor_scalar(
                    KP2[hb:hb + 64, h * 2048 + tb * 512: h * 2048 + (tb + 1) * 512],
                    S_ps[hb:hb + 64, tb * 512:(tb + 1) * 512],
                    bk_sb[hb:hb + 64, c:c + 1], None, ALU.add),
                    wait=([("pe", max(mm, qp_rd_pe))] if h == 0 else ()))
            return evk

        ev_k = production(1, evac_k_tb)
        ev_v_tb = [0] * 4

        def evac_v_tb(tb, mm):
            ev_v_tb[tb] = p.cop("vector", lambda e, c=c, tb=tb: e.tensor_scalar(
                VFM[:, tb * 512:(tb + 1) * 512], S_ps[:, tb * 512:(tb + 1) * 512],
                bv_sb[:, c:c + 1], None, ALU.add),
                wait=[("pe", max(mm, vfm_rd_pe))])
            return ev_v_tb[tb]

        ev_v = production(2, evac_v_tb)
        wsl_rd_pe[0] = qmm
        wsl_rd_pe[1] = wsl_rd_pe[2] = p.c("pe")

        vcp = ("dve", 0)
        for g in range(2):
            lastt = None
            for k8 in range(8):
                ts8 = g * 8 + k8
                lastt = p.pe(lambda e, ts8=ts8, k8=k8: e.transpose(
                    PT_ps[0][:, k8 * 128:(k8 + 1) * 128],
                    VFM[:, ts8 * 128:(ts8 + 1) * 128], identb[:]),
                    wait=([("dve", ev_v_tb[2 * g + 1]), pt_ps_rd[0]] if k8 == 0 else ()),
                    inc=(k8 == 7))
            vcp = ("dve", p.cop("vector", lambda e, g=g: e.tensor_copy(
                VTM[:].rearrange("p (k s o) -> p k s o", k=16, s=2, o=65)[:, g * 8:(g + 1) * 8, :, 0:64],
                PT_ps[0][:].rearrange("p (k s o) -> p k s o", k=8, s=2, o=64)),
                wait=[("pe", max(lastt, vtm_rd_pe))]))
            pt_ps_rd[0] = vcp
        vfm_rd_pe = p.c("pe")

        sqk = 0
        for h in range(2):
            hb = h * 64
            sqk = p.cop("vector", lambda e, h=h, hb=hb: e.tensor_tensor(
                QSQ[hb:hb + 64, :], KP2[hb:hb + 64, h * 2048:(h + 1) * 2048],
                KP2[hb:hb + 64, h * 2048:(h + 1) * 2048], ALU.mult),
                wait=[("pe", qsq_rd_pe)])

        def bound_group(sq_done, dstcol):
            for h in range(2):
                hb = h * 64
                for j in range(4):
                    sl = j
                    p.pe(lambda e, hb=hb, j=j, sl=sl: e.matmul(
                        S_ps[:, sl * 512:(sl + 1) * 512],
                        onesb[hb:hb + 64, :], QSQ[hb:hb + 64, j * 512:(j + 1) * 512],
                        start=True, stop=True),
                        wait=[("dve", sq_done), slot_rd[sl]], inc=True)
                    mmid = p.c("pe")
                    rid = p.cop("vector", lambda e, j=j, sl=sl: e.tensor_reduce(
                        BND[:, 4 + j:5 + j], S_ps[:, sl * 512:(sl + 1) * 512], AX.X, ALU.max),
                        wait=[("pe", mmid)])
                    slot_rd[sl] = ("dve", rid)
                p.cop("vector", lambda e: e.tensor_tensor(
                    BND[:, 8:9], BND[:, 4:5], BND[:, 5:6], ALU.max))
                p.cop("vector", lambda e: e.tensor_tensor(
                    BND[:, 9:10], BND[:, 6:7], BND[:, 7:8], ALU.max))
                p.cop("vector", lambda e, h=h, dstcol=dstcol: e.tensor_tensor(
                    BND[:, dstcol + h:dstcol + h + 1], BND[:, 8:9], BND[:, 9:10], ALU.max))
            return p.c("pe")

        k_ones_pe = bound_group(sqk, 2)
        sqq = p.cop("vector", lambda e: e.tensor_tensor(QSQ[:], QP[:], QP[:], ALU.mult),
                    wait=[("pe", k_ones_pe)])
        qsq_rd_pe = bound_group(sqq, 10)
        bneg_dve = [0, 0]
        for h in range(2):
            mrb = p.cop("vector", lambda e, h=h: e.tensor_reduce(
                BND[:, 8:9], RT[:, h * 4096:(h + 1) * 4096], AX.X, ALU.max),
                wait=[rt_ds[h]])
            p.cop("vector", lambda e, h=h: e.tensor_tensor(
                BND[:, 9:10], BND[:, 2 + h:3 + h], BND[:, 10 + h:11 + h], ALU.mult))
            sq = p.cop("scalar", lambda e: e.activation(
                BND[:, 9:10], BND[:, 9:10], AF.Sqrt, bias=eps_col[:], scale=1.0),
                wait=[("dve", p.c("dve"))])
            bneg_dve[h] = p.cop("vector", lambda e, h=h: e.tensor_scalar(
                BND[:, h:h + 1], BND[:, 9:10], -1.02, BND[:, 8:9], ALU.mult, ALU.subtract),
                wait=[("act", sq)])

        bounds_dve = p.c("dve")
        for h in range(2):
            hb = h * 64
            for qc in range(4):
                ex_ids = [0] * 16
                for kcp in range(8):
                    reg = kcp % 2
                    for sub in range(2):
                        kc = 2 * kcp + sub
                        off = 2047 - 128 * kc + qc * 512
                        w = []
                        if sub == 0:
                            w = [sps_rd[reg]]
                            if qc == 0 and kc == 0:
                                w += [rt_ds[h], ("dve", bounds_dve)]
                        p.pe(lambda e, h=h, off=off, reg=reg, sub=sub: e.matmul(
                            sreg[reg][:, sub * 512:(sub + 1) * 512], identb[:],
                            RT[:, h * 4096 + off: h * 4096 + off + 512],
                            start=True, stop=False), wait=w)
                        p.pe(lambda e, h=h, qc=qc, kc=kc, reg=reg, sub=sub: e.matmul(
                            sreg[reg][:, sub * 512:(sub + 1) * 512],
                            KP2[:, h * 2048 + kc * 128: h * 2048 + (kc + 1) * 128],
                            QP[:, qc * 512:(qc + 1) * 512],
                            start=False, stop=True),
                            wait=([("dve", ev_k)] if (qc == 0 and kc == 0) else ()),
                            inc=(sub == 1))
                    kqmm = p.c("pe")
                    exw = [("pe", max(kqmm, pbt_rd_pe[2 * kcp], pbt_rd_pe[2 * kcp + 1]))]
                    if qc == 0 and kcp == 0:
                        exw.append(("dve", bneg_dve[h]))
                    ex = p.cop("scalar", lambda e, reg=reg, kcp=kcp, h=h: e.activation(
                        PBT[:, kcp * 1024:(kcp + 1) * 1024], sreg[reg][:], AF.Exp,
                        bias=BND[:, h:h + 1], scale=1.0), wait=exw)
                    ex_ids[2 * kcp] = ex_ids[2 * kcp + 1] = ex
                    sps_rd[reg] = ("act", ex)
                for qt in range(4):
                    cp = qt % 2
                    j = qc * 4 + qt
                    ctx_mm = 0
                    for kc in range(16):
                        cw = []
                        if qt == 0:
                            cw.append(("act", ex_ids[kc]))
                        if kc == 0:
                            cw += [vcp, cevac[cp]]
                        ctx_mm = p.pe(lambda e, h=h, kc=kc, cp=cp, qt=qt: e.matmul(
                            cslot[cp][:],
                            PBT[:, kc * 512 + qt * 128: kc * 512 + qt * 128 + 128],
                            VTM[:, kc * 130 + h * 65: kc * 130 + h * 65 + 65],
                            start=(kc == 0), stop=(kc == 15)),
                            wait=cw, inc=(kc == 15 or qt == 3))
                        if qt == 3:
                            pbt_rd_pe[kc] = p.c("pe")
                    ctx_mm = p.c("pe")
                    rcp = p.cop("vector", lambda e, cp=cp: e.reciprocal(
                        recipc[:, cp:cp + 1], cslot[cp][:, 64:65]),
                        wait=[("pe", ctx_mm)])
                    ev = p.cop("vector", lambda e, cp=cp: e.tensor_scalar(
                        ctxh[:, cp * 64:(cp + 1) * 64], cslot[cp][:, 0:64],
                        recipc[:, cp:cp + 1], None, ALU.mult),
                        wait=[("pe", ctxh_tr[cp])])
                    cevac[cp] = ("dve", ev)
                    tr = p.pe(lambda e, cp=cp: e.transpose(
                        PT_ps[0][0:64, cp * 128:(cp + 1) * 128],
                        ctxh[:, cp * 64:(cp + 1) * 64], identb[:]),
                        wait=[("dve", ev), pt_ps_rd[0]], inc=True)
                    ctxh_tr[cp] = tr
                    cpy = p.cop("vector", lambda e, c=c, hb=hb, j=j, cp=cp: e.tensor_copy(
                        CTX[hb:hb + 64, c * 2048 + j * 128: c * 2048 + (j + 1) * 128],
                        PT_ps[0][0:64, cp * 128:(cp + 1) * 128]),
                        wait=[("pe", tr)])
                    pt_ps_rd[0] = ("dve", cpy)
            rt_rd_pe[h] = p.c("pe")
        qp_rd_pe = p.c("pe")
        vtm_rd_pe = p.c("pe")
        slot_rd[0] = slot_rd[1] = sps_rd[0]
        slot_rd[2] = slot_rd[3] = sps_rd[1]

    ctx_done = ("dve", p.c("dve"))
    attn_pe_last = p.c("pe")
    st_aps.close()
    st_ap.close()

    st_w = ExitStack()
    WOT = st_w.enter_context(nc.sbuf_tensor("WOT", [128, NDC * 1024], BF16))
    wo_ps = [st_w.enter_context(nc.psum_tensor(f"wo_ps{i}", [128, 512], F32)) for i in range(2)]
    bo_ps = st_w.enter_context(nc.psum_tensor("bo_ps", [128, 1024], F32))

    ds_bo = p.dma("sync", lambda e: e.dma_start(bo_row[:], bo_d))
    ds_wot = 0
    for oc in range(NDC):
        ds_wot = p.dma("gpsimd", lambda e, oc=oc: e.dma_start(
            WOT[:, oc * 1024:(oc + 1) * 1024], woT_d[oc * 128:(oc + 1) * 128, :]),
            wait=([dg_qkvo, ("pe", p.c("pe")), ("act", p.c("act")),
                   ("pool", p.c("pool")), ("dve", p.c("dve"))] if oc == 0 else ()))
    bomm = None
    for half in range(2):
        bomm = p.pe(lambda e, half=half: e.matmul(
            bo_ps[:, half * 512:(half + 1) * 512], ones_sb[:],
            bo_row[:, half * 512:(half + 1) * 512], start=True, stop=True),
            wait=([ds_bo, ds_ones, ctx_done] if half == 0 else ()), inc=(half == 1))

    wo_rd = [0, 0]
    attn_x_done = [0] * NTS
    for ts in range(NTS):
        for bank in range(2):
            par = bank
            for oc in range(NDC):
                w = []
                if oc == 0:
                    w = [ctx_done, ("dve", wo_rd[par]), ds_wot]
                p.pe(lambda e, ts=ts, bank=bank, oc=oc, par=par: e.matmul(
                    wo_ps[par][:],
                    CTX[:, oc * 2048 + ts * 128: oc * 2048 + (ts + 1) * 128],
                    WOT[:, oc * 1024 + bank * 512: oc * 1024 + (bank + 1) * 512],
                    start=(oc == 0), stop=(oc == NDC - 1)),
                    wait=w, inc=(oc == NDC - 1))
            wmm = p.c("pe")
            xsl = X[:, ts * 1024 + bank * 512: ts * 1024 + (bank + 1) * 512]
            p.cop("vector", lambda e, xsl=xsl, par=par: e.tensor_tensor(
                xsl, xsl, wo_ps[par][:], ALU.add), wait=[("pe", wmm)])
            tt2 = p.cop("vector", lambda e, xsl=xsl, bank=bank: e.tensor_tensor(
                xsl, xsl, bo_ps[:, bank * 512:(bank + 1) * 512], ALU.add),
                wait=[("pe", bomm)])
            wo_rd[par] = tt2
        attn_x_done[ts] = p.c("dve")

    if _DEBUG:
        p.dma("sync", lambda e: e.dma_start(taps["xattn"], X[:]),
              wait=[("dve", attn_x_done[NTS - 1])])
        p.barrier()
    st_w.close()
    st_a.close()
    p.barrier()

    st_f = ExitStack()

    def sbf(name, shape, dt):
        return st_f.enter_context(nc.sbuf_tensor(name, shape, dt))

    XLN2 = sbf("XLN2", [128, NDC * 1024], BF16)
    HID = sbf("HID", [128, 32 * 1024], BF16)
    W2Q = sbf("W2Q", [128, 2 * 32 * 256], BF16)
    W1R = sbf("W1R", [128, 2 * 2048], BF16)
    b1_sb = sbf("b1_sb", [128, E * 32], F32)
    b2_sb = sbf("b2_sb", [E, D], F32)
    WG = sbf("WG", [128, 8 * E], F32)
    WMT = sbf("WMT", [8, 8 * 128], F32)
    GSC = sbf("GSC", [128, 8 * 8], F32)
    gcol = sbf("gcol", [128, 8], F32)
    tmpq = sbf("tmpq", [128, 2 * 256], F32)

    w1_ps = [st_f.enter_context(nc.psum_tensor(f"w1_ps{i}", [128, 512], F32)) for i in range(2)]
    w2_ps = [st_f.enter_context(nc.psum_tensor(f"w2_ps{i}", [128, 256], F32)) for i in range(2)]
    psT2 = [st_f.enter_context(nc.psum_tensor(f"psT2_{i}", [128, 1024], BF16)) for i in range(2)]
    psT[0], psT[1] = psT2[0], psT2[1]

    ds_b1 = p.dma("sync", lambda e: e.dma_start(
        b1_sb[:].rearrange("p (e f) -> p e f", e=E), b1_d.rearrange("e p f -> p e f")))
    ds_b2 = p.dma("sync", lambda e: e.dma_start(b2_sb[:], b2_d))
    w1_slot_pe = [0, 0]
    w1_slot_ds = [("ds0", 0), ("ds0", 0)]
    w2_buf_pe = [0, 0]
    w2_buf_da = [("da0", 0), ("da1", 0)]
    w1_war = [0, 0]
    w2_war = [0, 0]
    w2_evac_war = [0, 0]
    hid_rd_pe = 0
    xln_rd_pe = attn_pe_last

    def issue_w1_load(th, e_idx, fcp):
        slot = (th * 128 + e_idx * 16 + fcp) % 2
        w1_slot_ds[slot] = p.dma("gpsimd", lambda en, e_idx=e_idx, fcp=fcp, slot=slot: en.dma_start(
            W1R[:, slot * 2048:(slot + 1) * 2048].rearrange("p (c f) -> p c f", c=NDC),
            w1b_d[e_idx][:, fcp * 256:(fcp + 1) * 256].rearrange("(c p) f -> p c f", p=128)),
            wait=[("pe", w1_slot_pe[slot]), w1cast[e_idx]])
        return slot

    for th in range(2):
        ln2_ids = []
        b2add_done = 0
        gmm = None
        for tsl in range(8):
            ts = th * 8 + tsl
            ln2_ids.append(emit_ln(ts, XLN2, 1024, tsl * 128, [("dve", attn_x_done[ts])], xln_rd_pe))
            for dc in range(NDC):
                gmm = p.pe(lambda e, dc=dc, tsl=tsl: e.matmul(
                    w2_ps[0][:, 0:8],
                    XLN2[:, dc * 1024 + tsl * 128: dc * 1024 + (tsl + 1) * 128],
                    gw_sb[:, dc * 8:(dc + 1) * 8],
                    start=(dc == 0), stop=(dc == NDC - 1)),
                    wait=([("dve", ln2_ids[tsl]), ("act", p.c("act")), dg_qkvo]
                          if dc == 0 else ()),
                    inc=(dc == NDC - 1))
            nmg = p.cop("vector", lambda e: e.tensor_reduce(
                gcol[:, 0:1], w2_ps[0][:, 0:8], AX.X, ALU.max, negate=True),
                wait=[("pe", gmm)])
            exg = p.cop("scalar", lambda e: e.activation(
                GSC[:, 0:8], w2_ps[0][:, 0:8], AF.Exp, bias=gcol[:, 0:1], scale=1.0,
                accum_out=gcol[:, 1:2]), wait=[("dve", nmg)])
            rcg = p.cop("vector", lambda e: e.reciprocal(gcol[:, 2:3], gcol[:, 1:2]),
                        wait=[("act", exg)])
            p.cop("vector", lambda e: e.tensor_scalar(
                GSC[:, 8:16], GSC[:, 0:8], gcol[:, 2:3], None, ALU.mult))
            p.cop("vector", lambda e: e.tensor_reduce(
                gcol[:, 3:4], GSC[:, 8:16], AX.X, ALU.max))
            p.cop("vector", lambda e: e.tensor_scalar(
                GSC[:, 16:24], GSC[:, 8:16], gcol[:, 3:4], None, ALU.is_ge))
            p.cop("vector", lambda e: e.tensor_scalar(
                GSC[:, 24:32], GSC[:, 16:24], -2.0, None, ALU.mult))
            p.cop("vector", lambda e: e.tensor_tensor(
                GSC[:, 24:32], GSC[:, 8:16], GSC[:, 24:32], ALU.add))
            p.cop("vector", lambda e: e.tensor_reduce(
                gcol[:, 4:5], GSC[:, 24:32], AX.X, ALU.max))
            p.cop("vector", lambda e: e.tensor_scalar(
                GSC[:, 32:40], GSC[:, 24:32], gcol[:, 4:5], None, ALU.is_ge))
            p.cop("vector", lambda e: e.tensor_tensor(
                GSC[:, 16:24], GSC[:, 16:24], GSC[:, 32:40], ALU.add))
            wgw = p.cop("vector", lambda e, tsl=tsl: e.tensor_tensor(
                WG[:, tsl * 8:(tsl + 1) * 8], GSC[:, 16:24], GSC[:, 8:16], ALU.mult),
                wait=[("act", p.c("act"))])
            wt = p.pe(lambda e, tsl=tsl: e.transpose(
                w2_ps[1][0:8, 0:128], WG[:, tsl * 8:(tsl + 1) * 8], identf[:]),
                wait=[("dve", wgw), ("act", p.c("act")), ds_identf], inc=True)
            wcp = p.cop("vector", lambda e, tsl=tsl: e.tensor_copy(
                WMT[:, tsl * 128:(tsl + 1) * 128], w2_ps[1][0:8, 0:128]),
                wait=[("pe", max(wt, xln_rd_pe))])
            for bank in range(2):
                bmm = p.pe(lambda e, tsl=tsl, bank=bank: e.matmul(
                    w1_ps[0][:],
                    WMT[:, tsl * 128:(tsl + 1) * 128],
                    b2_sb[:, bank * 512:(bank + 1) * 512],
                    start=True, stop=True),
                    wait=[("dve", max(wcp, b2add_done)), ("act", p.c("act")), ds_b1, ds_b2],
                    inc=True)
                b2add_done = p.cop("vector", lambda e, ts=ts, bank=bank: e.tensor_tensor(
                    X[:, ts * 1024 + bank * 512: ts * 1024 + (bank + 1) * 512],
                    X[:, ts * 1024 + bank * 512: ts * 1024 + (bank + 1) * 512],
                    w1_ps[0][:], ALU.add), wait=[("pe", bmm)])
        ln2_half = [ln2_ids[3], ln2_ids[7]]

        for e_idx in range(E):
            def issue_w2_load(q, e_idx=None):
                qb = q % 2
                w2_buf_da[qb] = p.dma("gpsimd", lambda en, e_idx=e_idx, q=q, qb=qb: en.dma_start(
                    W2Q[:, qb * 8192:(qb + 1) * 8192].rearrange("p (fc h) -> p fc h", fc=32),
                    w2_d[e_idx][:, q * 256:(q + 1) * 256].rearrange("(fc p) h -> p fc h", p=128)),
                    wait=[("pe", w2_buf_pe[qb])])

            if th == 0 and e_idx == 0:
                for fcp in range(2):
                    issue_w1_load(th, e_idx, fcp)
                issue_w2_load(0, e_idx=e_idx)
                issue_w2_load(1, e_idx=e_idx)
            e_next = e_idx + 1 if e_idx + 1 < E else (0 if th == 0 else None)
            relu_done = 0
            for fc in range(32):
                fcp = fc // 2
                slot = (th * 128 + e_idx * 16 + fcp) % 2
                sub = (fc % 2) * 128
                for tb in range(2):
                    par = (fc * 2 + tb) % 2
                    for dc in range(NDC):
                        w = []
                        if dc == 0:
                            w = [w1_slot_ds[slot],
                                 ("dve", max(ln2_half[tb], b2add_done)),
                                 ("act", w1_war[par])]
                        p.pe(lambda e, slot=slot, sub=sub, dc=dc, tb=tb, par=par: e.matmul(
                            w1_ps[par][:],
                            W1R[:, slot * 2048 + dc * 256 + sub: slot * 2048 + dc * 256 + sub + 128],
                            XLN2[:, dc * 1024 + tb * 512: dc * 1024 + (tb + 1) * 512],
                            start=(dc == 0), stop=(dc == NDC - 1)),
                            wait=w, inc=(dc == NDC - 1))
                    wmm = p.c("pe")
                    if tb == 1:
                        w1_slot_pe[slot] = wmm
                        if fc % 2 == 1 and fcp + 2 < 16:
                            issue_w1_load(th, e_idx, fcp + 2)
                    relu_done = p.cop("scalar", lambda e, e_idx=e_idx, fc=fc, tb=tb, par=par: e.activation(
                        HID[:, fc * 1024 + tb * 512: fc * 1024 + (tb + 1) * 512],
                        w1_ps[par][:], AF.Relu,
                        bias=b1_sb[:, e_idx * 32 + fc: e_idx * 32 + fc + 1], scale=1.0),
                        wait=[("pe", max(wmm, hid_rd_pe))])
                    w1_war[par] = relu_done
            if e_next is not None:
                issue_w1_load(th, e_next, 0)
                issue_w1_load(th, e_next, 1)
            for q in range(4):
                qb = q % 2
                for tsl in range(8):
                    ts = th * 8 + tsl
                    par = (q * 8 + tsl) % 2
                    for fc in range(32):
                        w = []
                        if fc == 0:
                            w = [w2_buf_da[qb], ("act", max(relu_done, w2_war[par]))]
                        p.pe(lambda e, fc=fc, tsl=tsl, qb=qb, par=par: e.matmul(
                            w2_ps[par][:],
                            HID[:, fc * 1024 + tsl * 128: fc * 1024 + tsl * 128 + 128],
                            W2Q[:, qb * 8192 + fc * 256: qb * 8192 + (fc + 1) * 256],
                            start=(fc == 0), stop=(fc == 31)),
                            wait=w, inc=(fc == 31))
                    w2mm = p.c("pe")
                    hid_rd_pe = w2mm
                    if tsl == 7:
                        w2_buf_pe[qb] = w2mm
                    evq = p.cop("scalar", lambda e, par=par, tsl=tsl, e_idx=e_idx: e.activation(
                        tmpq[:, par * 256:(par + 1) * 256], w2_ps[par][:], AF.Copy,
                        bias=0.0, scale=WG[:, tsl * 8 + e_idx: tsl * 8 + e_idx + 1]),
                        wait=[("pe", w2mm), ("pool", w2_evac_war[par])])
                    w2_war[par] = evq
                    xadd = p.cop("gpsimd", lambda e, ts=ts, q=q, par=par: e.tensor_tensor(
                        X[:, ts * 1024 + q * 256: ts * 1024 + q * 256 + 256],
                        X[:, ts * 1024 + q * 256: ts * 1024 + q * 256 + 256],
                        tmpq[:, par * 256:(par + 1) * 256], ALU.add),
                        wait=[("act", evq)])
                    w2_evac_war[par] = xadd
                if q < 2:
                    issue_w2_load(q + 2, e_idx=e_idx)
                elif e_next is not None:
                    issue_w2_load(q - 2, e_idx=e_next)
        if th == 0:
            ds_out0 = p.dma("sync", lambda e: e.dma_start(
                out_d.rearrange("(t p) d -> p t d", p=128)[:, 0:8],
                X[:, 0:8 * 1024].rearrange("p (t d) -> p t d", t=8)),
                wait=[("pool", p.c("pool"))])
        xln_rd_pe = p.c("pe")

    if _DEBUG:
        p.dma("sync", lambda e: e.dma_start(taps["wg"], WG[:]), wait=[("dve", p.c("dve"))])
        p.dma("gpsimd", lambda e: e.dma_start(taps["hid"], HID[:, 0:1024]), wait=[("act", p.c("act"))])

    st_f.close()

    p.barrier()
    ds_out = p.dma("sync", lambda e: e.dma_start(
        out_d.rearrange("(t p) d -> p t d", p=128)[:, 8:16],
        X[:, 8 * 1024:16 * 1024].rearrange("p (t d) -> p t d", t=8)))
    p.emit("sync", None, [ds_out, ds_out0], ())

    with nc.Block() as block:
        for engname in ENGS:
            def mk(name):
                def body(eng):
                    for wait, fn, inc in p.q[name]:
                        for s, v in wait:
                            eng.wait_ge(sem_objs[s], v)
                        inst = fn(eng) if fn is not None else None
                        for s, n in inc:
                            inst.then_inc(sem_objs[s], n)
                return body
            getattr(block, engname)(mk(engname))

    st_g.close()
    return nc


def _host_arrays(L0_attn_Wq, L0_attn_bq, L0_attn_Wk, L0_attn_bk, L0_attn_Wv, L0_attn_bv,
                 L0_attn_Wo, L0_attn_bo, L0_attn_rel_bias, L0_ffn_gate_w,
                 L0_ffn_w1, L0_ffn_b1, L0_ffn_w2, L0_ffn_b2):
    f = np.float32
    c = np.ascontiguousarray
    rel = np.asarray(L0_attn_rel_bias, dtype=f)
    idxT = np.clip(np.arange(4096)[None, :] - 2047 - np.arange(128)[:, None], -128, 128) + 128
    rexp = c(rel[:, idxT])
    shared = dict(
        wqT=c(np.asarray(L0_attn_Wq, f).T), wkT=c(np.asarray(L0_attn_Wk, f).T),
        wvT=c(np.asarray(L0_attn_Wv, f).T), woT=c(np.asarray(L0_attn_Wo, f).T),
        rexp=rexp,
        gwT=c(np.asarray(L0_ffn_gate_w, f).T),
        bq_c=c(np.asarray(L0_attn_bq, f).reshape(8, 128).T),
        bk_c=c(np.asarray(L0_attn_bk, f).reshape(8, 128).T),
        bv_c=c(np.asarray(L0_attn_bv, f).reshape(8, 128).T),
        bo_r=c(np.asarray(L0_attn_bo, f).reshape(1, D)),
        b1_c=c(np.asarray(L0_ffn_b1, f).reshape(E, 32, 128).transpose(0, 2, 1)),
        b2_r=c(np.asarray(L0_ffn_b2, f)),
        w1=c(np.asarray(L0_ffn_w1, f)),
        w2=c(np.asarray(L0_ffn_w2, f)),
        ones_r=np.ones((1, 128), f),
        ident=np.eye(128, dtype=f),
    )
    return shared


def kernel(L0_attn_x, L0_attn_Wq, L0_attn_bq, L0_attn_Wk, L0_attn_bk, L0_attn_Wv,
           L0_attn_bv, L0_attn_Wo, L0_attn_bo, L0_attn_rel_bias, L0_ffn_gate_w,
           L0_ffn_w1, L0_ffn_b1, L0_ffn_w2, L0_ffn_b2):
    x = np.asarray(L0_attn_x, dtype=np.float32)
    shared = _host_arrays(L0_attn_Wq, L0_attn_bq, L0_attn_Wk, L0_attn_bk, L0_attn_Wv,
                          L0_attn_bv, L0_attn_Wo, L0_attn_bo, L0_attn_rel_bias,
                          L0_ffn_gate_w, L0_ffn_w1, L0_ffn_b1, L0_ffn_w2, L0_ffn_b2)
    nc = build_program()
    in_maps = [dict(shared, x=np.ascontiguousarray(x[ci])) for ci in range(8)]
    res = run_bass_kernel_spmd(nc, in_maps, core_ids=list(range(8)))
    return np.stack([r["out"] for r in res.results]).astype(np.float32)


# version 26
# speedup vs baseline: 1.2065x; 1.0891x over previous
import numpy as np
import concourse.bass as bass
import concourse.mybir as mybir
from concourse.bass_utils import run_bass_kernel_spmd
from contextlib import ExitStack

F32 = mybir.dt.float32
BF16 = mybir.dt.bfloat16
AF = mybir.ActivationFunctionType
ALU = mybir.AluOpType
AX = mybir.AxisListType

_DEBUG = False

T, D, H, DK, E, FF = 2048, 1024, 16, 64, 8, 4096
NDC = 8
NTS = 16
NPAIR = 8

ENGS = ("sync", "gpsimd", "tensor", "vector", "scalar")
SEMS = ("pe", "act", "dve", "pool", "ds0", "ds1",
        "dg", "dg1", "dg2", "dg3", "dg4", "dg5", "da0", "da1")
ESEM = {"tensor": "pe", "vector": "dve", "scalar": "act", "gpsimd": "pool"}
DSEM = {"sync": ("ds0", "ds1"),
        "gpsimd": ("dg", "dg1", "dg2", "dg3", "dg4", "dg5"),
        "scalar": ("da0", "da1")}


class Prog:
    def __init__(self):
        self.q = {e: [] for e in ENGS}
        self.cnt = {s: 0 for s in SEMS}
        self.dq = {e: 0 for e in DSEM}

    def c(self, s):
        return self.cnt[s]

    def emit(self, eng, fn, wait=(), inc=()):
        wd = {}
        for s, v in wait:
            if v > 0:
                wd[s] = max(wd.get(s, 0), v)
        for s, n in inc:
            self.cnt[s] += n
        self.q[eng].append((tuple(wd.items()), fn, tuple(inc)))
        return {s: self.cnt[s] for s, _ in inc}

    def cop(self, eng, fn, wait=()):
        s = ESEM[eng]
        w = list(wait)
        if eng != "tensor":
            w.append((s, self.c(s)))
        return self.emit(eng, fn, w, [(s, 1)])[s]

    def pe(self, fn, wait=(), inc=False):
        if inc:
            return self.emit("tensor", fn, wait, [("pe", 1)])["pe"]
        self.emit("tensor", fn, wait, ())
        return None

    def dma(self, eng, fn, wait=()):
        sems = DSEM[eng]
        s = sems[self.dq[eng] % len(sems)]
        self.dq[eng] += 1
        w = list(wait)
        if len(sems) > 1 and self.c(s) > 0:
            w.append((s, self.c(s)))
        return (s, self.emit(eng, fn, w, [(s, 16)])[s])

    def barrier(self):
        snap = [(s, self.c(s)) for s in SEMS]
        for e in ENGS:
            self.emit(e, None, snap, ())


def build_program():
    nc = bass.Bass("TRN2", target_bir_lowering=False)
    p = Prog()

    x_d = nc.dram_tensor("x", [T, D], F32, kind="ExternalInput").ap()
    wqT_d = nc.dram_tensor("wqT", [D, D], F32, kind="ExternalInput").ap()
    wkT_d = nc.dram_tensor("wkT", [D, D], F32, kind="ExternalInput").ap()
    wvT_d = nc.dram_tensor("wvT", [D, D], F32, kind="ExternalInput").ap()
    woT_d = nc.dram_tensor("woT", [D, D], F32, kind="ExternalInput").ap()
    rexp_d = nc.dram_tensor("rexp", [H, 128, 4096], F32, kind="ExternalInput").ap()
    gwT_d = nc.dram_tensor("gwT", [D, E], F32, kind="ExternalInput").ap()
    bq_d = nc.dram_tensor("bq_c", [128, 8], F32, kind="ExternalInput").ap()
    bk_d = nc.dram_tensor("bk_c", [128, 8], F32, kind="ExternalInput").ap()
    bv_d = nc.dram_tensor("bv_c", [128, 8], F32, kind="ExternalInput").ap()
    bo_d = nc.dram_tensor("bo_r", [1, D], F32, kind="ExternalInput").ap()
    b1_d = nc.dram_tensor("b1_c", [E, 128, 32], F32, kind="ExternalInput").ap()
    b2_d = nc.dram_tensor("b2_r", [E, D], F32, kind="ExternalInput").ap()
    w1_d = nc.dram_tensor("w1", [E, D, FF], F32, kind="ExternalInput").ap()
    w2_d = nc.dram_tensor("w2", [E, FF, D], F32, kind="ExternalInput").ap()
    ones_d = nc.dram_tensor("ones_r", [1, 128], F32, kind="ExternalInput").ap()
    ident_d = nc.dram_tensor("ident", [128, 128], F32, kind="ExternalInput").ap()
    out_d = nc.dram_tensor("out", [T, D], F32, kind="ExternalOutput").ap()
    w1b_d = nc.dram_tensor("w1b", [E, D, FF], BF16).ap()

    taps = {}
    if _DEBUG:
        taps["xln1"] = nc.dram_tensor("tap_xln1", [128, NDC * T], F32, kind="ExternalOutput").ap()
        taps["xattn"] = nc.dram_tensor("tap_xattn", [128, NTS * D], F32, kind="ExternalOutput").ap()
        taps["wg"] = nc.dram_tensor("tap_wg", [128, 64], F32, kind="ExternalOutput").ap()
        taps["qp"] = nc.dram_tensor("tap_qp", [128, 2048], F32, kind="ExternalOutput").ap()
        taps["kp"] = nc.dram_tensor("tap_kp", [128, 2048], F32, kind="ExternalOutput").ap()
        taps["vtm"] = nc.dram_tensor("tap_vtm", [128, 2048], F32, kind="ExternalOutput").ap()
        taps["pb"] = nc.dram_tensor("tap_pb", [128, 2048], F32, kind="ExternalOutput").ap()
        taps["ctx"] = nc.dram_tensor("tap_ctx", [128, NPAIR * 2048], F32, kind="ExternalOutput").ap()
        taps["hid"] = nc.dram_tensor("tap_hid", [128, 1024], F32, kind="ExternalOutput").ap()

    st_g = ExitStack()

    def sb(name, shape, dt):
        return st_g.enter_context(nc.sbuf_tensor(name, shape, dt))

    X = sb("X", [128, NTS * 1024], F32)
    identb = sb("identb", [128, 128], BF16)
    identf = sb("identf", [128, 128], F32)
    ones_sb = sb("ones_sb", [1, 128], F32)
    eps_col = sb("eps_col", [128, 1], F32)
    bq_sb = sb("bq_sb", [128, 8], F32)
    bk_sb = sb("bk_sb", [128, 8], F32)
    bv_sb = sb("bv_sb", [128, 8], F32)
    gw_sb = sb("gw_sb", [128, NDC * E], BF16)
    lnstats = sb("lnstats", [128, 2 * 12], F32)
    lnmv = sb("lnmv", [128, 2 * 2], F32)
    lnsq = sb("lnsq", [128, 2], F32)
    lnrstd = sb("lnrstd", [128, 2], F32)
    lntm = sb("lntm", [128, 2 * 1024], BF16)

    sem_objs = {}
    for s in SEMS:
        sem_objs[s] = st_g.enter_context(nc.semaphore("sem_" + s))

    ds_x = []
    for xc in range(4):
        ds_x.append(p.dma("sync", lambda e, xc=xc: e.dma_start(
            X[:, xc * 4096:(xc + 1) * 4096].rearrange("p (t d) -> p t d", t=4),
            x_d.rearrange("(t p) d -> p t d", p=128)[:, xc * 4:(xc + 1) * 4])))
    ds_identf = p.dma("sync", lambda e: e.dma_start(identf[:], ident_d))
    ds_ones = p.dma("sync", lambda e: e.dma_start(ones_sb[:], ones_d))
    p.dma("sync", lambda e: e.dma_start(bq_sb[:], bq_d))
    p.dma("sync", lambda e: e.dma_start(bk_sb[:], bk_d))
    p.dma("sync", lambda e: e.dma_start(bv_sb[:], bv_d))

    p.cop("gpsimd", lambda e: e.memset(eps_col[:], 1e-5))
    p.dma("gpsimd", lambda e: e.dma_start(identb[:], ident_d))
    p.dma("gpsimd", lambda e: e.dma_start(
        gw_sb[:].rearrange("p (c e) -> p c e", c=NDC), gwT_d.rearrange("(c p) e -> p c e", p=128)))
    dg_qkvo = ("dg", 0)

    psT = [None, None]
    ln_war = {"sqrt": [0, 0], "recip": [0, 0], "trans": [0, 0], "copy": [0, 0]}

    def emit_ln(ts, dst, dst_stride, dst_off, x_wait, xln_war_pe):
        par = ts % 2
        so, mo = par * 12, par * 2
        pt = psT[par]
        xs = X[:, ts * 1024:(ts + 1) * 1024]
        p.cop("vector", lambda e, xs=xs, so=so: e.bn_stats(lnstats[:, so:so + 6], xs[:, 0:512]),
              wait=x_wait)
        p.cop("vector", lambda e, xs=xs, so=so: e.bn_stats(lnstats[:, so + 6:so + 12], xs[:, 512:1024]))
        aggr = p.cop("vector", lambda e, so=so, mo=mo: e.bn_aggr(lnmv[:, mo:mo + 2], lnstats[:, so:so + 12]),
                     wait=[("act", ln_war["sqrt"][par])])
        sq = p.cop("scalar", lambda e, par=par, mo=mo: e.activation(
            lnsq[:, par:par + 1], lnmv[:, mo + 1:mo + 2], AF.Sqrt, bias=eps_col[:], scale=1.0),
            wait=[("dve", max(aggr, ln_war["recip"][par]))])
        ln_war["sqrt"][par] = sq
        rc = p.cop("vector", lambda e, par=par: e.reciprocal(lnrstd[:, par:par + 1], lnsq[:, par:par + 1]),
                   wait=[("act", sq)])
        ln_war["recip"][par] = rc
        ap = p.cop("vector", lambda e, xs=xs, par=par, mo=mo: e.tensor_scalar(
            lntm[:, par * 1024:(par + 1) * 1024], xs, lnmv[:, mo:mo + 1],
            lnrstd[:, par:par + 1], ALU.subtract, ALU.mult),
            wait=[("pe", ln_war["trans"][par])])
        last = None
        for c in range(NDC):
            last = p.pe(lambda e, c=c, pt=pt, par=par: e.transpose(
                pt[:, c * 128:(c + 1) * 128],
                lntm[:, par * 1024 + c * 128: par * 1024 + (c + 1) * 128], identb[:]),
                wait=([("dve", max(ap, ln_war["copy"][par]))] if c == 0 else ()),
                inc=(c == NDC - 1))
        ln_war["trans"][par] = last
        cp = p.cop("vector", lambda e, pt=pt, dst=dst, dst_off=dst_off: e.tensor_copy(
            dst[:].rearrange("p (c t) -> p c t", c=NDC)[:, :, dst_off:dst_off + 128],
            pt[:].rearrange("p (c f) -> p c f", c=NDC)),
            wait=[("pe", last), ("pe", xln_war_pe)])
        ln_war["copy"][par] = cp
        return cp

    st_a = ExitStack()

    def sba(name, shape, dt):
        return st_a.enter_context(nc.sbuf_tensor(name, shape, dt))

    XLN = sba("XLN", [128, NDC * 2048], BF16)
    bo_row = sba("bo_row", [1, D], F32)
    CTX = sba("CTX", [128, NPAIR * 2048], BF16)

    ln1_done = 0
    with nc.psum_tensor("psT0", [128, 1024], BF16) as psT0, \
         nc.psum_tensor("psT1", [128, 1024], BF16) as psT1:
        psT[0], psT[1] = psT0, psT1
        for ts in range(NTS):
            ln1_done = emit_ln(ts, XLN, 2048, ts * 128, [ds_x[ts // 4], ("pool", 1)], 0)

    if _DEBUG:
        p.dma("gpsimd", lambda e: e.dma_start(taps["xln1"], XLN[:]), wait=[("dve", ln1_done)])

    st_ap = ExitStack()

    def sbp(name, shape, dt):
        return st_ap.enter_context(nc.sbuf_tensor(name, shape, dt))

    QP = sbp("QP", [128, 2048], BF16)
    KP2 = sbp("KP2", [128, 2 * 2048], BF16)
    VTM = sbp("VTM", [128, 16 * 130], BF16)
    VFM = sbp("VFM", [128, 2048], BF16)
    PBT = sbp("PBT", [128, 16 * 512], BF16)
    RT = sbp("RT", [128, 2 * 4096], BF16)
    WSL = sbp("WSL", [128, 3 * 1024], BF16)
    QSQ = sbp("QSQ", [128, 2048], BF16)
    onesb = sbp("onesb", [128, 128], BF16)
    BND = sbp("BND", [128, 12], F32)
    ctxh = sbp("ctxh", [128, 2 * 64], BF16)
    recipc = sbp("recipc", [128, 2], F32)

    st_aps = ExitStack()
    S_ps = st_aps.enter_context(nc.psum_tensor("S_ps", [128, 2048], F32))
    PT_ps = [st_aps.enter_context(nc.psum_tensor("PT_ps0", [128, 1024], BF16))]
    CTXPS = st_aps.enter_context(nc.psum_tensor("CTXPS", [128, 512], F32))
    sreg = [S_ps[:, 0:1024], S_ps[:, 1024:2048]]
    cslot = [CTXPS[:, 0:65], CTXPS[:, 65:130]]

    p.cop("gpsimd", lambda e: e.memset(onesb[:], 1.0))
    p.cop("gpsimd", lambda e: e.memset(KP2[64:128, 0:2048], 0.0))
    p.cop("gpsimd", lambda e: e.memset(KP2[0:64, 2048:4096], 0.0))
    for kc16 in range(16):
        for h2 in range(2):
            p.cop("gpsimd", lambda e, kc16=kc16, h2=h2: e.memset(
                VTM[:, kc16 * 130 + h2 * 65 + 64: kc16 * 130 + h2 * 65 + 65], 1.0))
    vtm_ones = p.c("pool")

    wsl_rd_pe = [0, 0, 0]
    rt_rd_pe = [0, 0]
    qp_rd_pe = 0
    vfm_rd_pe = 0
    vtm_rd_pe = 0
    qsq_rd_pe = 0
    pt_ps_rd = [("dve", 0)]
    slot_rd = [("dve", 0)] * 4
    sps_rd = [("act", 0), ("act", 0)]
    pbt_rd_pe = [0] * 16
    cevac = [("dve", 0), ("dve", 0)]
    ctxh_tr = [0, 0]

    wq_srcs = (wqT_d, wkT_d, wvT_d)
    w1cast = []

    for c in range(NPAIR):
        wsl_ds = []
        for m in range(3):
            wsl_ds.append(p.dma("gpsimd", lambda e, m=m, c=c: e.dma_start(
                WSL[:, m * 1024:(m + 1) * 1024].rearrange("p (k o) -> p k o", k=NDC),
                wq_srcs[m][:, c * 128:(c + 1) * 128].rearrange("(k p) o -> p k o", p=128)),
                wait=[("pe", wsl_rd_pe[m])]))
        rt_ds = []
        for hh in range(2):
            rt_ds.append(p.dma("gpsimd", lambda e, hh=hh, c=c: e.dma_start(
                RT[:, hh * 4096:(hh + 1) * 4096], rexp_d[2 * c + hh]),
                wait=[("pe", rt_rd_pe[hh])]))
        w1cast.append(p.dma("gpsimd", lambda e, c=c: e.dma_start(
            w1b_d[c], w1_d[c], max_dma_last_dim=8192)))

        ev_q = 0
        for tb in range(4):
            for dc in range(NDC):
                for h in range(2):
                    w = []
                    if dc == 0 and h == 0:
                        w = [wsl_ds[0], slot_rd[tb], ("dve", ln1_done), ("pool", vtm_ones)]
                    p.pe(lambda e, h=h, tb=tb, dc=dc: e.matmul(
                        S_ps[h * 64:h * 64 + 64, tb * 512:(tb + 1) * 512],
                        WSL[:, dc * 128 + h * 64: dc * 128 + h * 64 + 64],
                        XLN[:, dc * 2048 + tb * 512: dc * 2048 + (tb + 1) * 512],
                        start=(dc == 0), stop=(dc == NDC - 1)),
                        wait=w, inc=(dc == NDC - 1 and h == 1))
            qmm_tb = p.c("pe")
            for h in range(2):
                ev_q = p.cop("vector", lambda e, c=c, h=h, tb=tb: e.tensor_scalar(
                    QP[h * 64:h * 64 + 64, tb * 512:(tb + 1) * 512],
                    S_ps[h * 64:h * 64 + 64, tb * 512:(tb + 1) * 512],
                    bq_sb[h * 64:h * 64 + 64, c:c + 1],
                    0.125, ALU.add, ALU.mult),
                    wait=([("pe", max(qmm_tb, qp_rd_pe))] if h == 0 else ()))
            slot_rd[tb] = ("dve", ev_q)
        qmm = p.c("pe")

        def production(m, evac_tb_fn):
            ev = 0
            for tb in range(4):
                for dc in range(NDC):
                    w = []
                    if dc == 0:
                        w = [wsl_ds[m], slot_rd[tb]]
                    p.pe(lambda e, m=m, tb=tb, dc=dc: e.matmul(
                        S_ps[:, tb * 512:(tb + 1) * 512],
                        WSL[:, m * 1024 + dc * 128: m * 1024 + (dc + 1) * 128],
                        XLN[:, dc * 2048 + tb * 512: dc * 2048 + (tb + 1) * 512],
                        start=(dc == 0), stop=(dc == NDC - 1)),
                        wait=w, inc=(dc == NDC - 1))
                ev = evac_tb_fn(tb, p.c("pe"))
                slot_rd[tb] = ("dve", ev)
            return ev

        def evac_k_tb(tb, mm):
            evk = 0
            for h in range(2):
                hb = h * 64
                evk = p.cop("vector", lambda e, c=c, h=h, hb=hb, tb=tb: e.tensor_scalar(
                    KP2[hb:hb + 64, h * 2048 + tb * 512: h * 2048 + (tb + 1) * 512],
                    S_ps[hb:hb + 64, tb * 512:(tb + 1) * 512],
                    bk_sb[hb:hb + 64, c:c + 1], None, ALU.add),
                    wait=([("pe", max(mm, qp_rd_pe))] if h == 0 else ()))
            return evk

        ev_k = production(1, evac_k_tb)
        ev_v_tb = [0] * 4

        def evac_v_tb(tb, mm):
            ev_v_tb[tb] = p.cop("vector", lambda e, c=c, tb=tb: e.tensor_scalar(
                VFM[:, tb * 512:(tb + 1) * 512], S_ps[:, tb * 512:(tb + 1) * 512],
                bv_sb[:, c:c + 1], None, ALU.add),
                wait=[("pe", max(mm, vfm_rd_pe))])
            return ev_v_tb[tb]

        ev_v = production(2, evac_v_tb)
        wsl_rd_pe[0] = qmm
        wsl_rd_pe[1] = wsl_rd_pe[2] = p.c("pe")

        vcp = ("dve", 0)
        for g in range(2):
            lastt = None
            for k8 in range(8):
                ts8 = g * 8 + k8
                lastt = p.pe(lambda e, ts8=ts8, k8=k8: e.transpose(
                    PT_ps[0][:, k8 * 128:(k8 + 1) * 128],
                    VFM[:, ts8 * 128:(ts8 + 1) * 128], identb[:]),
                    wait=([("dve", ev_v_tb[2 * g + 1]), pt_ps_rd[0]] if k8 == 0 else ()),
                    inc=(k8 == 7))
            vcp = ("dve", p.cop("vector", lambda e, g=g: e.tensor_copy(
                VTM[:].rearrange("p (k s o) -> p k s o", k=16, s=2, o=65)[:, g * 8:(g + 1) * 8, :, 0:64],
                PT_ps[0][:].rearrange("p (k s o) -> p k s o", k=8, s=2, o=64)),
                wait=[("pe", max(lastt, vtm_rd_pe))]))
            pt_ps_rd[0] = vcp
        vfm_rd_pe = p.c("pe")

        sqk = 0
        for h in range(2):
            hb = h * 64
            sqk = p.cop("vector", lambda e, h=h, hb=hb: e.tensor_tensor(
                QSQ[hb:hb + 64, :], KP2[hb:hb + 64, h * 2048:(h + 1) * 2048],
                KP2[hb:hb + 64, h * 2048:(h + 1) * 2048], ALU.mult),
                wait=[("pe", qsq_rd_pe)])

        def bound_group(sq_done, dstcol):
            for h in range(2):
                hb = h * 64
                for j in range(4):
                    sl = j
                    p.pe(lambda e, hb=hb, j=j, sl=sl: e.matmul(
                        S_ps[:, sl * 512:(sl + 1) * 512],
                        onesb[hb:hb + 64, :], QSQ[hb:hb + 64, j * 512:(j + 1) * 512],
                        start=True, stop=True),
                        wait=[("dve", sq_done), slot_rd[sl]], inc=True)
                    mmid = p.c("pe")
                    rid = p.cop("vector", lambda e, j=j, sl=sl: e.tensor_reduce(
                        BND[:, 4 + j:5 + j], S_ps[:, sl * 512:(sl + 1) * 512], AX.X, ALU.max),
                        wait=[("pe", mmid)])
                    slot_rd[sl] = ("dve", rid)
                p.cop("vector", lambda e: e.tensor_tensor(
                    BND[:, 8:9], BND[:, 4:5], BND[:, 5:6], ALU.max))
                p.cop("vector", lambda e: e.tensor_tensor(
                    BND[:, 9:10], BND[:, 6:7], BND[:, 7:8], ALU.max))
                p.cop("vector", lambda e, h=h, dstcol=dstcol: e.tensor_tensor(
                    BND[:, dstcol + h:dstcol + h + 1], BND[:, 8:9], BND[:, 9:10], ALU.max))
            return p.c("pe")

        k_ones_pe = bound_group(sqk, 2)
        sqq = p.cop("vector", lambda e: e.tensor_tensor(QSQ[:], QP[:], QP[:], ALU.mult),
                    wait=[("pe", k_ones_pe)])
        qsq_rd_pe = bound_group(sqq, 10)
        bneg_dve = [0, 0]
        for h in range(2):
            mrb = p.cop("vector", lambda e, h=h: e.tensor_reduce(
                BND[:, 8:9], RT[:, h * 4096:(h + 1) * 4096], AX.X, ALU.max),
                wait=[rt_ds[h]])
            p.cop("vector", lambda e, h=h: e.tensor_tensor(
                BND[:, 9:10], BND[:, 2 + h:3 + h], BND[:, 10 + h:11 + h], ALU.mult))
            sq = p.cop("scalar", lambda e: e.activation(
                BND[:, 9:10], BND[:, 9:10], AF.Sqrt, bias=eps_col[:], scale=1.0),
                wait=[("dve", p.c("dve"))])
            bneg_dve[h] = p.cop("vector", lambda e, h=h: e.tensor_scalar(
                BND[:, h:h + 1], BND[:, 9:10], -1.02, BND[:, 8:9], ALU.mult, ALU.subtract),
                wait=[("act", sq)])

        bounds_dve = p.c("dve")
        for h in range(2):
            hb = h * 64
            for qc in range(4):
                ex_ids = [0] * 16
                for kcp in range(8):
                    reg = kcp % 2
                    for sub in range(2):
                        kc = 2 * kcp + sub
                        off = 2047 - 128 * kc + qc * 512
                        w = []
                        if sub == 0:
                            w = [sps_rd[reg]]
                            if qc == 0 and kc == 0:
                                w += [rt_ds[h], ("dve", bounds_dve)]
                        p.pe(lambda e, h=h, off=off, reg=reg, sub=sub: e.matmul(
                            sreg[reg][:, sub * 512:(sub + 1) * 512], identb[:],
                            RT[:, h * 4096 + off: h * 4096 + off + 512],
                            start=True, stop=False), wait=w)
                        p.pe(lambda e, h=h, qc=qc, kc=kc, reg=reg, sub=sub: e.matmul(
                            sreg[reg][:, sub * 512:(sub + 1) * 512],
                            KP2[:, h * 2048 + kc * 128: h * 2048 + (kc + 1) * 128],
                            QP[:, qc * 512:(qc + 1) * 512],
                            start=False, stop=True),
                            wait=([("dve", ev_k)] if (qc == 0 and kc == 0) else ()),
                            inc=(sub == 1))
                    kqmm = p.c("pe")
                    exw = [("pe", max(kqmm, pbt_rd_pe[2 * kcp], pbt_rd_pe[2 * kcp + 1]))]
                    if qc == 0 and kcp == 0:
                        exw.append(("dve", bneg_dve[h]))
                    ex = p.cop("scalar", lambda e, reg=reg, kcp=kcp, h=h: e.activation(
                        PBT[:, kcp * 1024:(kcp + 1) * 1024], sreg[reg][:], AF.Exp,
                        bias=BND[:, h:h + 1], scale=1.0), wait=exw)
                    ex_ids[2 * kcp] = ex_ids[2 * kcp + 1] = ex
                    sps_rd[reg] = ("act", ex)
                for qt in range(4):
                    cp = qt % 2
                    j = qc * 4 + qt
                    ctx_mm = 0
                    for kc in range(16):
                        cw = []
                        if qt == 0:
                            cw.append(("act", ex_ids[kc]))
                        if kc == 0:
                            cw += [vcp, cevac[cp]]
                        ctx_mm = p.pe(lambda e, h=h, kc=kc, cp=cp, qt=qt: e.matmul(
                            cslot[cp][:],
                            PBT[:, kc * 512 + qt * 128: kc * 512 + qt * 128 + 128],
                            VTM[:, kc * 130 + h * 65: kc * 130 + h * 65 + 65],
                            start=(kc == 0), stop=(kc == 15)),
                            wait=cw, inc=(kc == 15 or qt == 3))
                        if qt == 3:
                            pbt_rd_pe[kc] = p.c("pe")
                    ctx_mm = p.c("pe")
                    rcp = p.cop("vector", lambda e, cp=cp: e.reciprocal(
                        recipc[:, cp:cp + 1], cslot[cp][:, 64:65]),
                        wait=[("pe", ctx_mm)])
                    ev = p.cop("vector", lambda e, cp=cp: e.tensor_scalar(
                        ctxh[:, cp * 64:(cp + 1) * 64], cslot[cp][:, 0:64],
                        recipc[:, cp:cp + 1], None, ALU.mult),
                        wait=[("pe", ctxh_tr[cp])])
                    cevac[cp] = ("dve", ev)
                    tr = p.pe(lambda e, cp=cp: e.transpose(
                        PT_ps[0][0:64, cp * 128:(cp + 1) * 128],
                        ctxh[:, cp * 64:(cp + 1) * 64], identb[:]),
                        wait=[("dve", ev), pt_ps_rd[0]], inc=True)
                    ctxh_tr[cp] = tr
                    cpy = p.cop("vector", lambda e, c=c, hb=hb, j=j, cp=cp: e.tensor_copy(
                        CTX[hb:hb + 64, c * 2048 + j * 128: c * 2048 + (j + 1) * 128],
                        PT_ps[0][0:64, cp * 128:(cp + 1) * 128]),
                        wait=[("pe", tr)])
                    pt_ps_rd[0] = ("dve", cpy)
            rt_rd_pe[h] = p.c("pe")
        qp_rd_pe = p.c("pe")
        vtm_rd_pe = p.c("pe")
        slot_rd[0] = slot_rd[1] = sps_rd[0]
        slot_rd[2] = slot_rd[3] = sps_rd[1]

    ctx_done = ("dve", p.c("dve"))
    attn_pe_last = p.c("pe")
    st_aps.close()
    st_ap.close()

    st_w = ExitStack()
    WOT = st_w.enter_context(nc.sbuf_tensor("WOT", [128, NDC * 1024], BF16))
    wo_ps = [st_w.enter_context(nc.psum_tensor(f"wo_ps{i}", [128, 512], F32)) for i in range(2)]
    bo_ps = st_w.enter_context(nc.psum_tensor("bo_ps", [128, 1024], F32))

    ds_bo = p.dma("sync", lambda e: e.dma_start(bo_row[:], bo_d))
    ds_wot = 0
    for oc in range(NDC):
        ds_wot = p.dma("gpsimd", lambda e, oc=oc: e.dma_start(
            WOT[:, oc * 1024:(oc + 1) * 1024], woT_d[oc * 128:(oc + 1) * 128, :]),
            wait=([dg_qkvo, ("pe", p.c("pe")), ("act", p.c("act")),
                   ("pool", p.c("pool")), ("dve", p.c("dve"))] if oc == 0 else ()))
    bomm = None
    for half in range(2):
        bomm = p.pe(lambda e, half=half: e.matmul(
            bo_ps[:, half * 512:(half + 1) * 512], ones_sb[:],
            bo_row[:, half * 512:(half + 1) * 512], start=True, stop=True),
            wait=([ds_bo, ds_ones, ctx_done] if half == 0 else ()), inc=(half == 1))

    wo_rd = [0, 0]
    attn_x_done = [0] * NTS
    for ts in range(NTS):
        for bank in range(2):
            par = bank
            for oc in range(NDC):
                w = []
                if oc == 0:
                    w = [ctx_done, ("dve", wo_rd[par]), ds_wot]
                p.pe(lambda e, ts=ts, bank=bank, oc=oc, par=par: e.matmul(
                    wo_ps[par][:],
                    CTX[:, oc * 2048 + ts * 128: oc * 2048 + (ts + 1) * 128],
                    WOT[:, oc * 1024 + bank * 512: oc * 1024 + (bank + 1) * 512],
                    start=(oc == 0), stop=(oc == NDC - 1)),
                    wait=w, inc=(oc == NDC - 1))
            wmm = p.c("pe")
            xsl = X[:, ts * 1024 + bank * 512: ts * 1024 + (bank + 1) * 512]
            p.cop("vector", lambda e, xsl=xsl, par=par: e.tensor_tensor(
                xsl, xsl, wo_ps[par][:], ALU.add), wait=[("pe", wmm)])
            tt2 = p.cop("vector", lambda e, xsl=xsl, bank=bank: e.tensor_tensor(
                xsl, xsl, bo_ps[:, bank * 512:(bank + 1) * 512], ALU.add),
                wait=[("pe", bomm)])
            wo_rd[par] = tt2
        attn_x_done[ts] = p.c("dve")

    if _DEBUG:
        p.dma("sync", lambda e: e.dma_start(taps["xattn"], X[:]),
              wait=[("dve", attn_x_done[NTS - 1])])
        p.barrier()
    st_w.close()
    st_a.close()
    p.barrier()

    st_f = ExitStack()

    def sbf(name, shape, dt):
        return st_f.enter_context(nc.sbuf_tensor(name, shape, dt))

    XLN2 = sbf("XLN2", [128, NDC * 1024], BF16)
    HID = sbf("HID", [128, 32 * 1024], BF16)
    W2Q = sbf("W2Q", [128, 2 * 32 * 256], BF16)
    W1R = sbf("W1R", [128, 2 * 2048], BF16)
    b1_sb = sbf("b1_sb", [128, E * 32], F32)
    b2_sb = sbf("b2_sb", [E, D], F32)
    WG = sbf("WG", [128, 8 * E], F32)
    WMT = sbf("WMT", [8, 8 * 128], F32)
    GSC = sbf("GSC", [128, 8 * 8], F32)
    gcol = sbf("gcol", [128, 8], F32)
    tmpq = sbf("tmpq", [128, 2 * 256], F32)
    ln2mv = sbf("ln2mv", [128, 32], F32)
    ln2sq = sbf("ln2sq", [128, 16], F32)
    ln2rs = sbf("ln2rs", [128, 16], F32)

    w1_ps = [st_f.enter_context(nc.psum_tensor(f"w1_ps{i}", [128, 512], F32)) for i in range(2)]
    w2_ps = [st_f.enter_context(nc.psum_tensor(f"w2_ps{i}", [128, 256], F32)) for i in range(2)]
    psT2 = [st_f.enter_context(nc.psum_tensor(f"psT2_{i}", [128, 1024], BF16)) for i in range(2)]
    psT[0], psT[1] = psT2[0], psT2[1]

    ds_b1 = p.dma("sync", lambda e: e.dma_start(
        b1_sb[:].rearrange("p (e f) -> p e f", e=E), b1_d.rearrange("e p f -> p e f")))
    ds_b2 = p.dma("sync", lambda e: e.dma_start(b2_sb[:], b2_d))
    w1_slot_pe = [0, 0]
    w1_slot_ds = [("ds0", 0), ("ds0", 0)]
    w2_buf_pe = [0, 0]
    w2_buf_da = [("da0", 0), ("da1", 0)]
    w1_war = [0, 0]
    w2_war = [0, 0]
    w2_evac_war = [0, 0]
    ds_outq = []
    hid_rd_pe = 0
    xln_rd_pe = attn_pe_last

    for ts in range(NTS):
        so = (ts % 2) * 12
        xs = X[:, ts * 1024:(ts + 1) * 1024]
        p.cop("vector", lambda e, xs=xs, so=so: e.bn_stats(
            lnstats[:, so:so + 6], xs[:, 0:512]), wait=[("dve", attn_x_done[ts])])
        p.cop("vector", lambda e, xs=xs, so=so: e.bn_stats(
            lnstats[:, so + 6:so + 12], xs[:, 512:1024]))
        p.cop("vector", lambda e, ts=ts, so=so: e.bn_aggr(
            ln2mv[:, ts * 2:ts * 2 + 2], lnstats[:, so:so + 12]))
    sq2 = p.cop("scalar", lambda e: e.activation(
        ln2sq[:].rearrange("p (t one) -> p t one", one=1),
        ln2mv[:].rearrange("p (t two) -> p t two", two=2)[:, :, 1:2],
        AF.Sqrt, bias=eps_col[:], scale=1.0),
        wait=[("dve", p.c("dve"))])
    rc2 = p.cop("vector", lambda e: e.reciprocal(ln2rs[:], ln2sq[:]),
                wait=[("act", sq2)])

    def emit_ln2_apply(ts, tsl, xln_war_pe):
        par = ts % 2
        pt = psT[par]
        xs = X[:, ts * 1024:(ts + 1) * 1024]
        ap = p.cop("vector", lambda e, xs=xs, ts=ts, par=par: e.tensor_scalar(
            lntm[:, par * 1024:(par + 1) * 1024], xs, ln2mv[:, ts * 2:ts * 2 + 1],
            ln2rs[:, ts:ts + 1], ALU.subtract, ALU.mult),
            wait=[("pe", ln_war["trans"][par])])
        last = None
        for c in range(NDC):
            last = p.pe(lambda e, c=c, pt=pt, par=par: e.transpose(
                pt[:, c * 128:(c + 1) * 128],
                lntm[:, par * 1024 + c * 128: par * 1024 + (c + 1) * 128], identb[:]),
                wait=([("dve", max(ap, ln_war["copy"][par]))] if c == 0 else ()),
                inc=(c == NDC - 1))
        ln_war["trans"][par] = last
        cp = p.cop("vector", lambda e, pt=pt, tsl=tsl: e.tensor_copy(
            XLN2[:].rearrange("p (c t) -> p c t", c=NDC)[:, :, tsl * 128:tsl * 128 + 128],
            pt[:].rearrange("p (c f) -> p c f", c=NDC)),
            wait=[("pe", last), ("pe", xln_war_pe)])
        ln_war["copy"][par] = cp
        return cp

    def issue_w1_load(th, e_idx, fcp):
        slot = (th * 128 + e_idx * 16 + fcp) % 2
        w1_slot_ds[slot] = p.dma("gpsimd", lambda en, e_idx=e_idx, fcp=fcp, slot=slot: en.dma_start(
            W1R[:, slot * 2048:(slot + 1) * 2048].rearrange("p (c f) -> p c f", c=NDC),
            w1b_d[e_idx][:, fcp * 256:(fcp + 1) * 256].rearrange("(c p) f -> p c f", p=128)),
            wait=[("pe", w1_slot_pe[slot]), w1cast[e_idx]])
        return slot

    for th in range(2):
        ln2_ids = []
        b2add_done = 0
        gmm = None
        for tsl in range(8):
            ts = th * 8 + tsl
            ln2_ids.append(emit_ln2_apply(ts, tsl, xln_rd_pe))
            for dc in range(NDC):
                gmm = p.pe(lambda e, dc=dc, tsl=tsl: e.matmul(
                    w2_ps[0][:, 0:8],
                    XLN2[:, dc * 1024 + tsl * 128: dc * 1024 + (tsl + 1) * 128],
                    gw_sb[:, dc * 8:(dc + 1) * 8],
                    start=(dc == 0), stop=(dc == NDC - 1)),
                    wait=([("dve", ln2_ids[tsl]), ("act", p.c("act")), dg_qkvo]
                          if dc == 0 else ()),
                    inc=(dc == NDC - 1))
            nmg = p.cop("vector", lambda e: e.tensor_reduce(
                gcol[:, 0:1], w2_ps[0][:, 0:8], AX.X, ALU.max, negate=True),
                wait=[("pe", gmm)])
            exg = p.cop("scalar", lambda e: e.activation(
                GSC[:, 0:8], w2_ps[0][:, 0:8], AF.Exp, bias=gcol[:, 0:1], scale=1.0,
                accum_out=gcol[:, 1:2]), wait=[("dve", nmg)])
            rcg = p.cop("vector", lambda e: e.reciprocal(gcol[:, 2:3], gcol[:, 1:2]),
                        wait=[("act", exg)])
            p.cop("vector", lambda e: e.tensor_scalar(
                GSC[:, 8:16], GSC[:, 0:8], gcol[:, 2:3], None, ALU.mult))
            p.cop("vector", lambda e: e.tensor_reduce(
                gcol[:, 3:4], GSC[:, 8:16], AX.X, ALU.max))
            p.cop("vector", lambda e: e.tensor_scalar(
                GSC[:, 16:24], GSC[:, 8:16], gcol[:, 3:4], None, ALU.is_ge))
            p.cop("vector", lambda e: e.tensor_scalar(
                GSC[:, 24:32], GSC[:, 16:24], -2.0, None, ALU.mult))
            p.cop("vector", lambda e: e.tensor_tensor(
                GSC[:, 24:32], GSC[:, 8:16], GSC[:, 24:32], ALU.add))
            p.cop("vector", lambda e: e.tensor_reduce(
                gcol[:, 4:5], GSC[:, 24:32], AX.X, ALU.max))
            p.cop("vector", lambda e: e.tensor_scalar(
                GSC[:, 32:40], GSC[:, 24:32], gcol[:, 4:5], None, ALU.is_ge))
            p.cop("vector", lambda e: e.tensor_tensor(
                GSC[:, 16:24], GSC[:, 16:24], GSC[:, 32:40], ALU.add))
            wgw = p.cop("vector", lambda e, tsl=tsl: e.tensor_tensor(
                WG[:, tsl * 8:(tsl + 1) * 8], GSC[:, 16:24], GSC[:, 8:16], ALU.mult),
                wait=[("act", p.c("act"))])
            wt = p.pe(lambda e, tsl=tsl: e.transpose(
                w2_ps[1][0:8, 0:128], WG[:, tsl * 8:(tsl + 1) * 8], identf[:]),
                wait=[("dve", wgw), ("act", p.c("act")), ds_identf], inc=True)
            wcp = p.cop("vector", lambda e, tsl=tsl: e.tensor_copy(
                WMT[:, tsl * 128:(tsl + 1) * 128], w2_ps[1][0:8, 0:128]),
                wait=[("pe", max(wt, xln_rd_pe))])
            for bank in range(2):
                bmm = p.pe(lambda e, tsl=tsl, bank=bank: e.matmul(
                    w1_ps[0][:],
                    WMT[:, tsl * 128:(tsl + 1) * 128],
                    b2_sb[:, bank * 512:(bank + 1) * 512],
                    start=True, stop=True),
                    wait=[("dve", max(wcp, b2add_done)), ("act", p.c("act")), ds_b1, ds_b2],
                    inc=True)
                b2add_done = p.cop("vector", lambda e, ts=ts, bank=bank: e.tensor_tensor(
                    X[:, ts * 1024 + bank * 512: ts * 1024 + (bank + 1) * 512],
                    X[:, ts * 1024 + bank * 512: ts * 1024 + (bank + 1) * 512],
                    w1_ps[0][:], ALU.add), wait=[("pe", bmm)])
        ln2_half = [ln2_ids[3], ln2_ids[7]]

        for e_idx in range(E):
            def issue_w2_load(q, e_idx=None):
                qb = q % 2
                w2_buf_da[qb] = p.dma("gpsimd", lambda en, e_idx=e_idx, q=q, qb=qb: en.dma_start(
                    W2Q[:, qb * 8192:(qb + 1) * 8192].rearrange("p (fc h) -> p fc h", fc=32),
                    w2_d[e_idx][:, q * 256:(q + 1) * 256].rearrange("(fc p) h -> p fc h", p=128)),
                    wait=[("pe", w2_buf_pe[qb])])

            if th == 0 and e_idx == 0:
                for fcp in range(2):
                    issue_w1_load(th, e_idx, fcp)
                issue_w2_load(0, e_idx=e_idx)
                issue_w2_load(1, e_idx=e_idx)
            e_next = e_idx + 1 if e_idx + 1 < E else (0 if th == 0 else None)
            relu_done = 0
            for fc in range(32):
                fcp = fc // 2
                slot = (th * 128 + e_idx * 16 + fcp) % 2
                sub = (fc % 2) * 128
                for tb in range(2):
                    par = (fc * 2 + tb) % 2
                    for dc in range(NDC):
                        w = []
                        if dc == 0:
                            w = [w1_slot_ds[slot],
                                 ("dve", max(ln2_half[tb], b2add_done)),
                                 ("act", w1_war[par])]
                        p.pe(lambda e, slot=slot, sub=sub, dc=dc, tb=tb, par=par: e.matmul(
                            w1_ps[par][:],
                            W1R[:, slot * 2048 + dc * 256 + sub: slot * 2048 + dc * 256 + sub + 128],
                            XLN2[:, dc * 1024 + tb * 512: dc * 1024 + (tb + 1) * 512],
                            start=(dc == 0), stop=(dc == NDC - 1)),
                            wait=w, inc=(dc == NDC - 1))
                    wmm = p.c("pe")
                    if tb == 1:
                        w1_slot_pe[slot] = wmm
                        if fc % 2 == 1 and fcp + 2 < 16:
                            issue_w1_load(th, e_idx, fcp + 2)
                    relu_done = p.cop("scalar", lambda e, e_idx=e_idx, fc=fc, tb=tb, par=par: e.activation(
                        HID[:, fc * 1024 + tb * 512: fc * 1024 + (tb + 1) * 512],
                        w1_ps[par][:], AF.Relu,
                        bias=b1_sb[:, e_idx * 32 + fc: e_idx * 32 + fc + 1], scale=1.0),
                        wait=[("pe", max(wmm, hid_rd_pe))])
                    w1_war[par] = relu_done
            if e_next is not None:
                issue_w1_load(th, e_next, 0)
                issue_w1_load(th, e_next, 1)
            for q in range(4):
                qb = q % 2
                for tsl in range(8):
                    ts = th * 8 + tsl
                    par = (q * 8 + tsl) % 2
                    for fc in range(32):
                        w = []
                        if fc == 0:
                            w = [w2_buf_da[qb], ("act", max(relu_done, w2_war[par]))]
                        p.pe(lambda e, fc=fc, tsl=tsl, qb=qb, par=par: e.matmul(
                            w2_ps[par][:],
                            HID[:, fc * 1024 + tsl * 128: fc * 1024 + tsl * 128 + 128],
                            W2Q[:, qb * 8192 + fc * 256: qb * 8192 + (fc + 1) * 256],
                            start=(fc == 0), stop=(fc == 31)),
                            wait=w, inc=(fc == 31))
                    w2mm = p.c("pe")
                    hid_rd_pe = w2mm
                    if tsl == 7:
                        w2_buf_pe[qb] = w2mm
                    evq = p.cop("scalar", lambda e, par=par, tsl=tsl, e_idx=e_idx: e.activation(
                        tmpq[:, par * 256:(par + 1) * 256], w2_ps[par][:], AF.Copy,
                        bias=0.0, scale=WG[:, tsl * 8 + e_idx: tsl * 8 + e_idx + 1]),
                        wait=[("pe", w2mm), ("pool", w2_evac_war[par])])
                    w2_war[par] = evq
                    xadd = p.cop("gpsimd", lambda e, ts=ts, q=q, par=par: e.tensor_tensor(
                        X[:, ts * 1024 + q * 256: ts * 1024 + q * 256 + 256],
                        X[:, ts * 1024 + q * 256: ts * 1024 + q * 256 + 256],
                        tmpq[:, par * 256:(par + 1) * 256], ALU.add),
                        wait=[("act", evq)])
                    w2_evac_war[par] = xadd
                if q < 2:
                    issue_w2_load(q + 2, e_idx=e_idx)
                elif e_next is not None:
                    issue_w2_load(q - 2, e_idx=e_next)
                if th == 1 and e_idx == E - 1:
                    ds_outq.append(p.dma("sync", lambda e, q=q: e.dma_start(
                        out_d.rearrange("(t p) d -> p t d", p=128)[:, 8:16, q * 256:(q + 1) * 256],
                        X[:, 8 * 1024:16 * 1024].rearrange(
                            "p (t d) -> p t d", t=8)[:, :, q * 256:(q + 1) * 256]),
                        wait=[("pool", p.c("pool"))]))
        if th == 0:
            ds_out0 = p.dma("sync", lambda e: e.dma_start(
                out_d.rearrange("(t p) d -> p t d", p=128)[:, 0:8],
                X[:, 0:8 * 1024].rearrange("p (t d) -> p t d", t=8)),
                wait=[("pool", p.c("pool"))])
        xln_rd_pe = p.c("pe")

    if _DEBUG:
        p.dma("sync", lambda e: e.dma_start(taps["wg"], WG[:]), wait=[("dve", p.c("dve"))])
        p.dma("gpsimd", lambda e: e.dma_start(taps["hid"], HID[:, 0:1024]), wait=[("act", p.c("act"))])

    st_f.close()

    p.barrier()
    p.emit("sync", None, ds_outq + [ds_out0], ())

    with nc.Block() as block:
        for engname in ENGS:
            def mk(name):
                def body(eng):
                    for wait, fn, inc in p.q[name]:
                        for s, v in wait:
                            eng.wait_ge(sem_objs[s], v)
                        inst = fn(eng) if fn is not None else None
                        for s, n in inc:
                            inst.then_inc(sem_objs[s], n)
                return body
            getattr(block, engname)(mk(engname))

    st_g.close()
    return nc


def _host_arrays(L0_attn_Wq, L0_attn_bq, L0_attn_Wk, L0_attn_bk, L0_attn_Wv, L0_attn_bv,
                 L0_attn_Wo, L0_attn_bo, L0_attn_rel_bias, L0_ffn_gate_w,
                 L0_ffn_w1, L0_ffn_b1, L0_ffn_w2, L0_ffn_b2):
    f = np.float32
    c = np.ascontiguousarray
    rel = np.asarray(L0_attn_rel_bias, dtype=f)
    idxT = np.clip(np.arange(4096)[None, :] - 2047 - np.arange(128)[:, None], -128, 128) + 128
    rexp = c(rel[:, idxT])
    shared = dict(
        wqT=c(np.asarray(L0_attn_Wq, f).T), wkT=c(np.asarray(L0_attn_Wk, f).T),
        wvT=c(np.asarray(L0_attn_Wv, f).T), woT=c(np.asarray(L0_attn_Wo, f).T),
        rexp=rexp,
        gwT=c(np.asarray(L0_ffn_gate_w, f).T),
        bq_c=c(np.asarray(L0_attn_bq, f).reshape(8, 128).T),
        bk_c=c(np.asarray(L0_attn_bk, f).reshape(8, 128).T),
        bv_c=c(np.asarray(L0_attn_bv, f).reshape(8, 128).T),
        bo_r=c(np.asarray(L0_attn_bo, f).reshape(1, D)),
        b1_c=c(np.asarray(L0_ffn_b1, f).reshape(E, 32, 128).transpose(0, 2, 1)),
        b2_r=c(np.asarray(L0_ffn_b2, f)),
        w1=c(np.asarray(L0_ffn_w1, f)),
        w2=c(np.asarray(L0_ffn_w2, f)),
        ones_r=np.ones((1, 128), f),
        ident=np.eye(128, dtype=f),
    )
    return shared


def kernel(L0_attn_x, L0_attn_Wq, L0_attn_bq, L0_attn_Wk, L0_attn_bk, L0_attn_Wv,
           L0_attn_bv, L0_attn_Wo, L0_attn_bo, L0_attn_rel_bias, L0_ffn_gate_w,
           L0_ffn_w1, L0_ffn_b1, L0_ffn_w2, L0_ffn_b2):
    x = np.asarray(L0_attn_x, dtype=np.float32)
    shared = _host_arrays(L0_attn_Wq, L0_attn_bq, L0_attn_Wk, L0_attn_bk, L0_attn_Wv,
                          L0_attn_bv, L0_attn_Wo, L0_attn_bo, L0_attn_rel_bias,
                          L0_ffn_gate_w, L0_ffn_w1, L0_ffn_b1, L0_ffn_w2, L0_ffn_b2)
    nc = build_program()
    in_maps = [dict(shared, x=np.ascontiguousarray(x[ci])) for ci in range(8)]
    res = run_bass_kernel_spmd(nc, in_maps, core_ids=list(range(8)))
    return np.stack([r["out"] for r in res.results]).astype(np.float32)


# version 27
# speedup vs baseline: 1.2373x; 1.0256x over previous
import numpy as np
import concourse.bass as bass
import concourse.mybir as mybir
from concourse.bass_utils import run_bass_kernel_spmd
from contextlib import ExitStack

F32 = mybir.dt.float32
BF16 = mybir.dt.bfloat16
AF = mybir.ActivationFunctionType
ALU = mybir.AluOpType
AX = mybir.AxisListType

_DEBUG = False

T, D, H, DK, E, FF = 2048, 1024, 16, 64, 8, 4096
NDC = 8
NTS = 16
NPAIR = 8

ENGS = ("sync", "gpsimd", "tensor", "vector", "scalar")
SEMS = ("pe", "act", "dve", "pool", "ds0", "ds1",
        "dg", "dg1", "dg2", "dg3", "dg4", "dg5", "da0", "da1")
ESEM = {"tensor": "pe", "vector": "dve", "scalar": "act", "gpsimd": "pool"}
DSEM = {"sync": ("ds0", "ds1"),
        "gpsimd": ("dg", "dg1", "dg2", "dg3", "dg4", "dg5"),
        "scalar": ("da0", "da1")}


class Prog:
    def __init__(self):
        self.q = {e: [] for e in ENGS}
        self.cnt = {s: 0 for s in SEMS}
        self.dq = {e: 0 for e in DSEM}

    def c(self, s):
        return self.cnt[s]

    def emit(self, eng, fn, wait=(), inc=()):
        wd = {}
        for s, v in wait:
            if v > 0:
                wd[s] = max(wd.get(s, 0), v)
        for s, n in inc:
            self.cnt[s] += n
        self.q[eng].append((tuple(wd.items()), fn, tuple(inc)))
        return {s: self.cnt[s] for s, _ in inc}

    def cop(self, eng, fn, wait=()):
        s = ESEM[eng]
        w = list(wait)
        if eng != "tensor":
            w.append((s, self.c(s)))
        return self.emit(eng, fn, w, [(s, 1)])[s]

    def pe(self, fn, wait=(), inc=False):
        if inc:
            return self.emit("tensor", fn, wait, [("pe", 1)])["pe"]
        self.emit("tensor", fn, wait, ())
        return None

    def dma(self, eng, fn, wait=()):
        sems = DSEM[eng]
        s = sems[self.dq[eng] % len(sems)]
        self.dq[eng] += 1
        w = list(wait)
        if len(sems) > 1 and self.c(s) > 0:
            w.append((s, self.c(s)))
        return (s, self.emit(eng, fn, w, [(s, 16)])[s])

    def barrier(self):
        snap = [(s, self.c(s)) for s in SEMS]
        for e in ENGS:
            self.emit(e, None, snap, ())


def build_program():
    nc = bass.Bass("TRN2", target_bir_lowering=False)
    p = Prog()

    x_d = nc.dram_tensor("x", [T, D], F32, kind="ExternalInput").ap()
    wqT_d = nc.dram_tensor("wqT", [D, D], F32, kind="ExternalInput").ap()
    wkT_d = nc.dram_tensor("wkT", [D, D], F32, kind="ExternalInput").ap()
    wvT_d = nc.dram_tensor("wvT", [D, D], F32, kind="ExternalInput").ap()
    woT_d = nc.dram_tensor("woT", [D, D], F32, kind="ExternalInput").ap()
    rexp_d = nc.dram_tensor("rexp", [H, 128, 4096], F32, kind="ExternalInput").ap()
    gwT_d = nc.dram_tensor("gwT", [D, E], F32, kind="ExternalInput").ap()
    bq_d = nc.dram_tensor("bq_c", [128, 8], F32, kind="ExternalInput").ap()
    bk_d = nc.dram_tensor("bk_c", [128, 8], F32, kind="ExternalInput").ap()
    bv_d = nc.dram_tensor("bv_c", [128, 8], F32, kind="ExternalInput").ap()
    bo_d = nc.dram_tensor("bo_r", [1, D], F32, kind="ExternalInput").ap()
    b1_d = nc.dram_tensor("b1_c", [E, 128, 32], F32, kind="ExternalInput").ap()
    b2_d = nc.dram_tensor("b2_r", [E, D], F32, kind="ExternalInput").ap()
    w1_d = nc.dram_tensor("w1", [E, D, FF], F32, kind="ExternalInput").ap()
    w2_d = nc.dram_tensor("w2", [E, FF, D], F32, kind="ExternalInput").ap()
    ones_d = nc.dram_tensor("ones_r", [1, 128], F32, kind="ExternalInput").ap()
    ident_d = nc.dram_tensor("ident", [128, 128], F32, kind="ExternalInput").ap()
    out_d = nc.dram_tensor("out", [T, D], F32, kind="ExternalOutput").ap()
    w1b_d = nc.dram_tensor("w1b", [E, D, FF], BF16).ap()

    taps = {}
    if _DEBUG:
        taps["xln1"] = nc.dram_tensor("tap_xln1", [128, NDC * T], F32, kind="ExternalOutput").ap()
        taps["xattn"] = nc.dram_tensor("tap_xattn", [128, NTS * D], F32, kind="ExternalOutput").ap()
        taps["wg"] = nc.dram_tensor("tap_wg", [128, 64], F32, kind="ExternalOutput").ap()
        taps["qp"] = nc.dram_tensor("tap_qp", [128, 2048], F32, kind="ExternalOutput").ap()
        taps["kp"] = nc.dram_tensor("tap_kp", [128, 2048], F32, kind="ExternalOutput").ap()
        taps["vtm"] = nc.dram_tensor("tap_vtm", [128, 2048], F32, kind="ExternalOutput").ap()
        taps["pb"] = nc.dram_tensor("tap_pb", [128, 2048], F32, kind="ExternalOutput").ap()
        taps["ctx"] = nc.dram_tensor("tap_ctx", [128, NPAIR * 2048], F32, kind="ExternalOutput").ap()
        taps["hid"] = nc.dram_tensor("tap_hid", [128, 1024], F32, kind="ExternalOutput").ap()

    st_g = ExitStack()

    def sb(name, shape, dt):
        return st_g.enter_context(nc.sbuf_tensor(name, shape, dt))

    X = sb("X", [128, NTS * 1024], F32)
    identb = sb("identb", [128, 128], BF16)
    identf = sb("identf", [128, 128], F32)
    ones_sb = sb("ones_sb", [1, 128], F32)
    eps_col = sb("eps_col", [128, 1], F32)
    bq_sb = sb("bq_sb", [128, 8], F32)
    bk_sb = sb("bk_sb", [128, 8], F32)
    bv_sb = sb("bv_sb", [128, 8], F32)
    gw_sb = sb("gw_sb", [128, NDC * E], BF16)
    lnstats = sb("lnstats", [128, 2 * 12], F32)
    lnmv = sb("lnmv", [128, 2 * 2], F32)
    lnsq = sb("lnsq", [128, 2], F32)
    lnrstd = sb("lnrstd", [128, 2], F32)
    lntm = sb("lntm", [128, 2 * 1024], BF16)

    sem_objs = {}
    for s in SEMS:
        sem_objs[s] = st_g.enter_context(nc.semaphore("sem_" + s))

    ds_x = []
    for xc in range(4):
        ds_x.append(p.dma("sync", lambda e, xc=xc: e.dma_start(
            X[:, xc * 4096:(xc + 1) * 4096].rearrange("p (t d) -> p t d", t=4),
            x_d.rearrange("(t p) d -> p t d", p=128)[:, xc * 4:(xc + 1) * 4])))
    ds_identf = p.dma("sync", lambda e: e.dma_start(identf[:], ident_d))
    ds_ones = p.dma("sync", lambda e: e.dma_start(ones_sb[:], ones_d))
    p.dma("sync", lambda e: e.dma_start(bq_sb[:], bq_d))
    p.dma("sync", lambda e: e.dma_start(bk_sb[:], bk_d))
    p.dma("sync", lambda e: e.dma_start(bv_sb[:], bv_d))

    p.cop("gpsimd", lambda e: e.memset(eps_col[:], 1e-5))
    p.dma("gpsimd", lambda e: e.dma_start(identb[:], ident_d))
    p.dma("gpsimd", lambda e: e.dma_start(
        gw_sb[:].rearrange("p (c e) -> p c e", c=NDC), gwT_d.rearrange("(c p) e -> p c e", p=128)))
    dg_qkvo = ("dg", 0)

    psT = [None, None]
    ln_war = {"sqrt": [0, 0], "recip": [0, 0], "trans": [0, 0], "copy": [0, 0]}

    def emit_ln(ts, dst, dst_stride, dst_off, x_wait, xln_war_pe):
        par = ts % 2
        so, mo = par * 12, par * 2
        pt = psT[par]
        xs = X[:, ts * 1024:(ts + 1) * 1024]
        p.cop("vector", lambda e, xs=xs, so=so: e.bn_stats(lnstats[:, so:so + 6], xs[:, 0:512]),
              wait=x_wait)
        p.cop("vector", lambda e, xs=xs, so=so: e.bn_stats(lnstats[:, so + 6:so + 12], xs[:, 512:1024]))
        aggr = p.cop("vector", lambda e, so=so, mo=mo: e.bn_aggr(lnmv[:, mo:mo + 2], lnstats[:, so:so + 12]),
                     wait=[("act", ln_war["sqrt"][par])])
        sq = p.cop("scalar", lambda e, par=par, mo=mo: e.activation(
            lnsq[:, par:par + 1], lnmv[:, mo + 1:mo + 2], AF.Sqrt, bias=eps_col[:], scale=1.0),
            wait=[("dve", max(aggr, ln_war["recip"][par]))])
        ln_war["sqrt"][par] = sq
        rc = p.cop("vector", lambda e, par=par: e.reciprocal(lnrstd[:, par:par + 1], lnsq[:, par:par + 1]),
                   wait=[("act", sq)])
        ln_war["recip"][par] = rc
        ap = p.cop("vector", lambda e, xs=xs, par=par, mo=mo: e.tensor_scalar(
            lntm[:, par * 1024:(par + 1) * 1024], xs, lnmv[:, mo:mo + 1],
            lnrstd[:, par:par + 1], ALU.subtract, ALU.mult),
            wait=[("pe", ln_war["trans"][par])])
        last = None
        for c in range(NDC):
            last = p.pe(lambda e, c=c, pt=pt, par=par: e.transpose(
                pt[:, c * 128:(c + 1) * 128],
                lntm[:, par * 1024 + c * 128: par * 1024 + (c + 1) * 128], identb[:]),
                wait=([("dve", max(ap, ln_war["copy"][par]))] if c == 0 else ()),
                inc=(c == NDC - 1))
        ln_war["trans"][par] = last
        cp = p.cop("vector", lambda e, pt=pt, dst=dst, dst_off=dst_off: e.tensor_copy(
            dst[:].rearrange("p (c t) -> p c t", c=NDC)[:, :, dst_off:dst_off + 128],
            pt[:].rearrange("p (c f) -> p c f", c=NDC)),
            wait=[("pe", last), ("pe", xln_war_pe)])
        ln_war["copy"][par] = cp
        return cp

    st_a = ExitStack()

    def sba(name, shape, dt):
        return st_a.enter_context(nc.sbuf_tensor(name, shape, dt))

    XLN = sba("XLN", [128, NDC * 2048], BF16)
    bo_row = sba("bo_row", [1, D], F32)
    CTX = sba("CTX", [128, NPAIR * 2048], BF16)

    ln1_done = 0
    with nc.psum_tensor("psT0", [128, 1024], BF16) as psT0, \
         nc.psum_tensor("psT1", [128, 1024], BF16) as psT1:
        psT[0], psT[1] = psT0, psT1
        for ts in range(NTS):
            ln1_done = emit_ln(ts, XLN, 2048, ts * 128, [ds_x[ts // 4], ("pool", 1)], 0)

    if _DEBUG:
        p.dma("gpsimd", lambda e: e.dma_start(taps["xln1"], XLN[:]), wait=[("dve", ln1_done)])

    st_ap = ExitStack()

    def sbp(name, shape, dt):
        return st_ap.enter_context(nc.sbuf_tensor(name, shape, dt))

    QP = sbp("QP", [128, 2048], BF16)
    KP2 = sbp("KP2", [128, 2 * 2048], BF16)
    VTM = sbp("VTM", [128, 16 * 130], BF16)
    VFM = sbp("VFM", [128, 2048], BF16)
    PBT = sbp("PBT", [128, 16 * 512], BF16)
    RT = sbp("RT", [128, 2 * 4096], BF16)
    WSL = sbp("WSL", [128, 3 * 1024], BF16)
    QSQ = sbp("QSQ", [128, 2048], BF16)
    onesb = sbp("onesb", [128, 128], BF16)
    BND = sbp("BND", [128, 12], F32)
    ctxh = sbp("ctxh", [128, 2 * 64], BF16)
    recipc = sbp("recipc", [128, 2], F32)

    st_aps = ExitStack()
    S_ps = st_aps.enter_context(nc.psum_tensor("S_ps", [128, 2048], F32))
    PT_ps = [st_aps.enter_context(nc.psum_tensor("PT_ps0", [128, 1024], BF16))]
    CTXPS = st_aps.enter_context(nc.psum_tensor("CTXPS", [128, 512], F32))
    sreg = [S_ps[:, 0:1024], S_ps[:, 1024:2048]]
    cslot = [CTXPS[:, 0:65], CTXPS[:, 65:130]]

    p.cop("gpsimd", lambda e: e.memset(onesb[:], 1.0))
    p.cop("gpsimd", lambda e: e.memset(KP2[64:128, 0:2048], 0.0))
    p.cop("gpsimd", lambda e: e.memset(KP2[0:64, 2048:4096], 0.0))
    for kc16 in range(16):
        for h2 in range(2):
            p.cop("gpsimd", lambda e, kc16=kc16, h2=h2: e.memset(
                VTM[:, kc16 * 130 + h2 * 65 + 64: kc16 * 130 + h2 * 65 + 65], 1.0))
    vtm_ones = p.c("pool")

    wsl_rd_pe = [0, 0, 0]
    rt_rd_pe = [0, 0]
    qp_rd_pe = 0
    vfm_rd_pe = 0
    vtm_rd_pe = 0
    qsq_rd_pe = 0
    pt_ps_rd = [("dve", 0)]
    slot_rd = [("dve", 0)] * 4
    sps_rd = [("act", 0), ("act", 0)]
    pbt_rd_pe = [0] * 16
    cevac = [("dve", 0), ("dve", 0)]
    ctxh_tr = [0, 0]

    wq_srcs = (wqT_d, wkT_d, wvT_d)
    w1cast = []

    for c in range(NPAIR):
        wsl_ds = []
        for m in range(3):
            wsl_ds.append(p.dma("gpsimd", lambda e, m=m, c=c: e.dma_start(
                WSL[:, m * 1024:(m + 1) * 1024].rearrange("p (k o) -> p k o", k=NDC),
                wq_srcs[m][:, c * 128:(c + 1) * 128].rearrange("(k p) o -> p k o", p=128)),
                wait=[("pe", wsl_rd_pe[m])]))
        rt_ds = []
        for hh in range(2):
            rt_ds.append(p.dma("gpsimd", lambda e, hh=hh, c=c: e.dma_start(
                RT[:, hh * 4096:(hh + 1) * 4096], rexp_d[2 * c + hh]),
                wait=[("pe", rt_rd_pe[hh])]))
        w1cast.append(p.dma("gpsimd", lambda e, c=c: e.dma_start(
            w1b_d[c], w1_d[c], max_dma_last_dim=8192)))

        ev_q = 0
        for tb in range(4):
            for dc in range(NDC):
                for h in range(2):
                    w = []
                    if dc == 0 and h == 0:
                        w = [wsl_ds[0], slot_rd[tb], ("dve", ln1_done), ("pool", vtm_ones)]
                    p.pe(lambda e, h=h, tb=tb, dc=dc: e.matmul(
                        S_ps[h * 64:h * 64 + 64, tb * 512:(tb + 1) * 512],
                        WSL[:, dc * 128 + h * 64: dc * 128 + h * 64 + 64],
                        XLN[:, dc * 2048 + tb * 512: dc * 2048 + (tb + 1) * 512],
                        start=(dc == 0), stop=(dc == NDC - 1)),
                        wait=w, inc=(dc == NDC - 1 and h == 1))
            qmm_tb = p.c("pe")
            for h in range(2):
                ev_q = p.cop("vector", lambda e, c=c, h=h, tb=tb: e.tensor_scalar(
                    QP[h * 64:h * 64 + 64, tb * 512:(tb + 1) * 512],
                    S_ps[h * 64:h * 64 + 64, tb * 512:(tb + 1) * 512],
                    bq_sb[h * 64:h * 64 + 64, c:c + 1],
                    0.125, ALU.add, ALU.mult),
                    wait=([("pe", max(qmm_tb, qp_rd_pe))] if h == 0 else ()))
            slot_rd[tb] = ("dve", ev_q)
        qmm = p.c("pe")

        def production(m, evac_tb_fn):
            ev = 0
            for tb in range(4):
                for dc in range(NDC):
                    w = []
                    if dc == 0:
                        w = [wsl_ds[m], slot_rd[tb]]
                    p.pe(lambda e, m=m, tb=tb, dc=dc: e.matmul(
                        S_ps[:, tb * 512:(tb + 1) * 512],
                        WSL[:, m * 1024 + dc * 128: m * 1024 + (dc + 1) * 128],
                        XLN[:, dc * 2048 + tb * 512: dc * 2048 + (tb + 1) * 512],
                        start=(dc == 0), stop=(dc == NDC - 1)),
                        wait=w, inc=(dc == NDC - 1))
                ev = evac_tb_fn(tb, p.c("pe"))
                slot_rd[tb] = ("dve", ev)
            return ev

        def evac_k_tb(tb, mm):
            evk = 0
            for h in range(2):
                hb = h * 64
                evk = p.cop("vector", lambda e, c=c, h=h, hb=hb, tb=tb: e.tensor_scalar(
                    KP2[hb:hb + 64, h * 2048 + tb * 512: h * 2048 + (tb + 1) * 512],
                    S_ps[hb:hb + 64, tb * 512:(tb + 1) * 512],
                    bk_sb[hb:hb + 64, c:c + 1], None, ALU.add),
                    wait=([("pe", max(mm, qp_rd_pe))] if h == 0 else ()))
            return evk

        ev_k = production(1, evac_k_tb)
        ev_v_tb = [0] * 4

        def evac_v_tb(tb, mm):
            ev_v_tb[tb] = p.cop("vector", lambda e, c=c, tb=tb: e.tensor_scalar(
                VFM[:, tb * 512:(tb + 1) * 512], S_ps[:, tb * 512:(tb + 1) * 512],
                bv_sb[:, c:c + 1], None, ALU.add),
                wait=[("pe", max(mm, vfm_rd_pe))])
            return ev_v_tb[tb]

        sqk = 0
        for h in range(2):
            hb = h * 64
            sqk = p.cop("vector", lambda e, h=h, hb=hb: e.tensor_tensor(
                QSQ[hb:hb + 64, :], KP2[hb:hb + 64, h * 2048:(h + 1) * 2048],
                KP2[hb:hb + 64, h * 2048:(h + 1) * 2048], ALU.mult),
                wait=[("pe", qsq_rd_pe)])

        ev_v = production(2, evac_v_tb)
        wsl_rd_pe[0] = qmm
        wsl_rd_pe[1] = wsl_rd_pe[2] = p.c("pe")

        vcp = ("dve", 0)
        for g in range(2):
            lastt = None
            for k8 in range(8):
                ts8 = g * 8 + k8
                lastt = p.pe(lambda e, ts8=ts8, k8=k8: e.transpose(
                    PT_ps[0][:, k8 * 128:(k8 + 1) * 128],
                    VFM[:, ts8 * 128:(ts8 + 1) * 128], identb[:]),
                    wait=([("dve", ev_v_tb[2 * g + 1]), pt_ps_rd[0]] if k8 == 0 else ()),
                    inc=(k8 == 7))
            vcp = ("dve", p.cop("vector", lambda e, g=g: e.tensor_copy(
                VTM[:].rearrange("p (k s o) -> p k s o", k=16, s=2, o=65)[:, g * 8:(g + 1) * 8, :, 0:64],
                PT_ps[0][:].rearrange("p (k s o) -> p k s o", k=8, s=2, o=64)),
                wait=[("pe", max(lastt, vtm_rd_pe))]))
            pt_ps_rd[0] = vcp
        vfm_rd_pe = p.c("pe")

        def bound_group(sq_done, dstcol):
            for h in range(2):
                hb = h * 64
                for j in range(4):
                    sl = j
                    p.pe(lambda e, hb=hb, j=j, sl=sl: e.matmul(
                        S_ps[:, sl * 512:(sl + 1) * 512],
                        onesb[hb:hb + 64, :], QSQ[hb:hb + 64, j * 512:(j + 1) * 512],
                        start=True, stop=True),
                        wait=[("dve", sq_done), slot_rd[sl]], inc=True)
                    mmid = p.c("pe")
                    rid = p.cop("vector", lambda e, j=j, sl=sl: e.tensor_reduce(
                        BND[:, 4 + j:5 + j], S_ps[:, sl * 512:(sl + 1) * 512], AX.X, ALU.max),
                        wait=[("pe", mmid)])
                    slot_rd[sl] = ("dve", rid)
                p.cop("vector", lambda e: e.tensor_tensor(
                    BND[:, 8:9], BND[:, 4:5], BND[:, 5:6], ALU.max))
                p.cop("vector", lambda e: e.tensor_tensor(
                    BND[:, 9:10], BND[:, 6:7], BND[:, 7:8], ALU.max))
                p.cop("vector", lambda e, h=h, dstcol=dstcol: e.tensor_tensor(
                    BND[:, dstcol + h:dstcol + h + 1], BND[:, 8:9], BND[:, 9:10], ALU.max))
            return p.c("pe")

        k_ones_pe = bound_group(sqk, 2)
        sqq = p.cop("vector", lambda e: e.tensor_tensor(QSQ[:], QP[:], QP[:], ALU.mult),
                    wait=[("pe", k_ones_pe)])
        qsq_rd_pe = bound_group(sqq, 10)
        bneg_dve = [0, 0]
        for h in range(2):
            mrb = p.cop("vector", lambda e, h=h: e.tensor_reduce(
                BND[:, 8:9], RT[:, h * 4096:(h + 1) * 4096], AX.X, ALU.max),
                wait=[rt_ds[h]])
            p.cop("vector", lambda e, h=h: e.tensor_tensor(
                BND[:, 9:10], BND[:, 2 + h:3 + h], BND[:, 10 + h:11 + h], ALU.mult))
            sq = p.cop("scalar", lambda e: e.activation(
                BND[:, 9:10], BND[:, 9:10], AF.Sqrt, bias=eps_col[:], scale=1.0),
                wait=[("dve", p.c("dve"))])
            bneg_dve[h] = p.cop("vector", lambda e, h=h: e.tensor_scalar(
                BND[:, h:h + 1], BND[:, 9:10], -1.02, BND[:, 8:9], ALU.mult, ALU.subtract),
                wait=[("act", sq)])

        bounds_dve = p.c("dve")
        for h in range(2):
            hb = h * 64
            for qc in range(4):
                ex_ids = [0] * 16
                for kcp in range(8):
                    reg = kcp % 2
                    for sub in range(2):
                        kc = 2 * kcp + sub
                        off = 2047 - 128 * kc + qc * 512
                        w = []
                        if sub == 0:
                            w = [sps_rd[reg]]
                            if qc == 0 and kc == 0:
                                w += [rt_ds[h], ("dve", bounds_dve)]
                        p.pe(lambda e, h=h, off=off, reg=reg, sub=sub: e.matmul(
                            sreg[reg][:, sub * 512:(sub + 1) * 512], identb[:],
                            RT[:, h * 4096 + off: h * 4096 + off + 512],
                            start=True, stop=False), wait=w)
                        p.pe(lambda e, h=h, qc=qc, kc=kc, reg=reg, sub=sub: e.matmul(
                            sreg[reg][:, sub * 512:(sub + 1) * 512],
                            KP2[:, h * 2048 + kc * 128: h * 2048 + (kc + 1) * 128],
                            QP[:, qc * 512:(qc + 1) * 512],
                            start=False, stop=True),
                            wait=([("dve", ev_k)] if (qc == 0 and kc == 0) else ()),
                            inc=(sub == 1))
                    kqmm = p.c("pe")
                    exw = [("pe", max(kqmm, pbt_rd_pe[2 * kcp], pbt_rd_pe[2 * kcp + 1]))]
                    if qc == 0 and kcp == 0:
                        exw.append(("dve", bneg_dve[h]))
                    ex = p.cop("scalar", lambda e, reg=reg, kcp=kcp, h=h: e.activation(
                        PBT[:, kcp * 1024:(kcp + 1) * 1024], sreg[reg][:], AF.Exp,
                        bias=BND[:, h:h + 1], scale=1.0), wait=exw)
                    ex_ids[2 * kcp] = ex_ids[2 * kcp + 1] = ex
                    sps_rd[reg] = ("act", ex)
                for qt in range(4):
                    cp = qt % 2
                    j = qc * 4 + qt
                    ctx_mm = 0
                    for kc in range(16):
                        cw = []
                        if qt == 0:
                            cw.append(("act", ex_ids[kc]))
                        if kc == 0:
                            cw += [vcp, cevac[cp]]
                        ctx_mm = p.pe(lambda e, h=h, kc=kc, cp=cp, qt=qt: e.matmul(
                            cslot[cp][:],
                            PBT[:, kc * 512 + qt * 128: kc * 512 + qt * 128 + 128],
                            VTM[:, kc * 130 + h * 65: kc * 130 + h * 65 + 65],
                            start=(kc == 0), stop=(kc == 15)),
                            wait=cw, inc=(kc == 15 or qt == 3))
                        if qt == 3:
                            pbt_rd_pe[kc] = p.c("pe")
                    ctx_mm = p.c("pe")
                    rcp = p.cop("vector", lambda e, cp=cp: e.reciprocal(
                        recipc[:, cp:cp + 1], cslot[cp][:, 64:65]),
                        wait=[("pe", ctx_mm)])
                    ev = p.cop("vector", lambda e, cp=cp: e.tensor_scalar(
                        ctxh[:, cp * 64:(cp + 1) * 64], cslot[cp][:, 0:64],
                        recipc[:, cp:cp + 1], None, ALU.mult),
                        wait=[("pe", ctxh_tr[cp])])
                    cevac[cp] = ("dve", ev)
                    tr = p.pe(lambda e, cp=cp: e.transpose(
                        PT_ps[0][0:64, cp * 128:(cp + 1) * 128],
                        ctxh[:, cp * 64:(cp + 1) * 64], identb[:]),
                        wait=[("dve", ev), pt_ps_rd[0]], inc=True)
                    ctxh_tr[cp] = tr
                    cpy = p.cop("vector", lambda e, c=c, hb=hb, j=j, cp=cp: e.tensor_copy(
                        CTX[hb:hb + 64, c * 2048 + j * 128: c * 2048 + (j + 1) * 128],
                        PT_ps[0][0:64, cp * 128:(cp + 1) * 128]),
                        wait=[("pe", tr)])
                    pt_ps_rd[0] = ("dve", cpy)
            rt_rd_pe[h] = p.c("pe")
        qp_rd_pe = p.c("pe")
        vtm_rd_pe = p.c("pe")
        slot_rd[0] = slot_rd[1] = sps_rd[0]
        slot_rd[2] = slot_rd[3] = sps_rd[1]

    ctx_done = ("dve", p.c("dve"))
    attn_pe_last = p.c("pe")
    st_aps.close()
    st_ap.close()

    st_w = ExitStack()
    WOT = st_w.enter_context(nc.sbuf_tensor("WOT", [128, NDC * 1024], BF16))
    wo_ps = [st_w.enter_context(nc.psum_tensor(f"wo_ps{i}", [128, 512], F32)) for i in range(2)]
    bo_ps = st_w.enter_context(nc.psum_tensor("bo_ps", [128, 1024], F32))

    ds_bo = p.dma("sync", lambda e: e.dma_start(bo_row[:], bo_d))
    ds_wot = 0
    for oc in range(NDC):
        ds_wot = p.dma("gpsimd", lambda e, oc=oc: e.dma_start(
            WOT[:, oc * 1024:(oc + 1) * 1024], woT_d[oc * 128:(oc + 1) * 128, :]),
            wait=([dg_qkvo, ("pe", p.c("pe")), ("act", p.c("act")),
                   ("pool", p.c("pool")), ("dve", p.c("dve"))] if oc == 0 else ()))
    bomm = None
    for half in range(2):
        bomm = p.pe(lambda e, half=half: e.matmul(
            bo_ps[:, half * 512:(half + 1) * 512], ones_sb[:],
            bo_row[:, half * 512:(half + 1) * 512], start=True, stop=True),
            wait=([ds_bo, ds_ones, ctx_done] if half == 0 else ()), inc=(half == 1))

    wo_rd = [0, 0]
    attn_x_done = [0] * NTS
    for ts in range(NTS):
        for bank in range(2):
            par = bank
            for oc in range(NDC):
                w = []
                if oc == 0:
                    w = [ctx_done, ("dve", wo_rd[par]), ds_wot]
                p.pe(lambda e, ts=ts, bank=bank, oc=oc, par=par: e.matmul(
                    wo_ps[par][:],
                    CTX[:, oc * 2048 + ts * 128: oc * 2048 + (ts + 1) * 128],
                    WOT[:, oc * 1024 + bank * 512: oc * 1024 + (bank + 1) * 512],
                    start=(oc == 0), stop=(oc == NDC - 1)),
                    wait=w, inc=(oc == NDC - 1))
            wmm = p.c("pe")
            xsl = X[:, ts * 1024 + bank * 512: ts * 1024 + (bank + 1) * 512]
            p.cop("vector", lambda e, xsl=xsl, par=par: e.tensor_tensor(
                xsl, xsl, wo_ps[par][:], ALU.add), wait=[("pe", wmm)])
            tt2 = p.cop("vector", lambda e, xsl=xsl, bank=bank: e.tensor_tensor(
                xsl, xsl, bo_ps[:, bank * 512:(bank + 1) * 512], ALU.add),
                wait=[("pe", bomm)])
            wo_rd[par] = tt2
        attn_x_done[ts] = p.c("dve")

    if _DEBUG:
        p.dma("sync", lambda e: e.dma_start(taps["xattn"], X[:]),
              wait=[("dve", attn_x_done[NTS - 1])])
        p.barrier()
    st_w.close()
    st_a.close()
    p.barrier()

    st_f = ExitStack()

    def sbf(name, shape, dt):
        return st_f.enter_context(nc.sbuf_tensor(name, shape, dt))

    XLN2 = sbf("XLN2", [128, NDC * 1024], BF16)
    HID = sbf("HID", [128, 32 * 1024], BF16)
    W2Q = sbf("W2Q", [128, 2 * 32 * 256], BF16)
    W1R = sbf("W1R", [128, 2 * 2048], BF16)
    b1_sb = sbf("b1_sb", [128, E * 32], F32)
    b2_sb = sbf("b2_sb", [E, D], F32)
    WG = sbf("WG", [128, 8 * E], F32)
    WMT = sbf("WMT", [8, 8 * 128], F32)
    GSC = sbf("GSC", [128, 8 * 8], F32)
    gcol = sbf("gcol", [128, 8], F32)
    tmpq = sbf("tmpq", [128, 2 * 256], F32)
    ln2mv = sbf("ln2mv", [128, 32], F32)
    ln2sq = sbf("ln2sq", [128, 16], F32)
    ln2rs = sbf("ln2rs", [128, 16], F32)

    w1_ps = [st_f.enter_context(nc.psum_tensor(f"w1_ps{i}", [128, 512], F32)) for i in range(2)]
    w2_ps = [st_f.enter_context(nc.psum_tensor(f"w2_ps{i}", [128, 256], F32)) for i in range(2)]
    psT2 = [st_f.enter_context(nc.psum_tensor(f"psT2_{i}", [128, 1024], BF16)) for i in range(2)]
    psT[0], psT[1] = psT2[0], psT2[1]

    ds_b1 = p.dma("sync", lambda e: e.dma_start(
        b1_sb[:].rearrange("p (e f) -> p e f", e=E), b1_d.rearrange("e p f -> p e f")))
    ds_b2 = p.dma("sync", lambda e: e.dma_start(b2_sb[:], b2_d))
    w1_slot_pe = [0, 0]
    w1_slot_ds = [("ds0", 0), ("ds0", 0)]
    w2_buf_pe = [0, 0]
    w2_buf_da = [("da0", 0), ("da1", 0)]
    w1_war = [0, 0]
    w2_war = [0, 0]
    w2_evac_war = [0, 0]
    ds_outq = []
    hid_rd_pe = 0
    xln_rd_pe = attn_pe_last

    for ts in range(NTS):
        so = (ts % 2) * 12
        xs = X[:, ts * 1024:(ts + 1) * 1024]
        p.cop("vector", lambda e, xs=xs, so=so: e.bn_stats(
            lnstats[:, so:so + 6], xs[:, 0:512]), wait=[("dve", attn_x_done[ts])])
        p.cop("vector", lambda e, xs=xs, so=so: e.bn_stats(
            lnstats[:, so + 6:so + 12], xs[:, 512:1024]))
        p.cop("vector", lambda e, ts=ts, so=so: e.bn_aggr(
            ln2mv[:, ts * 2:ts * 2 + 2], lnstats[:, so:so + 12]))
    sq2 = p.cop("scalar", lambda e: e.activation(
        ln2sq[:].rearrange("p (t one) -> p t one", one=1),
        ln2mv[:].rearrange("p (t two) -> p t two", two=2)[:, :, 1:2],
        AF.Sqrt, bias=eps_col[:], scale=1.0),
        wait=[("dve", p.c("dve"))])
    rc2 = p.cop("vector", lambda e: e.reciprocal(ln2rs[:], ln2sq[:]),
                wait=[("act", sq2)])

    def emit_ln2_apply(ts, tsl, xln_war_pe):
        par = ts % 2
        pt = psT[par]
        xs = X[:, ts * 1024:(ts + 1) * 1024]
        ap = p.cop("vector", lambda e, xs=xs, ts=ts, par=par: e.tensor_scalar(
            lntm[:, par * 1024:(par + 1) * 1024], xs, ln2mv[:, ts * 2:ts * 2 + 1],
            ln2rs[:, ts:ts + 1], ALU.subtract, ALU.mult),
            wait=[("pe", ln_war["trans"][par])])
        last = None
        for c in range(NDC):
            last = p.pe(lambda e, c=c, pt=pt, par=par: e.transpose(
                pt[:, c * 128:(c + 1) * 128],
                lntm[:, par * 1024 + c * 128: par * 1024 + (c + 1) * 128], identb[:]),
                wait=([("dve", max(ap, ln_war["copy"][par]))] if c == 0 else ()),
                inc=(c == NDC - 1))
        ln_war["trans"][par] = last
        cp = p.cop("vector", lambda e, pt=pt, tsl=tsl: e.tensor_copy(
            XLN2[:].rearrange("p (c t) -> p c t", c=NDC)[:, :, tsl * 128:tsl * 128 + 128],
            pt[:].rearrange("p (c f) -> p c f", c=NDC)),
            wait=[("pe", last), ("pe", xln_war_pe)])
        ln_war["copy"][par] = cp
        return cp

    def issue_w1_load(th, e_idx, fcp):
        slot = (th * 128 + e_idx * 16 + fcp) % 2
        w1_slot_ds[slot] = p.dma("gpsimd", lambda en, e_idx=e_idx, fcp=fcp, slot=slot: en.dma_start(
            W1R[:, slot * 2048:(slot + 1) * 2048].rearrange("p (c f) -> p c f", c=NDC),
            w1b_d[e_idx][:, fcp * 256:(fcp + 1) * 256].rearrange("(c p) f -> p c f", p=128)),
            wait=[("pe", w1_slot_pe[slot]), w1cast[e_idx]])
        return slot

    for th in range(2):
        ln2_ids = []
        b2add_done = 0
        gmm = None
        for tsl in range(8):
            ts = th * 8 + tsl
            ln2_ids.append(emit_ln2_apply(ts, tsl, xln_rd_pe))
            for dc in range(NDC):
                gmm = p.pe(lambda e, dc=dc, tsl=tsl: e.matmul(
                    w2_ps[0][:, 0:8],
                    XLN2[:, dc * 1024 + tsl * 128: dc * 1024 + (tsl + 1) * 128],
                    gw_sb[:, dc * 8:(dc + 1) * 8],
                    start=(dc == 0), stop=(dc == NDC - 1)),
                    wait=([("dve", ln2_ids[tsl]), ("act", p.c("act")), dg_qkvo]
                          if dc == 0 else ()),
                    inc=(dc == NDC - 1))
            nmg = p.cop("vector", lambda e: e.tensor_reduce(
                gcol[:, 0:1], w2_ps[0][:, 0:8], AX.X, ALU.max, negate=True),
                wait=[("pe", gmm)])
            exg = p.cop("scalar", lambda e: e.activation(
                GSC[:, 0:8], w2_ps[0][:, 0:8], AF.Exp, bias=gcol[:, 0:1], scale=1.0,
                accum_out=gcol[:, 1:2]), wait=[("dve", nmg)])
            rcg = p.cop("vector", lambda e: e.reciprocal(gcol[:, 2:3], gcol[:, 1:2]),
                        wait=[("act", exg)])
            p.cop("vector", lambda e: e.tensor_scalar(
                GSC[:, 8:16], GSC[:, 0:8], gcol[:, 2:3], None, ALU.mult))
            p.cop("vector", lambda e: e.tensor_reduce(
                gcol[:, 3:4], GSC[:, 8:16], AX.X, ALU.max))
            p.cop("vector", lambda e: e.tensor_scalar(
                GSC[:, 16:24], GSC[:, 8:16], gcol[:, 3:4], None, ALU.is_ge))
            p.cop("vector", lambda e: e.tensor_scalar(
                GSC[:, 24:32], GSC[:, 16:24], -2.0, None, ALU.mult))
            p.cop("vector", lambda e: e.tensor_tensor(
                GSC[:, 24:32], GSC[:, 8:16], GSC[:, 24:32], ALU.add))
            p.cop("vector", lambda e: e.tensor_reduce(
                gcol[:, 4:5], GSC[:, 24:32], AX.X, ALU.max))
            p.cop("vector", lambda e: e.tensor_scalar(
                GSC[:, 32:40], GSC[:, 24:32], gcol[:, 4:5], None, ALU.is_ge))
            p.cop("vector", lambda e: e.tensor_tensor(
                GSC[:, 16:24], GSC[:, 16:24], GSC[:, 32:40], ALU.add))
            wgw = p.cop("vector", lambda e, tsl=tsl: e.tensor_tensor(
                WG[:, tsl * 8:(tsl + 1) * 8], GSC[:, 16:24], GSC[:, 8:16], ALU.mult),
                wait=[("act", p.c("act"))])
            wt = p.pe(lambda e, tsl=tsl: e.transpose(
                w2_ps[1][0:8, 0:128], WG[:, tsl * 8:(tsl + 1) * 8], identf[:]),
                wait=[("dve", wgw), ("act", p.c("act")), ds_identf], inc=True)
            wcp = p.cop("vector", lambda e, tsl=tsl: e.tensor_copy(
                WMT[:, tsl * 128:(tsl + 1) * 128], w2_ps[1][0:8, 0:128]),
                wait=[("pe", max(wt, xln_rd_pe))])
            for bank in range(2):
                bmm = p.pe(lambda e, tsl=tsl, bank=bank: e.matmul(
                    w1_ps[0][:],
                    WMT[:, tsl * 128:(tsl + 1) * 128],
                    b2_sb[:, bank * 512:(bank + 1) * 512],
                    start=True, stop=True),
                    wait=[("dve", max(wcp, b2add_done)), ("act", p.c("act")), ds_b1, ds_b2],
                    inc=True)
                b2add_done = p.cop("vector", lambda e, ts=ts, bank=bank: e.tensor_tensor(
                    X[:, ts * 1024 + bank * 512: ts * 1024 + (bank + 1) * 512],
                    X[:, ts * 1024 + bank * 512: ts * 1024 + (bank + 1) * 512],
                    w1_ps[0][:], ALU.add), wait=[("pe", bmm)])
        ln2_half = [ln2_ids[3], ln2_ids[7]]

        for e_idx in range(E):
            def issue_w2_load(q, e_idx=None):
                qb = q % 2
                w2_buf_da[qb] = p.dma("gpsimd", lambda en, e_idx=e_idx, q=q, qb=qb: en.dma_start(
                    W2Q[:, qb * 8192:(qb + 1) * 8192].rearrange("p (fc h) -> p fc h", fc=32),
                    w2_d[e_idx][:, q * 256:(q + 1) * 256].rearrange("(fc p) h -> p fc h", p=128)),
                    wait=[("pe", w2_buf_pe[qb])])

            if th == 0 and e_idx == 0:
                for fcp in range(2):
                    issue_w1_load(th, e_idx, fcp)
                issue_w2_load(0, e_idx=e_idx)
                issue_w2_load(1, e_idx=e_idx)
            e_next = e_idx + 1 if e_idx + 1 < E else (0 if th == 0 else None)
            relu_done = 0
            for fc in range(32):
                fcp = fc // 2
                slot = (th * 128 + e_idx * 16 + fcp) % 2
                sub = (fc % 2) * 128
                for tb in range(2):
                    par = (fc * 2 + tb) % 2
                    for dc in range(NDC):
                        w = []
                        if dc == 0:
                            w = [w1_slot_ds[slot],
                                 ("dve", max(ln2_half[tb], b2add_done)),
                                 ("act", w1_war[par])]
                        p.pe(lambda e, slot=slot, sub=sub, dc=dc, tb=tb, par=par: e.matmul(
                            w1_ps[par][:],
                            W1R[:, slot * 2048 + dc * 256 + sub: slot * 2048 + dc * 256 + sub + 128],
                            XLN2[:, dc * 1024 + tb * 512: dc * 1024 + (tb + 1) * 512],
                            start=(dc == 0), stop=(dc == NDC - 1)),
                            wait=w, inc=(dc == NDC - 1))
                    wmm = p.c("pe")
                    if tb == 1:
                        w1_slot_pe[slot] = wmm
                        if fc % 2 == 1 and fcp + 2 < 16:
                            issue_w1_load(th, e_idx, fcp + 2)
                    relu_done = p.cop("scalar", lambda e, e_idx=e_idx, fc=fc, tb=tb, par=par: e.activation(
                        HID[:, fc * 1024 + tb * 512: fc * 1024 + (tb + 1) * 512],
                        w1_ps[par][:], AF.Relu,
                        bias=b1_sb[:, e_idx * 32 + fc: e_idx * 32 + fc + 1], scale=1.0),
                        wait=[("pe", max(wmm, hid_rd_pe))])
                    w1_war[par] = relu_done
            if e_next is not None:
                issue_w1_load(th, e_next, 0)
                issue_w1_load(th, e_next, 1)
            for q in range(4):
                qb = q % 2
                for tsl in range(8):
                    ts = th * 8 + tsl
                    par = (q * 8 + tsl) % 2
                    for fc in range(32):
                        w = []
                        if fc == 0:
                            w = [w2_buf_da[qb], ("act", max(relu_done, w2_war[par]))]
                        p.pe(lambda e, fc=fc, tsl=tsl, qb=qb, par=par: e.matmul(
                            w2_ps[par][:],
                            HID[:, fc * 1024 + tsl * 128: fc * 1024 + tsl * 128 + 128],
                            W2Q[:, qb * 8192 + fc * 256: qb * 8192 + (fc + 1) * 256],
                            start=(fc == 0), stop=(fc == 31)),
                            wait=w, inc=(fc == 31))
                    w2mm = p.c("pe")
                    hid_rd_pe = w2mm
                    if tsl == 7:
                        w2_buf_pe[qb] = w2mm
                    evq = p.cop("scalar", lambda e, par=par, tsl=tsl, e_idx=e_idx: e.activation(
                        tmpq[:, par * 256:(par + 1) * 256], w2_ps[par][:], AF.Copy,
                        bias=0.0, scale=WG[:, tsl * 8 + e_idx: tsl * 8 + e_idx + 1]),
                        wait=[("pe", w2mm), ("pool", w2_evac_war[par])])
                    w2_war[par] = evq
                    xadd = p.cop("gpsimd", lambda e, ts=ts, q=q, par=par: e.tensor_tensor(
                        X[:, ts * 1024 + q * 256: ts * 1024 + q * 256 + 256],
                        X[:, ts * 1024 + q * 256: ts * 1024 + q * 256 + 256],
                        tmpq[:, par * 256:(par + 1) * 256], ALU.add),
                        wait=[("act", evq)])
                    w2_evac_war[par] = xadd
                if q < 2:
                    issue_w2_load(q + 2, e_idx=e_idx)
                elif e_next is not None:
                    issue_w2_load(q - 2, e_idx=e_next)
                if th == 1 and e_idx == E - 1:
                    ds_outq.append(p.dma("sync", lambda e, q=q: e.dma_start(
                        out_d.rearrange("(t p) d -> p t d", p=128)[:, 8:16, q * 256:(q + 1) * 256],
                        X[:, 8 * 1024:16 * 1024].rearrange(
                            "p (t d) -> p t d", t=8)[:, :, q * 256:(q + 1) * 256]),
                        wait=[("pool", p.c("pool"))]))
        if th == 0:
            ds_out0 = p.dma("sync", lambda e: e.dma_start(
                out_d.rearrange("(t p) d -> p t d", p=128)[:, 0:8],
                X[:, 0:8 * 1024].rearrange("p (t d) -> p t d", t=8)),
                wait=[("pool", p.c("pool"))])
        xln_rd_pe = p.c("pe")

    if _DEBUG:
        p.dma("sync", lambda e: e.dma_start(taps["wg"], WG[:]), wait=[("dve", p.c("dve"))])
        p.dma("gpsimd", lambda e: e.dma_start(taps["hid"], HID[:, 0:1024]), wait=[("act", p.c("act"))])

    st_f.close()

    p.barrier()
    p.emit("sync", None, ds_outq + [ds_out0], ())

    with nc.Block() as block:
        for engname in ENGS:
            def mk(name):
                def body(eng):
                    for wait, fn, inc in p.q[name]:
                        for s, v in wait:
                            eng.wait_ge(sem_objs[s], v)
                        inst = fn(eng) if fn is not None else None
                        for s, n in inc:
                            inst.then_inc(sem_objs[s], n)
                return body
            getattr(block, engname)(mk(engname))

    st_g.close()
    return nc


def _host_arrays(L0_attn_Wq, L0_attn_bq, L0_attn_Wk, L0_attn_bk, L0_attn_Wv, L0_attn_bv,
                 L0_attn_Wo, L0_attn_bo, L0_attn_rel_bias, L0_ffn_gate_w,
                 L0_ffn_w1, L0_ffn_b1, L0_ffn_w2, L0_ffn_b2):
    f = np.float32
    c = np.ascontiguousarray
    rel = np.asarray(L0_attn_rel_bias, dtype=f)
    idxT = np.clip(np.arange(4096)[None, :] - 2047 - np.arange(128)[:, None], -128, 128) + 128
    rexp = c(rel[:, idxT])
    shared = dict(
        wqT=c(np.asarray(L0_attn_Wq, f).T), wkT=c(np.asarray(L0_attn_Wk, f).T),
        wvT=c(np.asarray(L0_attn_Wv, f).T), woT=c(np.asarray(L0_attn_Wo, f).T),
        rexp=rexp,
        gwT=c(np.asarray(L0_ffn_gate_w, f).T),
        bq_c=c(np.asarray(L0_attn_bq, f).reshape(8, 128).T),
        bk_c=c(np.asarray(L0_attn_bk, f).reshape(8, 128).T),
        bv_c=c(np.asarray(L0_attn_bv, f).reshape(8, 128).T),
        bo_r=c(np.asarray(L0_attn_bo, f).reshape(1, D)),
        b1_c=c(np.asarray(L0_ffn_b1, f).reshape(E, 32, 128).transpose(0, 2, 1)),
        b2_r=c(np.asarray(L0_ffn_b2, f)),
        w1=c(np.asarray(L0_ffn_w1, f)),
        w2=c(np.asarray(L0_ffn_w2, f)),
        ones_r=np.ones((1, 128), f),
        ident=np.eye(128, dtype=f),
    )
    return shared


def kernel(L0_attn_x, L0_attn_Wq, L0_attn_bq, L0_attn_Wk, L0_attn_bk, L0_attn_Wv,
           L0_attn_bv, L0_attn_Wo, L0_attn_bo, L0_attn_rel_bias, L0_ffn_gate_w,
           L0_ffn_w1, L0_ffn_b1, L0_ffn_w2, L0_ffn_b2):
    x = np.asarray(L0_attn_x, dtype=np.float32)
    shared = _host_arrays(L0_attn_Wq, L0_attn_bq, L0_attn_Wk, L0_attn_bk, L0_attn_Wv,
                          L0_attn_bv, L0_attn_Wo, L0_attn_bo, L0_attn_rel_bias,
                          L0_ffn_gate_w, L0_ffn_w1, L0_ffn_b1, L0_ffn_w2, L0_ffn_b2)
    nc = build_program()
    in_maps = [dict(shared, x=np.ascontiguousarray(x[ci])) for ci in range(8)]
    res = run_bass_kernel_spmd(nc, in_maps, core_ids=list(range(8)))
    return np.stack([r["out"] for r in res.results]).astype(np.float32)
